# version 04
# baseline (speedup 1.0000x reference)
import numpy as np
import ml_dtypes
import concourse.bass as bass
import concourse.mybir as mybir
from concourse.bass_utils import run_bass_kernel_spmd
from contextlib import ExitStack

F32 = mybir.dt.float32
BF16 = mybir.dt.bfloat16
AF = mybir.ActivationFunctionType
ALU = mybir.AluOpType
AX = mybir.AxisListType

S = 2048
DM = 2048
DFF = 8192
NE = 8
NT = S // 128
ND = DM // 128
HALF = S // 2
NTH = NT // 2
EPS_RMS = 1.1920929e-07
EPS_LN = 1e-5
N_CORES = 8
SB_BASE = 20448
SB_LIMIT = 229344


class Sem:
    __slots__ = ("h", "n")

    def __init__(self, nc, name):
        self.h = nc.alloc_semaphore(name)
        self.n = 0

    def inc(self, instr, k=1):
        instr.then_inc(self.h, k)
        self.n += k
        return instr


def build_program(stages=("ffn1", "attn", "proj", "ffn2")):
    nc = bass.Bass("TRN2", target_bir_lowering=False)
    d = nc.dram_tensor
    x_in = d("x", [S, DM], F32, kind="ExternalInput").ap()
    wqkvT_p2 = d("wqkvT_p2", [48, DM, 128], F32, kind="ExternalInput").ap()
    wuT_blk = d("wuT_blk", [64, DM, 128], F32, kind="ExternalInput").ap()
    wgT_blk = d("wgT_blk", [64, DM, 128], F32, kind="ExternalInput").ap()
    woutT = d("woutT", [DFF, DM], F32, kind="ExternalInput").ap()
    gateT = d("gateT", [DM, NE], F32, kind="ExternalInput").ap()
    expb = d("expb", [NE, DM], F32, kind="ExternalInput").ap()
    expT_blk = d("expT_blk", [NE, 4, DM, 512], F32, kind="ExternalInput").ap()
    y_out = d("y", [S, DM], F32, kind="ExternalOutput").ap()
    xspill = d("xspill", [S, DM], F32, kind="Internal").ap()
    scr_r = d("scr_r", [2, NT, 128], F32, kind="Internal").ap()
    qkv_dram = d("qkvd", [3 * DM, S], F32, kind="Internal").ap()

    c_ident_bf = nc.inline_tensor(np.eye(128, dtype=ml_dtypes.bfloat16), "c_ib").ap()
    c_ident_f = nc.inline_tensor(np.eye(128, dtype=np.float32), "c_if").ap()
    c_eps = nc.inline_tensor(
        np.tile(np.array([[EPS_LN, EPS_RMS, 1e-6, 0.0]], dtype=np.float32), (128, 1)),
        "c_eps").ap()

    def at(name, shape, dtype, offset):
        offset += SB_BASE
        per = int(np.prod(shape[1:])) * mybir.dt.size(dtype)
        assert offset + per <= SB_LIMIT, f"sbuf overflow {name}: {offset + per}"
        return nc.alloc_sbuf_tensor_at(name, list(shape), dtype, offset=offset)

    o = [0]

    def nxt(name, shape, dtype):
        per = int(np.prod(shape[1:])) * mybir.dt.size(dtype)
        off = (o[0] + 31) // 32 * 32
        t = at(name, shape, dtype, off)
        o[0] = off + per
        return t

    ident_bf = nxt("ident_bf", [128, 128], BF16)
    ident_f = nxt("ident_f", [128, 128], F32)
    ones1_f = nxt("ones1_f", [1, 128], F32)
    gateT_bf = nxt("gateT_bf", [128, ND * NE], BF16)
    expb_bf = nxt("expb_bf", [NE, DM], BF16)
    wmaskT = nxt("wmaskT", [NE, S], F32)
    wmaskT_bf = nxt("wmaskT_bf", [NE, S], BF16)
    wm = nxt("wm", [128, NTH * NE], F32)
    gsc = nxt("gsc", [128, 32], F32)
    lnsc = nxt("lnsc", [128, 48], F32)
    rq_all = nxt("rq_all", [128, NT], F32)
    rk_all = nxt("rk_all", [128, NT], F32)
    rqkT = nxt("rqkT", [NT, 128], F32)
    ln_tm_off = (o[0] + 31) // 32 * 32
    ln_tm = nxt("ln_tm", [128, DM], BF16)
    ln_f32 = nxt("ln_f32", [128, DM], F32)
    rqh_row = at("rqh_row", [1, HALF], F32, ln_tm_off)
    rkh_row = at("rkh_row", [1, HALF], F32, ln_tm_off + 4096)
    zeros = nxt("zeros", [128, HALF], F32)
    stats_all = nxt("stats_all", [128, NT * 8 * 6], F32)
    rkb = nxt("rkb", [128, HALF], F32)
    ceps = nxt("ceps", [128, 4], F32)
    assert o[0] <= 49152, o[0]

    R1, R2, R2B, R3 = 49152, 114688, 147456, 180224
    xr1 = at("xr1", [128, NTH * DM], F32, R1)
    xrt = [at(f"xrt{i}", [128, DM], F32, R1 + i * 8192) for i in range(2)]
    wqk1 = [at(f"wqk1_{i}", [128, 16 * 512], BF16, R1 + i * 16384) for i in range(2)]
    yT = at("yT", [128, ND * S], BF16, R1)
    xr2 = at("xr2", [128, NTH * DM], F32, R1)
    lnT = at("lnT", [128, ND * S], BF16, R2)
    lnTh = at("lnTh", [128, ND * HALF], BF16, R2)
    wblk = [at(f"wblk{i}", [128, 16 * 512], BF16, R2B + i * 16384) for i in range(2)]
    h_sg = at("h_sg", [128, 16 * HALF], BF16, R2)
    wob = [at(f"wob{i}", [128, 16 * 512], BF16, R2B + i * 16384) for i in range(2)]
    wst = [at(f"wst{i}", [128, ND * 128], BF16, R3 + i * 4096) for i in range(2)]
    at1 = at("at1", [128, HALF], F32, R3 + 8192)
    at2 = at("at2", [128, HALF], F32, R3 + 12288)
    at3 = at("at3", [128, HALF], F32, R3 + 16384)
    at4 = at("at4", [128, HALF], F32, R3 + 20480)
    rqb = at("rqb", [128, HALF], F32, R3 + 24576)
    silu_t = at("silu_t", [128, HALF], F32, ln_tm_off)
    tmpout = [at(f"tmpout{i}", [128, 512], F32, ln_tm_off + 4096 + i * 2048)
              for i in range(2)]
    qkv_sb = [at(f"qkvsb{i}", [128, DM], F32, R3 + 8192 + i * 8192) for i in range(2)]
    accq = at("accq", [128, DM], F32, R1)
    acck = at("acck", [128, DM], F32, R1 + 8192)
    sq_t = at("sq_t", [128, DM], F32, R1 + 16384)
    ones_col = at("ones_col", [128, 1], F32, R1 + 24576)
    vstrip = [at(f"vstrip{i}", [128, 512], F32, ln_tm_off + 8192 + i * 2048)
              for i in range(2)]

    ctx = ExitStack()
    pb = [ctx.enter_context(nc.psum_tensor(f"pb{i}", [128, 512], F32)) for i in range(8)]

    def pbf(i):
        return pb[i][:].bitcast(BF16)

    sems = {}

    def S_(name):
        if name not in sems:
            sems[name] = Sem(nc, "s_" + name)
        return sems[name]

    s_x = S_("x")
    s_pe = S_("pe")
    s_ve = S_("ve")
    s_se = S_("se")
    s_ev = S_("ev")
    s_sp = S_("sp")
    s_ad = S_("ad")
    s_od = S_("od")
    s_rr = S_("rr")
    adp_sems = [S_("adp0"), S_("adp1")]
    wsets = {}
    for nm in ("A", "B", "C", "D"):
        wsets[nm] = {"ld": [S_("wl" + nm + "0"), S_("wl" + nm + "1")], "i": 0, "done": []}

    def w_stream(g, setname, buf_list, src_ap):
        ws = wsets[setname]
        i = ws["i"]
        buf = buf_list[i % 2]
        if i >= 2:
            sem, val = ws["done"][i - 2]
            g.wait_ge(sem.h, val)
        nblk = src_ap.shape[1]
        ws["ld"][i % 2].inc(g.dma_start(
            out=buf[:].rearrange("p (b i) -> p b i", b=nblk), in_=src_ap), 16)
        ws["i"] = i + 1
        return buf

    def vedge(v, instr):
        s_ve.inc(instr)
        v.wait_ge(s_ve.h, s_ve.n)

    def sedge(e, instr):
        s_se.inc(instr)
        e.wait_ge(s_se.h, s_se.n)

    blk_ctx = nc.Block()
    blk = blk_ctx.__enter__()

    s_bar = S_("bar")
    bar_n = [0]

    def barrier():
        bar_n[0] += 1
        tgt = 5 * bar_n[0]
        for ename in ("sync", "tensor", "vector", "scalar", "gpsimd"):
            def _f(e, tgt=tgt):
                e.drain()
                e.nop().then_inc(s_bar.h, 1)
                e.wait_ge(s_bar.h, tgt)
            getattr(blk, ename)(_f)

    s_cg = S_("cg")

    @blk.sync
    def _(sy):
        s_x.inc(sy.dma_start(out=ident_bf[:], in_=c_ident_bf[:]), 16)
        s_x.inc(sy.dma_start(out=ceps[:], in_=c_eps[:]), 16)
        s_x.inc(sy.dma_start(out=ident_f[:], in_=c_ident_f[:]), 16)
        sy.wait_ge(s_x.h, s_x.n)

    @blk.gpsimd
    def _(g):
        s_cg.inc(g.dma_start(out=gateT_bf[:].rearrange("p (c e) -> p c e", e=NE),
                             in_=gateT.rearrange("(c p) e -> p c e", p=128)), 16)
        s_cg.inc(g.dma_start(out=expb_bf[:], in_=expb), 16)
        g.wait_ge(s_cg.h, s_cg.n)

    @blk.vector
    def _(v):
        v.memset(ones1_f[:], 1.0)
        s_ve.inc(v.memset(zeros[:], 0.0))

    barrier()

    s_lnt = S_("lnt")
    ln2_tm = [at(f"ln2tm{i}", [128, DM], BF16, R3 + i * 4096) for i in range(2)]
    a1f32 = [at(f"a1f{i}", [128, DM], F32, R3 + 8192 + i * 8192) for i in range(2)]

    def emit_ln_batch(srcs, dst_lnT, col_of, rms_fold, accum_rows=None):
        n = len(srcs)
        L = lnsc
        mean_str = L[:, 0:2 * n:2]
        var_str = L[:, 1:2 * n:2]

        @blk.vector
        def _(v):
            for ci in range(n):
                for i in range(4):
                    ins = v.bn_stats(stats_all[:, ci * 24 + i * 6: ci * 24 + (i + 1) * 6],
                                     srcs[ci][:, i * 512:(i + 1) * 512])
            vedge(v, ins)
            for ci in range(n):
                ins = v.bn_aggr(L[:, 2 * ci:2 * ci + 2], stats_all[:, ci * 24:ci * 24 + 24])
            vedge(v, ins)

        @blk.scalar
        def _(e):
            e.wait_ge(s_ve.h, s_ve.n)
            sedge(e, e.activation(L[:, 16:16 + n], var_str, AF.Ln, bias=ceps[:, 0:1]))
            sedge(e, e.activation(L[:, 16:16 + n], L[:, 16:16 + n], AF.Exp, scale=-0.5))

        if rms_fold:
            @blk.vector
            def _(v):
                v.wait_ge(s_se.h, s_se.n)
                vedge(v, v.tensor_tensor(L[:, 40:40 + n], L[:, 16:16 + n],
                                         L[:, 16:16 + n], op=ALU.mult))
                vedge(v, v.tensor_tensor(L[:, 40:40 + n], var_str, L[:, 40:40 + n],
                                         op=ALU.mult))

            @blk.scalar
            def _(e):
                e.wait_ge(s_ve.h, s_ve.n)
                sedge(e, e.activation(L[:, 40:40 + n], L[:, 40:40 + n], AF.Ln,
                                      bias=ceps[:, 1:2]))
                sedge(e, e.activation(L[:, 40:40 + n], L[:, 40:40 + n], AF.Exp, scale=-0.5))

        @blk.vector
        def _(v):
            v.wait_ge(s_se.h, s_se.n)
            if rms_fold:
                vedge(v, v.tensor_tensor(L[:, 24:24 + n], L[:, 16:16 + n], L[:, 40:40 + n],
                                         op=ALU.mult))
            else:
                vedge(v, v.tensor_copy(L[:, 24:24 + n], L[:, 16:16 + n]))
            ins = v.scalar_tensor_tensor(L[:, 32:32 + n], mean_str, -1.0, L[:, 24:24 + n],
                                         op0=ALU.mult, op1=ALU.mult)
            vedge(v, ins)

        affine_ve = [0] * n
        ad_marks = [adp_sems[0].n, adp_sems[1].n]
        width = dst_lnT.shape[1] // ND

        def emit_affine(ci):
            @blk.vector
            def _(v):
                if ci >= 2:
                    v.wait_ge(s_pe.h, tr_done[ci - 2])
                ins = v.tensor_scalar(ln2_tm[ci % 2][:], srcs[ci][:],
                                      L[:, 24 + ci:25 + ci], L[:, 32 + ci:33 + ci],
                                      op0=ALU.mult, op1=ALU.add)
                if accum_rows is None:
                    vedge(v, ins)
                else:
                    s_ve.inc(ins)
                    v.wait_ge(adp_sems[ci % 2].h, ad_marks[ci % 2])
                    ins = v.tensor_scalar(a1f32[ci % 2][:], srcs[ci][:],
                                          L[:, 24 + ci:25 + ci], L[:, 32 + ci:33 + ci],
                                          op0=ALU.mult, op1=ALU.add)
                    vedge(v, ins)
            affine_ve[ci] = s_ve.n

        tr_done = [0] * n
        emit_affine(0)
        for ci in range(n):
            col = col_of(ci)

            @blk.tensor
            def _(t):
                t.wait_ge(s_ve.h, affine_ve[ci])
                t.wait_ge(s_lnt.h, s_lnt.n)
                for h8 in range(2):
                    bank = 6 + h8
                    for i in range(8):
                        dt = h8 * 8 + i
                        ins = t.transpose(pbf(bank)[:, i * 128:(i + 1) * 128],
                                          ln2_tm[ci % 2][:, dt * 128:(dt + 1) * 128],
                                          ident_bf[:])
                    s_pe.inc(ins)
            tr_done[ci] = s_pe.n

            if accum_rows is not None:
                @blk.gpsimd
                def _(g):
                    g.wait_ge(s_ve.h, affine_ve[ci])
                    rows = accum_rows + ci * 128
                    ins = g.dma_start(out=xspill[rows:rows + 128, :],
                                      in_=a1f32[ci % 2][:], accum_op=ALU.add)
                    adp_sems[ci % 2].inc(ins, 16)
                ad_marks[ci % 2] = adp_sems[ci % 2].n

            if ci + 1 < n:
                emit_affine(ci + 1)

            @blk.vector
            def _(v):
                for h8 in range(2):
                    v.wait_ge(s_pe.h, tr_done[ci] - 1 + h8)
                    bank = 6 + h8
                    src = pbf(bank)[:, 0:1024].rearrange("p (b i) -> p b i", b=8)
                    dsts = dst_lnT[:].rearrange("p (dt w) -> p dt w", dt=ND)[
                        :, h8 * 8:(h8 + 1) * 8, col:col + 128]
                    ins = v.tensor_copy(dsts, src)
                s_lnt.inc(ins)
                v.wait_ge(s_lnt.h, s_lnt.n)

    def emit_gating_batch(gbase, lnT_t):
        G = gsc
        glog = stats_all[:, 256:320]
        mk1 = stats_all[:, 320:384]
        mk2 = stats_all[:, 384:448]
        width = lnT_t.shape[1] // ND

        @blk.tensor
        def _(t):
            t.wait_ge(s_lnt.h, s_lnt.n)
            t.wait_ge(s_ev.h, s_ev.n)
            for ci in range(NTH):
                for dt in range(ND):
                    ins = t.matmul(
                        pb[5][:, ci * 8:(ci + 1) * 8],
                        lnT_t[:, dt * width + ci * 128: dt * width + ci * 128 + 128],
                        gateT_bf[:, dt * 8:(dt + 1) * 8],
                        start=(dt == 0), stop=(dt == ND - 1))
            s_pe.inc(ins)

        @blk.vector
        def _(v):
            v.wait_ge(s_pe.h, s_pe.n)
            ins = v.tensor_copy(glog, pb[5][:, 0:64])
            s_ev.inc(ins)
            v.wait_ge(s_ev.h, s_ev.n)
            for ci in range(NTH):
                gl = glog[:, ci * 8:(ci + 1) * 8]
                vedge(v, v.reduce_max(G[:, ci:ci + 1], gl, axis=AX.X))
                vedge(v, v.tensor_scalar(mk1[:, ci * 8:(ci + 1) * 8], gl,
                                         G[:, ci:ci + 1], None, op0=ALU.is_ge))
                vedge(v, v.scalar_tensor_tensor(mk2[:, ci * 8:(ci + 1) * 8],
                                                mk1[:, ci * 8:(ci + 1) * 8], -1e30, gl,
                                                op0=ALU.mult, op1=ALU.add))
                vedge(v, v.reduce_max(G[:, 8 + ci:9 + ci],
                                      mk2[:, ci * 8:(ci + 1) * 8], axis=AX.X))
                vedge(v, v.tensor_scalar(mk2[:, ci * 8:(ci + 1) * 8],
                                         mk2[:, ci * 8:(ci + 1) * 8],
                                         G[:, 8 + ci:9 + ci], None, op0=ALU.is_ge))
            vedge(v, v.tensor_tensor(G[:, 16:24], G[:, 0:8], G[:, 8:16], op=ALU.subtract))

        @blk.scalar
        def _(e):
            e.wait_ge(s_ve.h, s_ve.n)
            sedge(e, e.activation(G[:, 24:32], G[:, 16:24], AF.Exp, scale=-1.0))
            sedge(e, e.activation(G[:, 24:32], G[:, 24:32], AF.Ln, bias=1.0))
            sedge(e, e.activation(G[:, 24:32], G[:, 24:32], AF.Exp, scale=-1.0))

        @blk.vector
        def _(v):
            v.wait_ge(s_se.h, s_se.n)
            vedge(v, v.tensor_scalar_mul(G[:, 24:32], G[:, 24:32], 0.5))
            vedge(v, v.tensor_scalar(G[:, 8:16], G[:, 24:32], -1.0, 0.5,
                                     op0=ALU.mult, op1=ALU.add))
            for ci in range(NTH):
                vedge(v, v.tensor_scalar_mul(mk1[:, ci * 8:(ci + 1) * 8],
                                             mk1[:, ci * 8:(ci + 1) * 8],
                                             G[:, 24 + ci:25 + ci]))
                vedge(v, v.tensor_scalar_mul(mk2[:, ci * 8:(ci + 1) * 8],
                                             mk2[:, ci * 8:(ci + 1) * 8],
                                             G[:, 8 + ci:9 + ci]))
            vedge(v, v.tensor_tensor(wm[:], mk1[:, 0:64], mk2[:, 0:64], op=ALU.add))

        for rnd in range(2):
            @blk.tensor
            def _(t):
                t.wait_ge(s_ve.h, s_ve.n)
                t.wait_ge(s_ev.h, s_ev.n)
                for j in range(4):
                    ci = rnd * 4 + j
                    ins = t.transpose(pb[5][0:8, j * 128:(j + 1) * 128],
                                      wm[:, ci * 8:(ci + 1) * 8], ident_f[:])
                s_pe.inc(ins)

            @blk.vector
            def _(v):
                v.wait_ge(s_pe.h, s_pe.n)
                gc = (gbase + rnd * 4) * 128
                ins = v.tensor_copy(wmaskT[0:8, gc:gc + 512], pb[5][0:8, 0:512])
                s_ev.inc(ins)
                v.wait_ge(s_ev.h, s_ev.n)

        @blk.vector
        def _(v):
            gc = gbase * 128
            vedge(v, v.tensor_copy(wmaskT_bf[0:8, gc:gc + HALF], wmaskT[0:8, gc:gc + HALF]))

    def emit_ffn(ffn_idx, xr, src_dram, do_output):
        for half in range(2):
            gbase = half * NTH
            rows = half * HALF

            @blk.sync
            def _(sy):
                if ffn_idx == 1:
                    sy.wait_ge(s_ad.h, s_ad.n)
                    for sp_ in adp_sems:
                        sy.wait_ge(sp_.h, sp_.n)
                sy.wait_ge(s_od.h, s_od.n)
                sy.wait_ge(s_sp.h, s_sp.n)
                ins = sy.dma_start(
                    out=xr[:].rearrange("p (c d) -> p c d", c=NTH),
                    in_=src_dram[rows:rows + HALF, :].rearrange("(c p) d -> p c d", p=128))
                s_x.inc(ins, 16)

            xload_t = s_x.n

            @blk.vector
            def _(v):
                v.wait_ge(s_x.h, xload_t)

            emit_ln_batch([xr[:, tc * DM:(tc + 1) * DM] for tc in range(NTH)],
                          lnTh, lambda ci: ci * 128, rms_fold=False)
            emit_gating_batch(gbase, lnTh)

            for tc in range(NTH):
                gc = (gbase + tc) * 128

                @blk.tensor
                def _(t):
                    t.wait_ge(s_ve.h, s_ve.n)
                    t.wait_ge(s_ev.h, s_ev.n)
                    for ot in range(4):
                        ins = t.matmul(pb[ot][:], wmaskT_bf[0:8, gc:gc + 128],
                                       expb_bf[0:8, ot * 512:(ot + 1) * 512],
                                       start=True, stop=True)
                        s_pe.inc(ins)

                @blk.vector
                def _(v):
                    v.wait_ge(s_pe.h, s_pe.n)
                    for ot in range(4):
                        dst = xr[:, tc * DM + ot * 512: tc * DM + (ot + 1) * 512]
                        ins = v.tensor_tensor(dst, dst, pb[ot][:], op=ALU.add)
                        s_ev.inc(ins)
                    v.wait_ge(s_ev.h, s_ev.n)

            ev_base = s_ev.n
            nb = 0
            for ie in range(NE):
                for ot in range(4):
                    @blk.gpsimd
                    def _(g):
                        w_stream(g, "A", wblk,
                                 expT_blk[ie, ot].rearrange("(dc p) o -> p dc o", p=128))

                    ld_s = wsets["A"]["ld"][(wsets["A"]["i"] - 1) % 2]
                    ld_t = ld_s.n
                    bi = nb

                    @blk.tensor
                    def _(t):
                        t.wait_ge(ld_s.h, ld_t)
                        if bi == 0:
                            t.wait_ge(s_ev.h, ev_base)
                        for tc in range(NTH):
                            if bi > 0:
                                t.wait_ge(s_ev.h, ev_base + (bi - 1) * NTH + tc + 1)
                            for dt in range(ND):
                                ins = t.matmul(
                                    pb[tc][:],
                                    lnTh[:, dt * HALF + tc * 128: dt * HALF + tc * 128 + 128],
                                    wblk[bi % 2][:, dt * 512:(dt + 1) * 512],
                                    start=(dt == 0), stop=(dt == ND - 1))
                            s_pe.inc(ins)
                    wsets["A"]["done"].append((s_pe, s_pe.n))

                    pe_base = s_pe.n - NTH

                    @blk.vector
                    def _(v):
                        for tc in range(NTH):
                            v.wait_ge(s_pe.h, pe_base + tc + 1)
                            dst = xr[:, tc * DM + ot * 512: tc * DM + (ot + 1) * 512]
                            ins = v.scalar_tensor_tensor(
                                dst, pb[tc][:], wm[:, tc * 8 + ie: tc * 8 + ie + 1], dst,
                                op0=ALU.mult, op1=ALU.add)
                            s_ev.inc(ins)
                    nb += 1

            @blk.sync
            def _(sy):
                sy.wait_ge(s_ev.h, s_ev.n)
                tgt = y_out if do_output else xspill
                ins = sy.dma_start(
                    out=tgt[rows:rows + HALF, :].rearrange("(c p) d -> p c d", p=128),
                    in_=xr[:].rearrange("p (c d) -> p c d", c=NTH))
                (s_od if do_output else s_sp).inc(ins, 16)

            barrier()

    a1x = at("a1x", [128, NTH * DM], F32, R1)

    def emit_a1():
        ve_group_done = [0]
        for grp in range(2):
            rows = grp * HALF

            @blk.sync
            def _(sy):
                if grp == 0:
                    sy.wait_ge(s_sp.h, s_sp.n)
                else:
                    sy.wait_ge(s_ve.h, ve_group_done[0])
                ins = sy.dma_start(
                    out=a1x[:].rearrange("p (c d) -> p c d", c=NTH),
                    in_=xspill[rows:rows + HALF, :].rearrange("(c p) d -> p c d", p=128))
                s_x.inc(ins, 16)

            xl = s_x.n

            @blk.vector
            def _(v):
                v.wait_ge(s_x.h, xl)

            emit_ln_batch([a1x[:, tc * DM:(tc + 1) * DM] for tc in range(NTH)],
                          lnT, lambda ci, g=grp: (g * NTH + ci) * 128,
                          rms_fold=True, accum_rows=rows)
            ve_group_done[0] = s_ve.n

        barrier()

    s_qw = [S_("qw0"), S_("qw1")]
    s_qld = S_("qld")
    chain_ve = [0]
    a3_done_ve = [0, 0]
    yT_ready_ve = [0, 0]
    bank_guard = {0: 0, 1: 0}
    part1_qw = []
    carry_kv = stats_all[:, 0:NT]
    carry_w = stats_all[:, NT:2 * NT]

    def a3_chain(half, fc, v_in_at2=False):
        cols = slice(half * HALF, (half + 1) * HALF)

        @blk.sync
        def _(sy):
            sy.wait_ge(s_ve.h, chain_ve[0])
            sy.wait_ge(s_rr.h, s_rr.n)
            for sp_ in s_qw:
                sy.wait_ge(sp_.h, sp_.n)
            sy.dma_start(out=at3[:], in_=qkv_dram[fc * 128:(fc + 1) * 128, cols])\
                .then_inc(s_qld.h, 16)
            s_qld.n += 16
            sy.dma_start(out=at1[:], in_=qkv_dram[DM + fc * 128: DM + (fc + 1) * 128, cols])\
                .then_inc(s_qld.h, 16)
            s_qld.n += 16
            if not v_in_at2:
                sy.dma_start(out=at2[:],
                             in_=qkv_dram[2 * DM + fc * 128: 2 * DM + (fc + 1) * 128, cols])\
                    .then_inc(s_qld.h, 16)
                s_qld.n += 16
        qld_t = s_qld.n

        @blk.vector
        def _(v):
            v.wait_ge(s_qld.h, qld_t)
            v.wait_ge(s_ev.h, s_ev.n)
            vedge(v, v.tensor_tensor(at1[:], at1[:], rkb[:], op=ALU.mult))

        @blk.scalar
        def _(e):
            e.wait_ge(s_ve.h, s_ve.n)
            sedge(e, e.activation(at1[:], at1[:], AF.Exp))
        exp_se = s_se.n

        @blk.vector
        def _(v):
            v.wait_ge(s_se.h, exp_se)
            vedge(v, v.tensor_tensor(at2[:], at2[:], at1[:], op=ALU.mult))
            vedge(v, v.tensor_tensor(at3[:], at3[:], rqb[:], op=ALU.mult))
            vedge(v, v.tensor_tensor_scan(at4[:], at1[:], zeros[:],
                                          carry_w[:, fc:fc + 1],
                                          op0=ALU.add, op1=ALU.add))
            vedge(v, v.tensor_copy(carry_w[:, fc:fc + 1], at4[:, HALF - 1:HALF]))
        wcum_ve = s_ve.n

        @blk.scalar
        def _(e):
            e.wait_ge(s_ve.h, wcum_ve - 2)
            sedge(e, e.activation(at3[:], at3[:], AF.Exp, scale=-1.0))
            sedge(e, e.activation(at3[:], at3[:], AF.Ln, bias=1.0))
            sedge(e, e.activation(at3[:], at3[:], AF.Exp, scale=-1.0))
            e.wait_ge(s_ve.h, wcum_ve)
            sedge(e, e.activation(at4[:], at4[:], AF.Ln, bias=ceps[:, 2:3]))
            sedge(e, e.activation(at4[:], at4[:], AF.Exp, scale=-1.0))

        @blk.vector
        def _(v):
            vedge(v, v.tensor_tensor_scan(at1[:], at2[:], zeros[:],
                                          carry_kv[:, fc:fc + 1],
                                          op0=ALU.add, op1=ALU.add))
            vedge(v, v.tensor_copy(carry_kv[:, fc:fc + 1], at1[:, HALF - 1:HALF]))
            v.wait_ge(s_se.h, s_se.n)
            vedge(v, v.tensor_tensor(at1[:], at1[:], at4[:], op=ALU.mult))
            vedge(v, v.tensor_tensor(
                yT[:, fc * S + half * HALF: fc * S + (half + 1) * HALF],
                at1[:], at3[:], op=ALU.mult))
        chain_ve[0] = s_ve.n

    def emit_bcast(half):
        @blk.sync
        def _(sy):
            sy.wait_ge(s_rr.h, s_rr.n)
            flat0 = scr_r[0].rearrange("a b -> (a b)").unsqueeze(0)
            flat1 = scr_r[1].rearrange("a b -> (a b)").unsqueeze(0)
            sy.dma_start(out=rqh_row[:], in_=flat0[:, half * HALF:(half + 1) * HALF])\
                .then_inc(s_x.h, 16)
            s_x.n += 16
            sy.dma_start(out=rkh_row[:], in_=flat1[:, half * HALF:(half + 1) * HALF])\
                .then_inc(s_x.h, 16)
            s_x.n += 16
        xl = s_x.n

        @blk.tensor
        def _(t):
            t.wait_ge(s_x.h, xl)
            t.wait_ge(s_ve.h, chain_ve[0])
            t.wait_ge(s_ev.h, s_ev.n)
            t.wait_ge(s_se.h, s_se.n)
            for i in range(2):
                t.matmul(pb[i][:], ones1_f[:], rqh_row[:, i * 512:(i + 1) * 512],
                         start=True, stop=True)
            for i in range(2):
                ins = t.matmul(pb[2 + i][:], ones1_f[:], rkh_row[:, i * 512:(i + 1) * 512],
                               start=True, stop=True)
            s_pe.inc(ins)

        @blk.vector
        def _(v):
            v.wait_ge(s_pe.h, s_pe.n)
            v.tensor_copy(rqb[:, 0:512], pb[0][:])
            v.tensor_copy(rqb[:, 512:1024], pb[1][:])
            v.tensor_copy(rkb[:, 0:512], pb[2][:])
            ins = v.tensor_copy(rkb[:, 512:1024], pb[3][:])
            s_ev.inc(ins)
            v.wait_ge(s_ev.h, s_ev.n)
        bank_guard[0] = s_ev.n

    def emit_qkv_attn():
        @blk.vector
        def _(v):
            for sp_ in adp_sems:
                v.wait_ge(sp_.h, sp_.n)
            v.memset(accq[:], 0.0)
            v.memset(acck[:], 0.0)
            v.memset(ones_col[:], 1.0)
            v.memset(carry_kv, 0.0)
            vedge(v, v.memset(carry_w, 0.0))

        def qkv_chunk(i, jc, do_sq):
            @blk.gpsimd
            def _(g):
                w_stream(g, "C", wst, wqkvT_p2[jc].rearrange("(dc p) j -> p dc j", p=128))

            ld_s = wsets["C"]["ld"][(wsets["C"]["i"] - 1) % 2]
            ld_t = ld_s.n
            banks = (0, 1, 2, 3) if i % 2 == 0 else (4, 5, 6, 7)
            evq_guard = bank_guard[i % 2]

            @blk.tensor
            def _(t):
                t.wait_ge(ld_s.h, ld_t)
                t.wait_ge(s_ev.h, evq_guard)
                for ni in range(4):
                    for dt in range(ND):
                        ins = t.matmul(
                            pb[banks[ni]][:],
                            wst[(wsets["C"]["i"] - 1) % 2][:, dt * 128:(dt + 1) * 128],
                            lnT[:, dt * S + ni * 512:(dt) * S + (ni + 1) * 512],
                            start=(dt == 0), stop=(dt == ND - 1))
                    s_pe.inc(ins)
            wsets["C"]["done"].append((s_pe, s_pe.n))
            pe_b = s_pe.n - 4

            if do_sq is not None:
                @blk.vector
                def _(v):
                    v.wait_ge(s_qw[i % 2].h, s_qw[i % 2].n)
                    for ni in range(4):
                        v.wait_ge(s_pe.h, pe_b + ni + 1)
                        ins = v.tensor_copy(qkv_sb[i % 2][:, ni * 512:(ni + 1) * 512],
                                            pb[banks[ni]][:])
                        s_ev.inc(ins)
                    v.wait_ge(s_ev.h, s_ev.n)
                ev_t = s_ev.n

                @blk.sync
                def _(sy):
                    sy.wait_ge(s_ev.h, ev_t)
                    ins = sy.dma_start(out=qkv_dram[jc * 128:(jc + 1) * 128, :],
                                       in_=qkv_sb[i % 2][:])
                    s_qw[i % 2].inc(ins, 16)

                @blk.scalar
                def _(e):
                    e.wait_ge(s_ev.h, ev_t)
                    e.wait_ge(s_ve.h, s_ve.n)
                    sedge(e, e.activation(sq_t[:], qkv_sb[i % 2][:], AF.Square))

                @blk.vector
                def _(v):
                    v.wait_ge(s_se.h, s_se.n)
                    vedge(v, v.tensor_tensor(do_sq, do_sq, sq_t[:], op=ALU.add))
            else:
                @blk.vector
                def _(v):
                    v.wait_ge(s_ve.h, chain_ve[0])
                    for s_, n_ in part1_qw:
                        v.wait_ge(s_.h, n_)
                    for ni in range(2):
                        v.wait_ge(s_pe.h, pe_b + ni + 1)
                        ins = v.tensor_copy(at2[:, ni * 512:(ni + 1) * 512],
                                            pb[banks[ni]][:])
                        s_ev.inc(ins)
                    for ni in range(2, 4):
                        v.wait_ge(s_pe.h, pe_b + ni + 1)
                        v.wait_ge(s_qw[ni % 2].h, s_qw[ni % 2].n)
                        ins = v.tensor_copy(vstrip[ni % 2][:], pb[banks[ni]][:])
                        s_ev.inc(ins)
                    v.wait_ge(s_ev.h, s_ev.n)
                ev_t = s_ev.n

                @blk.sync
                def _(sy):
                    sy.wait_ge(s_ev.h, ev_t - 1)
                    ins = sy.dma_start(
                        out=qkv_dram[jc * 128:(jc + 1) * 128, HALF:HALF + 512],
                        in_=vstrip[0][:])
                    s_qw[0].inc(ins, 16)
                    sy.wait_ge(s_ev.h, ev_t)
                    ins = sy.dma_start(
                        out=qkv_dram[jc * 128:(jc + 1) * 128, HALF + 512:S],
                        in_=vstrip[1][:])
                    s_qw[1].inc(ins, 16)
            bank_guard[i % 2] = s_ev.n

        idx = 0
        bank_guard[0] = s_ev.n
        bank_guard[1] = s_ev.n
        for jc in range(ND):
            qkv_chunk(idx, jc, accq[:])
            idx += 1
        for jc in range(ND, 2 * ND):
            qkv_chunk(idx, jc, acck[:])
            idx += 1

        for which, acc in ((0, accq), (1, acck)):
            @blk.tensor
            def _(t):
                t.wait_ge(s_ve.h, s_ve.n)
                t.wait_ge(s_ev.h, s_ev.n)
                t.wait_ge(s_se.h, s_se.n)
                for ni in range(4):
                    ins = t.matmul(pb[ni][0:1, :], ones_col[:],
                                   acc[:, ni * 512:(ni + 1) * 512],
                                   start=True, stop=True)
                    s_pe.inc(ins)

            @blk.scalar
            def _(e):
                e.wait_ge(s_pe.h, s_pe.n)
                e.wait_ge(s_rr.h, s_rr.n)
                for ni in range(3):
                    e.activation(sq_t[0:1, ni * 512:(ni + 1) * 512], pb[ni][0:1, :],
                                 AF.Ln, bias=ceps[0:1, 1:2], scale=1.0 / DM)
                sedge(e, e.activation(sq_t[0:1, 1536:2048], pb[3][0:1, :],
                                      AF.Ln, bias=ceps[0:1, 1:2], scale=1.0 / DM))
                sedge(e, e.activation(sq_t[0:1, 0:DM], sq_t[0:1, 0:DM], AF.Exp,
                                      scale=-0.5))
            se_t = s_se.n

            @blk.sync
            def _(sy):
                sy.wait_ge(s_se.h, se_t)
                ins = sy.dma_start(out=scr_r[which].rearrange("a b -> (a b)").unsqueeze(0),
                                   in_=sq_t[0:1, 0:DM])
                s_rr.inc(ins, 16)

        part1_qw.extend((s_, s_.n) for s_ in s_qw)
        emit_bcast(0)
        for fc in range(ND):
            i = idx
            jc = 2 * ND + fc
            qkv_chunk(i, jc, None)
            a3_chain(0, fc, v_in_at2=True)
            idx += 1
        a3_done_ve[0] = s_ve.n
        yT_ready_ve[0] = s_ve.n

        emit_bcast(1)

    def emit_a3h1_hook(mi):
        a3_chain(1, mi)
        if mi == ND - 1:
            a3_done_ve[1] = s_ve.n
            yT_ready_ve[1] = s_ve.n

    def emit_p5(pre_m_hook=None):
        p5_entry_ev = s_ev.n
        s_swh = S_("swh")
        s_opv = S_("opv")
        s_adp = adp_sems
        pending = []
        planned = [s_adp[0].n // 16, s_adp[1].n // 16]
        nD_blocks = [0]

        def flush_one():
            if pending:
                pending.pop(0)()

        for half in range(2):
            for sg in range(4):
                swh_base = s_swh.n
                opv_sg_guard = s_opv.n
                for mi in range(16):
                    if pre_m_hook is not None and half == 0 and sg == 0:
                        pre_m_hook(mi)
                    m = sg * 16 + mi
                    pe_marks = []
                    for gi, src_blk in ((0, wuT_blk), (1, wgT_blk)):
                        @blk.gpsimd
                        def _(g):
                            w_stream(g, "C", wst,
                                     src_blk[m].rearrange("(dc p) j -> p dc j", p=128))
                        if gi == 1:
                            flush_one()

                        ld_s = wsets["C"]["ld"][(wsets["C"]["i"] - 1) % 2]
                        ld_t = ld_s.n
                        banks = (((0, 1), (2, 3)) if mi % 2 == 0 else ((4, 5), (6, 7)))[gi]

                        @blk.tensor
                        def _(t):
                            t.wait_ge(ld_s.h, ld_t)
                            if half == 0 and sg == 0 and mi < 2:
                                t.wait_ge(s_ev.h, p5_entry_ev)
                            if gi == 0:
                                t.wait_ge(s_ve.h, yT_ready_ve[half])
                                if mi >= 2:
                                    t.wait_ge(s_swh.h, swh_base + mi - 1)
                                if mi == 0:
                                    t.wait_ge(s_opv.h, opv_sg_guard)
                            for dt in range(ND):
                                for ni in range(2):
                                    ins = t.matmul(
                                        pb[banks[ni]][:],
                                        wst[(wsets["C"]["i"] - 1) % 2][:, dt * 128:(dt + 1) * 128],
                                        yT[:, dt * S + half * HALF + ni * 512:
                                           dt * S + half * HALF + (ni + 1) * 512],
                                        start=(dt == 0), stop=(dt == ND - 1))
                            s_pe.inc(ins)
                        wsets["C"]["done"].append((s_pe, s_pe.n))
                        pe_marks.append(s_pe.n)

                    pe_u, pe_g = pe_marks
                    bu, bg = (((0, 1), (2, 3)) if mi % 2 == 0 else ((4, 5), (6, 7)))

                    @blk.scalar
                    def _(e):
                        e.wait_ge(s_pe.h, pe_g)
                        e.wait_ge(s_swh.h, swh_base + mi)
                        e.activation(silu_t[:, 0:512], pb[bg[0]][:], AF.Silu)
                        sedge(e, e.activation(silu_t[:, 512:1024], pb[bg[1]][:], AF.Silu))

                    @blk.vector
                    def _(v):
                        v.wait_ge(s_se.h, s_se.n)
                        v.wait_ge(s_pe.h, pe_u)
                        col = mi * HALF
                        v.tensor_tensor(h_sg[:, col:col + 512], pb[bu[0]][:],
                                        silu_t[:, 0:512], op=ALU.mult)
                        ins = v.tensor_tensor(h_sg[:, col + 512:col + 1024], pb[bu[1]][:],
                                              silu_t[:, 512:1024], op=ALU.mult)
                        s_swh.inc(ins)
                        v.wait_ge(s_swh.h, s_swh.n)

                for oq in range(4):
                    @blk.gpsimd
                    def _(g):
                        if nD_blocks[0] < 2:
                            for sp_ in s_qw:
                                g.wait_ge(sp_.h, sp_.n)
                            g.wait_ge(s_ve.h, a3_done_ve[0])
                        nD_blocks[0] += 1
                        w_stream(g, "D", wob,
                                 woutT[sg * DM:(sg + 1) * DM, oq * 512:(oq + 1) * 512]
                                 .rearrange("(dc p) o -> p dc o", p=128))
                    flush_one()

                    ld_s = wsets["D"]["ld"][(wsets["D"]["i"] - 1) % 2]
                    ld_t = ld_s.n
                    opv_base = s_opv.n
                    ad_base = [planned[0], planned[1]]

                    @blk.tensor
                    def _(t):
                        t.wait_ge(ld_s.h, ld_t)
                        t.wait_ge(s_swh.h, swh_base + 16)
                        for tc in range(NTH):
                            if tc >= 2:
                                t.wait_ge(s_opv.h, opv_base + tc - 1)
                            elif oq == 0 and sg == 0 and half == 0:
                                pass
                            else:
                                t.wait_ge(s_opv.h, opv_base - 2 + tc + 1)
                            for dffc in range(16):
                                ins = t.matmul(
                                    pb[tc % 2][:],
                                    h_sg[:, dffc * HALF + tc * 128: dffc * HALF + tc * 128 + 128],
                                    wob[(wsets["D"]["i"] - 1) % 2][:, dffc * 512:(dffc + 1) * 512],
                                    start=(dffc == 0), stop=(dffc == 15))
                            s_pe.inc(ins)
                    wsets["D"]["done"].append((s_pe, s_pe.n))

                    pe_base = s_pe.n - NTH

                    @blk.vector
                    def _(v):
                        for tc in range(NTH):
                            v.wait_ge(s_pe.h, pe_base + tc + 1)
                            v.wait_ge(s_adp[tc % 2].h,
                                      (ad_base[tc % 2] + tc // 2) * 16)
                            ins = v.tensor_copy(tmpout[tc % 2][:], pb[tc % 2][:])
                            s_opv.inc(ins)

                    def mk_accum(opv_base=opv_base, half=half, sg=sg, oq=oq):
                        def emit():
                            @blk.gpsimd
                            def _(g):
                                for tc in range(NTH):
                                    g.wait_ge(s_opv.h, opv_base + tc + 1)
                                    rows = half * HALF + tc * 128
                                    ins = g.dma_start(
                                        out=xspill[rows:rows + 128,
                                                   oq * 512:(oq + 1) * 512],
                                        in_=tmpout[tc % 2][:], accum_op=ALU.add)
                                    s_adp[tc % 2].inc(ins, 16)
                        return emit

                    pending.append(mk_accum())
                    planned[0] += 4
                    planned[1] += 4

            while pending:
                flush_one()
            barrier()

    if "ffn1" in stages:
        emit_ffn(0, xr1, x_in, do_output=(stages == ("ffn1",)))
    if "attn" in stages:
        emit_a1()
        emit_qkv_attn()
    if "proj" in stages:
        emit_p5(pre_m_hook=emit_a3h1_hook)
    if "ffn2" in stages:
        emit_ffn(1, xr2, xspill, do_output=True)
    elif stages != ("ffn1",):
        @blk.sync
        def _(sy):
            for tc in range(NT):
                ins = sy.dma_start(out=xrt[0][:], in_=xspill[tc * 128:(tc + 1) * 128, :])
                s_x.inc(ins, 16)
                sy.wait_ge(s_x.h, s_x.n)
                ins = sy.dma_start(out=y_out[tc * 128:(tc + 1) * 128, :], in_=xrt[0][:])
                s_od.inc(ins, 16)
                sy.wait_ge(s_od.h, s_od.n)

    @blk.sync
    def _(sy):
        sy.wait_ge(s_od.h, s_od.n)

    blk_ctx.__exit__(None, None, None)
    ctx.close()
    return nc


def np_split_inputs(inputs):
    x = np.asarray(inputs["L0_attn_x"], dtype=np.float32)
    w_qkv = np.asarray(inputs["L0_attn_w_qkv"], dtype=np.float32)
    w_sw = np.asarray(inputs["L0_attn_w_swiglu"], dtype=np.float32)
    w_out = np.asarray(inputs["L0_attn_w_out"], dtype=np.float32)
    gate_w = np.asarray(inputs["L0_ffn_gate_w"], dtype=np.float32)
    exp_w = np.asarray(inputs["L0_ffn_expert_w"], dtype=np.float32)
    exp_b = np.asarray(inputs["L0_ffn_expert_b"], dtype=np.float32)

    wqkvT = np.ascontiguousarray(w_qkv.T)
    shared = {
        "wqkvT_p2": np.ascontiguousarray(wqkvT.reshape(DM, 48, 128).transpose(1, 0, 2)),
        "wuT_blk": np.ascontiguousarray(
            w_sw[:DFF].T.reshape(DM, 64, 128).transpose(1, 0, 2)),
        "wgT_blk": np.ascontiguousarray(
            w_sw[DFF:].T.reshape(DM, 64, 128).transpose(1, 0, 2)),
        "woutT": np.ascontiguousarray(w_out.T),
        "gateT": np.ascontiguousarray(gate_w.T),
        "expb": np.ascontiguousarray(exp_b),
        "expT_blk": np.ascontiguousarray(
            exp_w.transpose(0, 2, 1).reshape(NE, DM, 4, 512).transpose(0, 2, 1, 3)),
    }
    return [dict(shared, x=np.ascontiguousarray(x[c])) for c in range(N_CORES)]


def kernel(**inputs):
    in_maps = np_split_inputs(inputs)
    nc = build_program()
    res = run_bass_kernel_spmd(nc, in_maps, core_ids=list(range(N_CORES)))
    return np.stack([np.asarray(r["y"], dtype=np.float32) for r in res.results], axis=0)


# version 05
# speedup vs baseline: 1.1311x; 1.1311x over previous
import numpy as np
import ml_dtypes
import concourse.bass as bass
import concourse.mybir as mybir
from concourse.bass_utils import run_bass_kernel_spmd
from contextlib import ExitStack

F32 = mybir.dt.float32
BF16 = mybir.dt.bfloat16
AF = mybir.ActivationFunctionType
ALU = mybir.AluOpType
AX = mybir.AxisListType

S = 2048
DM = 2048
DFF = 8192
NE = 8
NT = S // 128
ND = DM // 128
HALF = S // 2
NTH = NT // 2
EPS_RMS = 1.1920929e-07
EPS_LN = 1e-5
N_CORES = 8
SB_BASE = 20448
SB_LIMIT = 229344


class Sem:
    __slots__ = ("h", "n")

    def __init__(self, nc, name):
        self.h = nc.alloc_semaphore(name)
        self.n = 0

    def inc(self, instr, k=1):
        instr.then_inc(self.h, k)
        self.n += k
        return instr


def build_program(stages=("ffn1", "attn", "proj", "ffn2")):
    nc = bass.Bass("TRN2", target_bir_lowering=False)
    d = nc.dram_tensor
    x_in = d("x", [S, DM], F32, kind="ExternalInput").ap()
    wqkvT_p2 = d("wqkvT_p2", [48, DM, 128], F32, kind="ExternalInput").ap()
    wuT_blk = d("wuT_blk", [64, DM, 128], F32, kind="ExternalInput").ap()
    wgT_blk = d("wgT_blk", [64, DM, 128], F32, kind="ExternalInput").ap()
    woutT = d("woutT", [DFF, DM], F32, kind="ExternalInput").ap()
    gateT = d("gateT", [DM, NE], F32, kind="ExternalInput").ap()
    expb = d("expb", [NE, DM], F32, kind="ExternalInput").ap()
    expT_blk = d("expT_blk", [NE, 4, DM, 512], F32, kind="ExternalInput").ap()
    y_out = d("y", [S, DM], F32, kind="ExternalOutput").ap()
    xspill = d("xspill", [S, DM], F32, kind="Internal").ap()
    scr_r = d("scr_r", [2, NT, 128], F32, kind="Internal").ap()
    qkv_dram = d("qkvd", [3 * DM, S], F32, kind="Internal").ap()

    c_ident_bf = nc.inline_tensor(np.eye(128, dtype=ml_dtypes.bfloat16), "c_ib").ap()
    c_ident_f = nc.inline_tensor(np.eye(128, dtype=np.float32), "c_if").ap()
    c_eps = nc.inline_tensor(
        np.tile(np.array([[EPS_LN, EPS_RMS, 1e-6, 0.0]], dtype=np.float32), (128, 1)),
        "c_eps").ap()

    def at(name, shape, dtype, offset):
        offset += SB_BASE
        per = int(np.prod(shape[1:])) * mybir.dt.size(dtype)
        assert offset + per <= SB_LIMIT, f"sbuf overflow {name}: {offset + per}"
        return nc.alloc_sbuf_tensor_at(name, list(shape), dtype, offset=offset)

    o = [0]

    def nxt(name, shape, dtype):
        per = int(np.prod(shape[1:])) * mybir.dt.size(dtype)
        off = (o[0] + 31) // 32 * 32
        t = at(name, shape, dtype, off)
        o[0] = off + per
        return t

    ident_bf = nxt("ident_bf", [128, 128], BF16)
    ident_f = nxt("ident_f", [128, 128], F32)
    ones1_f = nxt("ones1_f", [1, 128], F32)
    gateT_bf = nxt("gateT_bf", [128, ND * NE], BF16)
    expb_bf = nxt("expb_bf", [NE, DM], BF16)
    wmaskT = nxt("wmaskT", [NE, S], F32)
    wmaskT_bf = nxt("wmaskT_bf", [NE, S], BF16)
    wm = nxt("wm", [128, NTH * NE], F32)
    gsc = nxt("gsc", [128, 32], F32)
    lnsc = nxt("lnsc", [128, 48], F32)
    rq_all = nxt("rq_all", [128, NT], F32)
    rk_all = nxt("rk_all", [128, NT], F32)
    rqkT = nxt("rqkT", [NT, 128], F32)
    ln_tm_off = (o[0] + 31) // 32 * 32
    ln_tm = nxt("ln_tm", [128, DM], BF16)
    ln_f32 = nxt("ln_f32", [128, DM], F32)
    rqh_row = at("rqh_row", [1, HALF], F32, ln_tm_off)
    rkh_row = at("rkh_row", [1, HALF], F32, ln_tm_off + 4096)
    zeros = nxt("zeros", [128, HALF], F32)
    stats_all = nxt("stats_all", [128, NT * 8 * 6], F32)
    rkb = nxt("rkb", [128, HALF], F32)
    ceps = nxt("ceps", [128, 4], F32)
    assert o[0] <= 49152, o[0]

    R1, R2, R2B, R3 = 49152, 114688, 147456, 180224
    xr1 = at("xr1", [128, NTH * DM], F32, R1)
    xrt = [at(f"xrt{i}", [128, DM], F32, R1 + i * 8192) for i in range(2)]
    wqk1 = [at(f"wqk1_{i}", [128, 16 * 512], BF16, R1 + i * 16384) for i in range(2)]
    yT = at("yT", [128, ND * S], BF16, R1)
    xr2 = at("xr2", [128, NTH * DM], F32, R1)
    lnT = at("lnT", [128, ND * S], BF16, R2)
    lnTh = at("lnTh", [128, ND * HALF], BF16, R2)
    wblk = [at(f"wblk{i}", [128, 16 * 512], BF16, R2B + i * 16384) for i in range(2)]
    h_sg = at("h_sg", [128, 16 * HALF], BF16, R2)
    wob = [at(f"wob{i}", [128, 16 * 512], BF16, R2B + i * 16384) for i in range(2)]
    wst = [at(f"wst{i}", [128, ND * 128], BF16, R3 + i * 4096) for i in range(2)]
    at1 = at("at1", [128, HALF], F32, R3 + 8192)
    at2 = at("at2", [128, HALF], F32, R3 + 12288)
    at3 = at("at3", [128, HALF], F32, R3 + 16384)
    at4 = at("at4", [128, HALF], F32, R3 + 20480)
    rqb = at("rqb", [128, HALF], F32, R3 + 24576)
    silu_t = at("silu_t", [128, HALF], F32, ln_tm_off)
    tmpout = [at(f"tmpout{i}", [128, 512], F32, ln_tm_off + 4096 + i * 2048)
              for i in range(2)]
    qkv_sb = [at(f"qkvsb{i}", [128, DM], F32, R3 + 8192 + i * 8192) for i in range(2)]
    accq = at("accq", [128, DM], F32, R1)
    acck = at("acck", [128, DM], F32, R1 + 8192)
    sq_t = at("sq_t", [128, DM], F32, R1 + 16384)
    ones_col = at("ones_col", [128, 1], F32, R1 + 24576)
    sk_row = at("sk_row", [1, DM], F32, R1 + 24608)
    vstrip = [at(f"vstrip{i}", [128, 512], F32, ln_tm_off + 8192 + i * 2048)
              for i in range(2)]

    ctx = ExitStack()
    pb = [ctx.enter_context(nc.psum_tensor(f"pb{i}", [128, 512], F32)) for i in range(8)]

    def pbf(i):
        return pb[i][:].bitcast(BF16)

    sems = {}

    def S_(name):
        if name not in sems:
            sems[name] = Sem(nc, "s_" + name)
        return sems[name]

    s_x = S_("x")
    s_pe = S_("pe")
    s_ve = S_("ve")
    s_se = S_("se")
    s_ev = S_("ev")
    s_sp = S_("sp")
    s_ad = S_("ad")
    s_od = S_("od")
    s_rr = S_("rr")
    adp_sems = [S_("adp0"), S_("adp1")]
    wsets = {}
    for nm in ("A", "B", "C", "D"):
        wsets[nm] = {"ld": [S_("wl" + nm + "0"), S_("wl" + nm + "1")], "i": 0, "done": []}

    def w_stream(g, setname, buf_list, src_ap):
        ws = wsets[setname]
        i = ws["i"]
        buf = buf_list[i % 2]
        if i >= 2:
            sem, val = ws["done"][i - 2]
            g.wait_ge(sem.h, val)
        nblk = src_ap.shape[1]
        ws["ld"][i % 2].inc(g.dma_start(
            out=buf[:].rearrange("p (b i) -> p b i", b=nblk), in_=src_ap), 16)
        ws["i"] = i + 1
        return buf

    def vedge(v, instr):
        s_ve.inc(instr)
        v.wait_ge(s_ve.h, s_ve.n)

    def sedge(e, instr):
        s_se.inc(instr)
        e.wait_ge(s_se.h, s_se.n)

    blk_ctx = nc.Block()
    blk = blk_ctx.__enter__()

    s_bar = S_("bar")
    bar_n = [0]

    def barrier():
        bar_n[0] += 1
        tgt = 5 * bar_n[0]
        for ename in ("sync", "tensor", "vector", "scalar", "gpsimd"):
            def _f(e, tgt=tgt):
                e.drain()
                e.nop().then_inc(s_bar.h, 1)
                e.wait_ge(s_bar.h, tgt)
            getattr(blk, ename)(_f)

    s_cg = S_("cg")

    @blk.sync
    def _(sy):
        s_x.inc(sy.dma_start(out=ident_bf[:], in_=c_ident_bf[:]), 16)
        s_x.inc(sy.dma_start(out=ceps[:], in_=c_eps[:]), 16)
        s_x.inc(sy.dma_start(out=ident_f[:], in_=c_ident_f[:]), 16)
        sy.wait_ge(s_x.h, s_x.n)

    @blk.gpsimd
    def _(g):
        s_cg.inc(g.dma_start(out=gateT_bf[:].rearrange("p (c e) -> p c e", e=NE),
                             in_=gateT.rearrange("(c p) e -> p c e", p=128)), 16)
        s_cg.inc(g.dma_start(out=expb_bf[:], in_=expb), 16)
        g.wait_ge(s_cg.h, s_cg.n)

    @blk.vector
    def _(v):
        v.memset(ones1_f[:], 1.0)
        s_ve.inc(v.memset(zeros[:], 0.0))

    barrier()

    s_lnt = S_("lnt")
    ln2_tm = [at(f"ln2tm{i}", [128, DM], BF16, R3 + i * 4096) for i in range(2)]
    a1f32 = [at(f"a1f{i}", [128, DM], F32, R3 + 8192 + i * 8192) for i in range(2)]

    def emit_ln_batch(srcs, dst_lnT, col_of, rms_fold, accum_rows=None):
        n = len(srcs)
        L = lnsc
        mean_str = L[:, 0:2 * n:2]
        var_str = L[:, 1:2 * n:2]

        @blk.vector
        def _(v):
            for ci in range(n):
                for i in range(4):
                    ins = v.bn_stats(stats_all[:, ci * 24 + i * 6: ci * 24 + (i + 1) * 6],
                                     srcs[ci][:, i * 512:(i + 1) * 512])
            vedge(v, ins)
            for ci in range(n):
                ins = v.bn_aggr(L[:, 2 * ci:2 * ci + 2], stats_all[:, ci * 24:ci * 24 + 24])
            vedge(v, ins)

        @blk.scalar
        def _(e):
            e.wait_ge(s_ve.h, s_ve.n)
            sedge(e, e.activation(L[:, 16:16 + n], var_str, AF.Ln, bias=ceps[:, 0:1]))
            sedge(e, e.activation(L[:, 16:16 + n], L[:, 16:16 + n], AF.Exp, scale=-0.5))

        if rms_fold:
            @blk.vector
            def _(v):
                v.wait_ge(s_se.h, s_se.n)
                vedge(v, v.tensor_tensor(L[:, 40:40 + n], L[:, 16:16 + n],
                                         L[:, 16:16 + n], op=ALU.mult))
                vedge(v, v.tensor_tensor(L[:, 40:40 + n], var_str, L[:, 40:40 + n],
                                         op=ALU.mult))

            @blk.scalar
            def _(e):
                e.wait_ge(s_ve.h, s_ve.n)
                sedge(e, e.activation(L[:, 40:40 + n], L[:, 40:40 + n], AF.Ln,
                                      bias=ceps[:, 1:2]))
                sedge(e, e.activation(L[:, 40:40 + n], L[:, 40:40 + n], AF.Exp, scale=-0.5))

        @blk.vector
        def _(v):
            v.wait_ge(s_se.h, s_se.n)
            if rms_fold:
                vedge(v, v.tensor_tensor(L[:, 24:24 + n], L[:, 16:16 + n], L[:, 40:40 + n],
                                         op=ALU.mult))
            else:
                vedge(v, v.tensor_copy(L[:, 24:24 + n], L[:, 16:16 + n]))
            ins = v.scalar_tensor_tensor(L[:, 32:32 + n], mean_str, -1.0, L[:, 24:24 + n],
                                         op0=ALU.mult, op1=ALU.mult)
            vedge(v, ins)

        affine_ve = [0] * n
        ad_marks = [adp_sems[0].n, adp_sems[1].n]
        width = dst_lnT.shape[1] // ND

        def emit_affine(ci):
            @blk.vector
            def _(v):
                if ci >= 2:
                    v.wait_ge(s_pe.h, tr_done[ci - 2])
                ins = v.tensor_scalar(ln2_tm[ci % 2][:], srcs[ci][:],
                                      L[:, 24 + ci:25 + ci], L[:, 32 + ci:33 + ci],
                                      op0=ALU.mult, op1=ALU.add)
                if accum_rows is None:
                    vedge(v, ins)
                else:
                    s_ve.inc(ins)
                    v.wait_ge(adp_sems[ci % 2].h, ad_marks[ci % 2])
                    ins = v.tensor_scalar(a1f32[ci % 2][:], srcs[ci][:],
                                          L[:, 24 + ci:25 + ci], L[:, 32 + ci:33 + ci],
                                          op0=ALU.mult, op1=ALU.add)
                    vedge(v, ins)
            affine_ve[ci] = s_ve.n

        tr_done = [0] * n
        emit_affine(0)
        for ci in range(n):
            col = col_of(ci)

            @blk.tensor
            def _(t):
                t.wait_ge(s_ve.h, affine_ve[ci])
                t.wait_ge(s_lnt.h, s_lnt.n)
                for h8 in range(2):
                    bank = 6 + h8
                    for i in range(8):
                        dt = h8 * 8 + i
                        ins = t.transpose(pbf(bank)[:, i * 128:(i + 1) * 128],
                                          ln2_tm[ci % 2][:, dt * 128:(dt + 1) * 128],
                                          ident_bf[:])
                    s_pe.inc(ins)
            tr_done[ci] = s_pe.n

            if accum_rows is not None:
                @blk.gpsimd
                def _(g):
                    g.wait_ge(s_ve.h, affine_ve[ci])
                    rows = accum_rows + ci * 128
                    ins = g.dma_start(out=xspill[rows:rows + 128, :],
                                      in_=a1f32[ci % 2][:], accum_op=ALU.add)
                    adp_sems[ci % 2].inc(ins, 16)
                ad_marks[ci % 2] = adp_sems[ci % 2].n

            if ci + 1 < n:
                emit_affine(ci + 1)

            @blk.vector
            def _(v):
                for h8 in range(2):
                    v.wait_ge(s_pe.h, tr_done[ci] - 1 + h8)
                    bank = 6 + h8
                    src = pbf(bank)[:, 0:1024].rearrange("p (b i) -> p b i", b=8)
                    dsts = dst_lnT[:].rearrange("p (dt w) -> p dt w", dt=ND)[
                        :, h8 * 8:(h8 + 1) * 8, col:col + 128]
                    ins = v.tensor_copy(dsts, src)
                s_lnt.inc(ins)
                v.wait_ge(s_lnt.h, s_lnt.n)

    def emit_gating_batch(gbase, lnT_t):
        G = gsc
        glog = stats_all[:, 256:320]
        mk1 = stats_all[:, 320:384]
        mk2 = stats_all[:, 384:448]
        width = lnT_t.shape[1] // ND

        @blk.tensor
        def _(t):
            t.wait_ge(s_lnt.h, s_lnt.n)
            t.wait_ge(s_ev.h, s_ev.n)
            for ci in range(NTH):
                for dt in range(ND):
                    ins = t.matmul(
                        pb[5][:, ci * 8:(ci + 1) * 8],
                        lnT_t[:, dt * width + ci * 128: dt * width + ci * 128 + 128],
                        gateT_bf[:, dt * 8:(dt + 1) * 8],
                        start=(dt == 0), stop=(dt == ND - 1))
            s_pe.inc(ins)

        @blk.vector
        def _(v):
            v.wait_ge(s_pe.h, s_pe.n)
            ins = v.tensor_copy(glog, pb[5][:, 0:64])
            s_ev.inc(ins)
            v.wait_ge(s_ev.h, s_ev.n)
            for ci in range(NTH):
                gl = glog[:, ci * 8:(ci + 1) * 8]
                vedge(v, v.reduce_max(G[:, ci:ci + 1], gl, axis=AX.X))
                vedge(v, v.tensor_scalar(mk1[:, ci * 8:(ci + 1) * 8], gl,
                                         G[:, ci:ci + 1], None, op0=ALU.is_ge))
                vedge(v, v.scalar_tensor_tensor(mk2[:, ci * 8:(ci + 1) * 8],
                                                mk1[:, ci * 8:(ci + 1) * 8], -1e30, gl,
                                                op0=ALU.mult, op1=ALU.add))
                vedge(v, v.reduce_max(G[:, 8 + ci:9 + ci],
                                      mk2[:, ci * 8:(ci + 1) * 8], axis=AX.X))
                vedge(v, v.tensor_scalar(mk2[:, ci * 8:(ci + 1) * 8],
                                         mk2[:, ci * 8:(ci + 1) * 8],
                                         G[:, 8 + ci:9 + ci], None, op0=ALU.is_ge))
            vedge(v, v.tensor_tensor(G[:, 16:24], G[:, 0:8], G[:, 8:16], op=ALU.subtract))

        @blk.scalar
        def _(e):
            e.wait_ge(s_ve.h, s_ve.n)
            sedge(e, e.activation(G[:, 24:32], G[:, 16:24], AF.Exp, scale=-1.0))
            sedge(e, e.activation(G[:, 24:32], G[:, 24:32], AF.Ln, bias=1.0))
            sedge(e, e.activation(G[:, 24:32], G[:, 24:32], AF.Exp, scale=-1.0))

        @blk.vector
        def _(v):
            v.wait_ge(s_se.h, s_se.n)
            vedge(v, v.tensor_scalar_mul(G[:, 24:32], G[:, 24:32], 0.5))
            vedge(v, v.tensor_scalar(G[:, 8:16], G[:, 24:32], -1.0, 0.5,
                                     op0=ALU.mult, op1=ALU.add))
            for ci in range(NTH):
                vedge(v, v.tensor_scalar_mul(mk1[:, ci * 8:(ci + 1) * 8],
                                             mk1[:, ci * 8:(ci + 1) * 8],
                                             G[:, 24 + ci:25 + ci]))
                vedge(v, v.tensor_scalar_mul(mk2[:, ci * 8:(ci + 1) * 8],
                                             mk2[:, ci * 8:(ci + 1) * 8],
                                             G[:, 8 + ci:9 + ci]))
            vedge(v, v.tensor_tensor(wm[:], mk1[:, 0:64], mk2[:, 0:64], op=ALU.add))

        for rnd in range(2):
            @blk.tensor
            def _(t):
                t.wait_ge(s_ve.h, s_ve.n)
                t.wait_ge(s_ev.h, s_ev.n)
                for j in range(4):
                    ci = rnd * 4 + j
                    ins = t.transpose(pb[5][0:8, j * 128:(j + 1) * 128],
                                      wm[:, ci * 8:(ci + 1) * 8], ident_f[:])
                s_pe.inc(ins)

            @blk.vector
            def _(v):
                v.wait_ge(s_pe.h, s_pe.n)
                gc = (gbase + rnd * 4) * 128
                ins = v.tensor_copy(wmaskT[0:8, gc:gc + 512], pb[5][0:8, 0:512])
                s_ev.inc(ins)
                v.wait_ge(s_ev.h, s_ev.n)

        @blk.vector
        def _(v):
            gc = gbase * 128
            vedge(v, v.tensor_copy(wmaskT_bf[0:8, gc:gc + HALF], wmaskT[0:8, gc:gc + HALF]))

    def emit_ffn(ffn_idx, xr, src_dram, do_output):
        for half in range(2):
            gbase = half * NTH
            rows = half * HALF

            @blk.sync
            def _(sy):
                if ffn_idx == 1:
                    sy.wait_ge(s_ad.h, s_ad.n)
                    for sp_ in adp_sems:
                        sy.wait_ge(sp_.h, sp_.n)
                sy.wait_ge(s_od.h, s_od.n)
                sy.wait_ge(s_sp.h, s_sp.n)
                ins = sy.dma_start(
                    out=xr[:].rearrange("p (c d) -> p c d", c=NTH),
                    in_=src_dram[rows:rows + HALF, :].rearrange("(c p) d -> p c d", p=128))
                s_x.inc(ins, 16)

            xload_t = s_x.n

            @blk.vector
            def _(v):
                v.wait_ge(s_x.h, xload_t)

            emit_ln_batch([xr[:, tc * DM:(tc + 1) * DM] for tc in range(NTH)],
                          lnTh, lambda ci: ci * 128, rms_fold=False)
            emit_gating_batch(gbase, lnTh)

            entry_ve = s_ve.n
            entry_ev = s_ev.n
            bias_marks = {}
            for tc in range(NTH):
                gc = (gbase + tc) * 128
                bb = (0, 1, 2, 3) if tc % 2 == 0 else (4, 5, 6, 7)

                @blk.tensor
                def _(t, tc=tc, bb=bb, gc=gc):
                    if tc < 2:
                        t.wait_ge(s_ve.h, entry_ve)
                        t.wait_ge(s_ev.h, entry_ev)
                    else:
                        t.wait_ge(s_ev.h, bias_marks[tc - 2])
                    for ot in range(4):
                        ins = t.matmul(pb[bb[ot]][:], wmaskT_bf[0:8, gc:gc + 128],
                                       expb_bf[0:8, ot * 512:(ot + 1) * 512],
                                       start=True, stop=True)
                        s_pe.inc(ins)
                pe_mark = s_pe.n

                @blk.vector
                def _(v, tc=tc, bb=bb, pe_mark=pe_mark):
                    v.wait_ge(s_pe.h, pe_mark)
                    for ot in range(4):
                        dst = xr[:, tc * DM + ot * 512: tc * DM + (ot + 1) * 512]
                        ins = v.tensor_tensor(dst, dst, pb[bb[ot]][:], op=ALU.add)
                        s_ev.inc(ins)
                bias_marks[tc] = s_ev.n

            ev_base = s_ev.n
            nb = 0
            for ie in range(NE):
                for ot in range(4):
                    @blk.gpsimd
                    def _(g):
                        w_stream(g, "A", wblk,
                                 expT_blk[ie, ot].rearrange("(dc p) o -> p dc o", p=128))

                    ld_s = wsets["A"]["ld"][(wsets["A"]["i"] - 1) % 2]
                    ld_t = ld_s.n
                    bi = nb

                    @blk.tensor
                    def _(t):
                        t.wait_ge(ld_s.h, ld_t)
                        if bi == 0:
                            t.wait_ge(s_ev.h, ev_base)
                        for tc in range(NTH):
                            if bi > 0:
                                t.wait_ge(s_ev.h, ev_base + (bi - 1) * NTH + tc + 1)
                            for dt in range(ND):
                                ins = t.matmul(
                                    pb[tc][:],
                                    lnTh[:, dt * HALF + tc * 128: dt * HALF + tc * 128 + 128],
                                    wblk[bi % 2][:, dt * 512:(dt + 1) * 512],
                                    start=(dt == 0), stop=(dt == ND - 1))
                            s_pe.inc(ins)
                    wsets["A"]["done"].append((s_pe, s_pe.n))

                    pe_base = s_pe.n - NTH

                    @blk.vector
                    def _(v):
                        for tc in range(NTH):
                            v.wait_ge(s_pe.h, pe_base + tc + 1)
                            dst = xr[:, tc * DM + ot * 512: tc * DM + (ot + 1) * 512]
                            ins = v.scalar_tensor_tensor(
                                dst, pb[tc][:], wm[:, tc * 8 + ie: tc * 8 + ie + 1], dst,
                                op0=ALU.mult, op1=ALU.add)
                            s_ev.inc(ins)
                    nb += 1

            @blk.sync
            def _(sy):
                sy.wait_ge(s_ev.h, s_ev.n)
                tgt = y_out if do_output else xspill
                ins = sy.dma_start(
                    out=tgt[rows:rows + HALF, :].rearrange("(c p) d -> p c d", p=128),
                    in_=xr[:].rearrange("p (c d) -> p c d", c=NTH))
                (s_od if do_output else s_sp).inc(ins, 16)

            barrier()

    a1x = at("a1x", [128, NTH * DM], F32, R1)

    def emit_a1():
        ve_group_done = [0]
        for grp in range(2):
            rows = grp * HALF

            @blk.sync
            def _(sy):
                if grp == 0:
                    sy.wait_ge(s_sp.h, s_sp.n)
                else:
                    sy.wait_ge(s_ve.h, ve_group_done[0])
                ins = sy.dma_start(
                    out=a1x[:].rearrange("p (c d) -> p c d", c=NTH),
                    in_=xspill[rows:rows + HALF, :].rearrange("(c p) d -> p c d", p=128))
                s_x.inc(ins, 16)

            xl = s_x.n

            @blk.vector
            def _(v):
                v.wait_ge(s_x.h, xl)

            emit_ln_batch([a1x[:, tc * DM:(tc + 1) * DM] for tc in range(NTH)],
                          lnT, lambda ci, g=grp: (g * NTH + ci) * 128,
                          rms_fold=True, accum_rows=rows)
            ve_group_done[0] = s_ve.n

        barrier()

    s_qw = [S_("qw0"), S_("qw1")]
    s_qld = S_("qld")
    s_qld2 = [S_("qld_a"), S_("qld_b")]
    chain_ve = [0]
    a3_done_ve = [0, 0]
    yT_ready_ve = [0, 0]
    bank_guard = {0: 0, 1: 0}
    part1_qw = []
    part2_qw = []
    ch_m = {"d6": [0, 0], "d7": [0, 0]}
    ev_at2 = [0, 0]
    carry_kv = stats_all[:, 0:NT]
    carry_w = stats_all[:, NT:2 * NT]

    def a3_chain(half, fc, v_in_at2=False):
        SUBW = 512
        cs = [slice(s * SUBW, (s + 1) * SUBW) for s in range(2)]
        prev_d7 = list(ch_m["d7"])
        prev_d6 = list(ch_m["d6"])
        qld = [0, 0]
        vld = [0, 0]
        ev_v = list(ev_at2)

        @blk.sync
        def _(sy):
            for s_, n_ in (part1_qw if half == 0 else part2_qw):
                sy.wait_ge(s_.h, n_)
            sy.wait_ge(s_rr.h, s_rr.n)
            for s in range(2):
                c0 = half * HALF + s * SUBW
                sq2 = s_qld2[s]
                sy.wait_ge(s_ve.h, prev_d7[s])
                sy.dma_start(out=at3[:, cs[s]],
                             in_=qkv_dram[fc * 128:(fc + 1) * 128, c0:c0 + SUBW])\
                    .then_inc(sq2.h, 16)
                sq2.n += 16
                sy.dma_start(out=at1[:, cs[s]],
                             in_=qkv_dram[DM + fc * 128: DM + (fc + 1) * 128,
                                          c0:c0 + SUBW])\
                    .then_inc(sq2.h, 16)
                sq2.n += 16
                if not v_in_at2:
                    sy.wait_ge(s_ve.h, prev_d6[s])
                    sy.dma_start(out=at2[:, cs[s]],
                                 in_=qkv_dram[2 * DM + fc * 128:
                                              2 * DM + (fc + 1) * 128, c0:c0 + SUBW])\
                        .then_inc(sq2.h, 16)
                    sq2.n += 16
                qld[s] = sq2.n
                vld[s] = sq2.n

        mv = lambda ins: (s_ve.inc(ins), s_ve.n)[1]
        ms = lambda ins: (s_se.inc(ins), s_se.n)[1]
        m_d1 = [0, 0]; m_d3 = [0, 0]; m_s1 = [0, 0]; m_d4 = [0, 0]
        m_c4 = [0, 0]; m_s3 = [0, 0]; m_s4 = [0, 0]; m_d5 = [0, 0]
        m_d6 = [0, 0]; m_c6 = [0, 0]; m_s5 = [0, 0]; m_d7 = [0, 0]

        @blk.vector
        def _(v):
            for s in range(2):
                v.wait_ge(s_qld2[s].h, qld[s])
                m_d1[s] = mv(v.tensor_tensor(at1[:, cs[s]], at1[:, cs[s]],
                                             rkb[:, cs[s]], op=ALU.mult))
                m_d3[s] = mv(v.tensor_tensor(at3[:, cs[s]], at3[:, cs[s]],
                                             rqb[:, cs[s]], op=ALU.mult))

        @blk.scalar
        def _(e):
            for s in range(2):
                e.wait_ge(s_ve.h, m_d1[s])
                m_s1[s] = ms(e.activation(at1[:, cs[s]], at1[:, cs[s]], AF.Exp))
                e.wait_ge(s_ve.h, m_d3[s])
                mtmp = ms(e.activation(at3[:, cs[s]], at3[:, cs[s]], AF.Exp,
                                       scale=-1.0))
                e.wait_ge(s_se.h, mtmp)
                m_s3[s] = ms(e.activation(at3[:, cs[s]], at3[:, cs[s]],
                                          AF.Ln, bias=1.0))

        @blk.vector
        def _(v):
            for s in range(2):
                v.wait_ge(s_se.h, m_s1[s])
                if v_in_at2:
                    v.wait_ge(s_ev.h, ev_v[s])
                else:
                    v.wait_ge(s_qld2[s].h, vld[s])
                v.tensor_tensor(at2[:, cs[s]], at2[:, cs[s]],
                                at1[:, cs[s]], op=ALU.mult)
                if s == 1:
                    v.wait_ge(s_ve.h, m_c4[0])
                m_d4[s] = mv(v.tensor_tensor_scan(at4[:, cs[s]], at1[:, cs[s]],
                                                  zeros[:, 0:SUBW],
                                                  carry_w[:, fc:fc + 1],
                                                  op0=ALU.add, op1=ALU.add))
                v.wait_ge(s_ve.h, m_d4[s])
                m_c4[s] = mv(v.tensor_copy(carry_w[:, fc:fc + 1],
                                           at4[:, (s + 1) * SUBW - 1:(s + 1) * SUBW]))

        @blk.scalar
        def _(e):
            for s in range(2):
                e.wait_ge(s_ve.h, m_c4[s])
                m_s4[s] = ms(e.activation(at4[:, cs[s]], at4[:, cs[s]],
                                          AF.Ln, bias=ceps[:, 2:3]))

        @blk.vector
        def _(v):
            for s in range(2):
                v.wait_ge(s_se.h, m_s3[s])
                v.wait_ge(s_se.h, m_s4[s])
                m_d5[s] = mv(v.tensor_tensor(at3[:, cs[s]], at3[:, cs[s]],
                                             at4[:, cs[s]], op=ALU.add))
                if s == 1:
                    v.wait_ge(s_ve.h, m_c6[0])
                m_d6[s] = mv(v.tensor_tensor_scan(at1[:, cs[s]], at2[:, cs[s]],
                                                  zeros[:, 0:SUBW],
                                                  carry_kv[:, fc:fc + 1],
                                                  op0=ALU.add, op1=ALU.add))
                v.wait_ge(s_ve.h, m_d6[s])
                m_c6[s] = mv(v.tensor_copy(carry_kv[:, fc:fc + 1],
                                           at1[:, (s + 1) * SUBW - 1:(s + 1) * SUBW]))

        @blk.scalar
        def _(e):
            for s in range(2):
                e.wait_ge(s_ve.h, m_d5[s])
                m_s5[s] = ms(e.activation(at3[:, cs[s]], at3[:, cs[s]],
                                          AF.Exp, scale=-1.0))

        @blk.vector
        def _(v):
            for s in range(2):
                v.wait_ge(s_se.h, m_s5[s])
                v.wait_ge(s_ve.h, m_c6[s])
                m_d7[s] = mv(v.tensor_tensor(
                    yT[:, fc * S + half * HALF + s * SUBW:
                       fc * S + half * HALF + (s + 1) * SUBW],
                    at1[:, cs[s]], at3[:, cs[s]], op=ALU.mult))

        ch_m["d6"] = list(m_d6)
        ch_m["d7"] = list(m_d7)
        chain_ve[0] = m_d7[1]

    def emit_bcast(half):
        @blk.sync
        def _(sy):
            sy.wait_ge(s_rr.h, s_rr.n)
            flat0 = scr_r[0].rearrange("a b -> (a b)").unsqueeze(0)
            flat1 = scr_r[1].rearrange("a b -> (a b)").unsqueeze(0)
            sy.dma_start(out=rqh_row[:], in_=flat0[:, half * HALF:(half + 1) * HALF])\
                .then_inc(s_x.h, 16)
            s_x.n += 16
            sy.dma_start(out=rkh_row[:], in_=flat1[:, half * HALF:(half + 1) * HALF])\
                .then_inc(s_x.h, 16)
            s_x.n += 16
        xl = s_x.n

        @blk.tensor
        def _(t):
            t.wait_ge(s_x.h, xl)
            t.wait_ge(s_ve.h, chain_ve[0])
            t.wait_ge(s_ev.h, s_ev.n)
            t.wait_ge(s_se.h, s_se.n)
            for i in range(2):
                t.matmul(pb[i][:], ones1_f[:], rqh_row[:, i * 512:(i + 1) * 512],
                         start=True, stop=True)
            for i in range(2):
                ins = t.matmul(pb[2 + i][:], ones1_f[:], rkh_row[:, i * 512:(i + 1) * 512],
                               start=True, stop=True)
            s_pe.inc(ins)

        @blk.vector
        def _(v):
            v.wait_ge(s_pe.h, s_pe.n)
            v.tensor_copy(rqb[:, 0:512], pb[0][:])
            v.tensor_copy(rqb[:, 512:1024], pb[1][:])
            v.tensor_copy(rkb[:, 0:512], pb[2][:])
            ins = v.tensor_copy(rkb[:, 512:1024], pb[3][:])
            s_ev.inc(ins)
            v.wait_ge(s_ev.h, s_ev.n)
        bank_guard[0] = s_ev.n

    def emit_qkv_attn():
        @blk.vector
        def _(v):
            for sp_ in adp_sems:
                v.wait_ge(sp_.h, sp_.n)
            v.memset(accq[:], 0.0)
            v.memset(acck[:], 0.0)
            v.memset(ones_col[:], 1.0)
            v.memset(carry_kv, 0.0)
            vedge(v, v.memset(carry_w, 0.0))

        def qkv_chunk(i, jc, do_sq):
            @blk.gpsimd
            def _(g):
                w_stream(g, "C", wst, wqkvT_p2[jc].rearrange("(dc p) j -> p dc j", p=128))

            ld_s = wsets["C"]["ld"][(wsets["C"]["i"] - 1) % 2]
            ld_t = ld_s.n
            banks = (0, 1, 2, 3) if i % 2 == 0 else (4, 5, 6, 7)
            evq_guard = bank_guard[i % 2]

            @blk.tensor
            def _(t):
                t.wait_ge(ld_s.h, ld_t)
                t.wait_ge(s_ev.h, evq_guard)
                for ni in range(4):
                    for dt in range(ND):
                        ins = t.matmul(
                            pb[banks[ni]][:],
                            wst[(wsets["C"]["i"] - 1) % 2][:, dt * 128:(dt + 1) * 128],
                            lnT[:, dt * S + ni * 512:(dt) * S + (ni + 1) * 512],
                            start=(dt == 0), stop=(dt == ND - 1))
                    s_pe.inc(ins)
            wsets["C"]["done"].append((s_pe, s_pe.n))
            pe_b = s_pe.n - 4

            if do_sq is not None:
                @blk.vector
                def _(v):
                    v.wait_ge(s_qw[i % 2].h, s_qw[i % 2].n)
                    for ni in range(4):
                        v.wait_ge(s_pe.h, pe_b + ni + 1)
                        ins = v.tensor_copy(qkv_sb[i % 2][:, ni * 512:(ni + 1) * 512],
                                            pb[banks[ni]][:])
                        s_ev.inc(ins)
                    v.wait_ge(s_ev.h, s_ev.n)
                ev_t = s_ev.n

                @blk.sync
                def _(sy):
                    sy.wait_ge(s_ev.h, ev_t)
                    ins = sy.dma_start(out=qkv_dram[jc * 128:(jc + 1) * 128, :],
                                       in_=qkv_sb[i % 2][:])
                    s_qw[i % 2].inc(ins, 16)

                @blk.scalar
                def _(e):
                    e.wait_ge(s_ev.h, ev_t)
                    e.wait_ge(s_ve.h, s_ve.n)
                    sedge(e, e.activation(sq_t[:], qkv_sb[i % 2][:], AF.Square))

                @blk.vector
                def _(v):
                    v.wait_ge(s_se.h, s_se.n)
                    vedge(v, v.tensor_tensor(do_sq, do_sq, sq_t[:], op=ALU.add))
            else:
                d6w = list(ch_m["d6"])

                @blk.vector
                def _(v):
                    for s_, n_ in part1_qw:
                        v.wait_ge(s_.h, n_)
                    for ni in range(2):
                        v.wait_ge(s_ve.h, d6w[ni])
                        v.wait_ge(s_pe.h, pe_b + ni + 1)
                        ins = v.tensor_copy(at2[:, ni * 512:(ni + 1) * 512],
                                            pb[banks[ni]][:])
                        s_ev.inc(ins)
                        ev_at2[ni] = s_ev.n
                    for ni in range(2, 4):
                        v.wait_ge(s_pe.h, pe_b + ni + 1)
                        v.wait_ge(s_qw[ni % 2].h, s_qw[ni % 2].n)
                        ins = v.tensor_copy(vstrip[ni % 2][:], pb[banks[ni]][:])
                        s_ev.inc(ins)
                ev_t = s_ev.n

                @blk.sync
                def _(sy):
                    sy.wait_ge(s_ev.h, ev_t - 1)
                    ins = sy.dma_start(
                        out=qkv_dram[jc * 128:(jc + 1) * 128, HALF:HALF + 512],
                        in_=vstrip[0][:])
                    s_qw[0].inc(ins, 16)
                    sy.wait_ge(s_ev.h, ev_t)
                    ins = sy.dma_start(
                        out=qkv_dram[jc * 128:(jc + 1) * 128, HALF + 512:S],
                        in_=vstrip[1][:])
                    s_qw[1].inc(ins, 16)
            bank_guard[i % 2] = s_ev.n

        idx = 0
        bank_guard[0] = s_ev.n
        bank_guard[1] = s_ev.n
        for jc in range(ND):
            qkv_chunk(idx, jc, accq[:])
            idx += 1
        for jc in range(ND, 2 * ND):
            qkv_chunk(idx, jc, acck[:])
            idx += 1

        @blk.tensor
        def _(t):
            t.wait_ge(s_ve.h, s_ve.n)
            t.wait_ge(s_ev.h, s_ev.n)
            t.wait_ge(s_se.h, s_se.n)
            for w_, acc in ((0, accq), (1, acck)):
                for ni in range(4):
                    ins = t.matmul(pb[w_ * 4 + ni][0:1, :], ones_col[:],
                                   acc[:, ni * 512:(ni + 1) * 512],
                                   start=True, stop=True)
                    s_pe.inc(ins)

        @blk.scalar
        def _(e):
            e.wait_ge(s_pe.h, s_pe.n)
            for w_, row in ((0, sq_t), (1, sk_row)):
                for ni in range(4):
                    ins = e.activation(row[0:1, ni * 512:(ni + 1) * 512],
                                       pb[w_ * 4 + ni][0:1, :],
                                       AF.Ln, bias=ceps[0:1, 1:2], scale=1.0 / DM)
                    if w_ == 1 and ni == 3:
                        s_se.inc(ins)
            e.wait_ge(s_se.h, s_se.n)
            e.activation(sq_t[0:1, 0:DM], sq_t[0:1, 0:DM], AF.Exp, scale=-0.5)
            sedge(e, e.activation(sk_row[0:1, 0:DM], sk_row[0:1, 0:DM], AF.Exp,
                                  scale=-0.5))
        se_t = s_se.n

        @blk.sync
        def _(sy):
            sy.wait_ge(s_se.h, se_t)
            sy.dma_start(out=scr_r[0].rearrange("b c -> (b c)").unsqueeze(0),
                         in_=sq_t[0:1, 0:DM]).then_inc(s_rr.h, 16)
            s_rr.n += 16
            ins = sy.dma_start(out=scr_r[1].rearrange("b c -> (b c)").unsqueeze(0),
                               in_=sk_row[0:1, 0:DM])
            s_rr.inc(ins, 16)

        part1_qw.extend((s_, s_.n) for s_ in s_qw)
        emit_bcast(0)
        for fc in range(ND):
            i = idx
            jc = 2 * ND + fc
            qkv_chunk(i, jc, None)
            a3_chain(0, fc, v_in_at2=True)
            idx += 1
        a3_done_ve[0] = s_ve.n
        yT_ready_ve[0] = s_ve.n
        part2_qw.extend((s_, s_.n) for s_ in s_qw)

        emit_bcast(1)

    def emit_a3h1_hook(mi):
        a3_chain(1, mi)
        if mi == ND - 1:
            a3_done_ve[1] = s_ve.n
            yT_ready_ve[1] = s_ve.n

    def emit_p5(pre_m_hook=None):
        p5_entry_ev = s_ev.n
        s_swh = S_("swh")
        s_opv = S_("opv")
        s_adp = adp_sems
        pending = []
        planned = [s_adp[0].n // 16, s_adp[1].n // 16]
        nD_blocks = [0]

        def flush_one():
            if pending:
                pending.pop(0)()

        for half in range(2):
            for sg in range(4):
                swh_base = s_swh.n
                opv_sg_guard = s_opv.n
                for mi in range(16):
                    if pre_m_hook is not None and half == 0 and sg == 0:
                        pre_m_hook(mi)
                    m = sg * 16 + mi
                    pe_marks = []
                    for gi, src_blk in ((0, wuT_blk), (1, wgT_blk)):
                        @blk.gpsimd
                        def _(g):
                            w_stream(g, "C", wst,
                                     src_blk[m].rearrange("(dc p) j -> p dc j", p=128))
                        if gi == 1:
                            flush_one()

                        ld_s = wsets["C"]["ld"][(wsets["C"]["i"] - 1) % 2]
                        ld_t = ld_s.n
                        banks = (((0, 1), (2, 3)) if mi % 2 == 0 else ((4, 5), (6, 7)))[gi]

                        @blk.tensor
                        def _(t):
                            t.wait_ge(ld_s.h, ld_t)
                            if half == 0 and sg == 0 and mi < 2:
                                t.wait_ge(s_ev.h, p5_entry_ev)
                            if gi == 0:
                                t.wait_ge(s_ve.h, yT_ready_ve[half])
                                if mi >= 2:
                                    t.wait_ge(s_swh.h, swh_base + mi - 1)
                                if mi == 0:
                                    t.wait_ge(s_opv.h, opv_sg_guard)
                            for dt in range(ND):
                                for ni in range(2):
                                    ins = t.matmul(
                                        pb[banks[ni]][:],
                                        wst[(wsets["C"]["i"] - 1) % 2][:, dt * 128:(dt + 1) * 128],
                                        yT[:, dt * S + half * HALF + ni * 512:
                                           dt * S + half * HALF + (ni + 1) * 512],
                                        start=(dt == 0), stop=(dt == ND - 1))
                            s_pe.inc(ins)
                        wsets["C"]["done"].append((s_pe, s_pe.n))
                        pe_marks.append(s_pe.n)

                    pe_u, pe_g = pe_marks
                    bu, bg = (((0, 1), (2, 3)) if mi % 2 == 0 else ((4, 5), (6, 7)))

                    @blk.scalar
                    def _(e):
                        e.wait_ge(s_pe.h, pe_g)
                        e.wait_ge(s_swh.h, swh_base + mi)
                        e.activation(silu_t[:, 0:512], pb[bg[0]][:], AF.Silu)
                        sedge(e, e.activation(silu_t[:, 512:1024], pb[bg[1]][:], AF.Silu))

                    @blk.vector
                    def _(v):
                        v.wait_ge(s_se.h, s_se.n)
                        v.wait_ge(s_pe.h, pe_u)
                        col = mi * HALF
                        v.tensor_tensor(h_sg[:, col:col + 512], pb[bu[0]][:],
                                        silu_t[:, 0:512], op=ALU.mult)
                        ins = v.tensor_tensor(h_sg[:, col + 512:col + 1024], pb[bu[1]][:],
                                              silu_t[:, 512:1024], op=ALU.mult)
                        s_swh.inc(ins)
                        v.wait_ge(s_swh.h, s_swh.n)

                for oq in range(4):
                    @blk.gpsimd
                    def _(g):
                        if nD_blocks[0] < 2:
                            for sp_ in s_qw:
                                g.wait_ge(sp_.h, sp_.n)
                            g.wait_ge(s_ve.h, a3_done_ve[0])
                        nD_blocks[0] += 1
                        w_stream(g, "D", wob,
                                 woutT[sg * DM:(sg + 1) * DM, oq * 512:(oq + 1) * 512]
                                 .rearrange("(dc p) o -> p dc o", p=128))
                    flush_one()

                    ld_s = wsets["D"]["ld"][(wsets["D"]["i"] - 1) % 2]
                    ld_t = ld_s.n
                    opv_base = s_opv.n
                    ad_base = [planned[0], planned[1]]

                    @blk.tensor
                    def _(t):
                        t.wait_ge(ld_s.h, ld_t)
                        t.wait_ge(s_swh.h, swh_base + 16)
                        for tc in range(NTH):
                            if tc >= 2:
                                t.wait_ge(s_opv.h, opv_base + tc - 1)
                            elif oq == 0 and sg == 0 and half == 0:
                                pass
                            else:
                                t.wait_ge(s_opv.h, opv_base - 2 + tc + 1)
                            for dffc in range(16):
                                ins = t.matmul(
                                    pb[tc % 2][:],
                                    h_sg[:, dffc * HALF + tc * 128: dffc * HALF + tc * 128 + 128],
                                    wob[(wsets["D"]["i"] - 1) % 2][:, dffc * 512:(dffc + 1) * 512],
                                    start=(dffc == 0), stop=(dffc == 15))
                            s_pe.inc(ins)
                    wsets["D"]["done"].append((s_pe, s_pe.n))

                    pe_base = s_pe.n - NTH

                    @blk.vector
                    def _(v):
                        for tc in range(NTH):
                            v.wait_ge(s_pe.h, pe_base + tc + 1)
                            v.wait_ge(s_adp[tc % 2].h,
                                      (ad_base[tc % 2] + tc // 2) * 16)
                            ins = v.tensor_copy(tmpout[tc % 2][:], pb[tc % 2][:])
                            s_opv.inc(ins)

                    def mk_accum(opv_base=opv_base, half=half, sg=sg, oq=oq):
                        def emit():
                            @blk.gpsimd
                            def _(g):
                                for tc in range(NTH):
                                    g.wait_ge(s_opv.h, opv_base + tc + 1)
                                    rows = half * HALF + tc * 128
                                    ins = g.dma_start(
                                        out=xspill[rows:rows + 128,
                                                   oq * 512:(oq + 1) * 512],
                                        in_=tmpout[tc % 2][:], accum_op=ALU.add)
                                    s_adp[tc % 2].inc(ins, 16)
                        return emit

                    pending.append(mk_accum())
                    planned[0] += 4
                    planned[1] += 4

            while pending:
                flush_one()
            barrier()

    if "ffn1" in stages:
        emit_ffn(0, xr1, x_in, do_output=(stages == ("ffn1",)))
    if "attn" in stages:
        emit_a1()
        emit_qkv_attn()
    if "proj" in stages:
        emit_p5(pre_m_hook=emit_a3h1_hook)
    if "ffn2" in stages:
        emit_ffn(1, xr2, xspill, do_output=True)
    elif stages != ("ffn1",):
        @blk.sync
        def _(sy):
            for tc in range(NT):
                ins = sy.dma_start(out=xrt[0][:], in_=xspill[tc * 128:(tc + 1) * 128, :])
                s_x.inc(ins, 16)
                sy.wait_ge(s_x.h, s_x.n)
                ins = sy.dma_start(out=y_out[tc * 128:(tc + 1) * 128, :], in_=xrt[0][:])
                s_od.inc(ins, 16)
                sy.wait_ge(s_od.h, s_od.n)

    @blk.sync
    def _(sy):
        sy.wait_ge(s_od.h, s_od.n)

    blk_ctx.__exit__(None, None, None)
    ctx.close()
    return nc


def np_split_inputs(inputs):
    x = np.asarray(inputs["L0_attn_x"], dtype=np.float32)
    w_qkv = np.asarray(inputs["L0_attn_w_qkv"], dtype=np.float32)
    w_sw = np.asarray(inputs["L0_attn_w_swiglu"], dtype=np.float32)
    w_out = np.asarray(inputs["L0_attn_w_out"], dtype=np.float32)
    gate_w = np.asarray(inputs["L0_ffn_gate_w"], dtype=np.float32)
    exp_w = np.asarray(inputs["L0_ffn_expert_w"], dtype=np.float32)
    exp_b = np.asarray(inputs["L0_ffn_expert_b"], dtype=np.float32)

    wqkvT = np.ascontiguousarray(w_qkv.T)
    shared = {
        "wqkvT_p2": np.ascontiguousarray(wqkvT.reshape(DM, 48, 128).transpose(1, 0, 2)),
        "wuT_blk": np.ascontiguousarray(
            w_sw[:DFF].T.reshape(DM, 64, 128).transpose(1, 0, 2)),
        "wgT_blk": np.ascontiguousarray(
            w_sw[DFF:].T.reshape(DM, 64, 128).transpose(1, 0, 2)),
        "woutT": np.ascontiguousarray(w_out.T),
        "gateT": np.ascontiguousarray(gate_w.T),
        "expb": np.ascontiguousarray(exp_b),
        "expT_blk": np.ascontiguousarray(
            exp_w.transpose(0, 2, 1).reshape(NE, DM, 4, 512).transpose(0, 2, 1, 3)),
    }
    return [dict(shared, x=np.ascontiguousarray(x[c])) for c in range(N_CORES)]


def kernel(**inputs):
    in_maps = np_split_inputs(inputs)
    nc = build_program()
    res = run_bass_kernel_spmd(nc, in_maps, core_ids=list(range(N_CORES)))
    return np.stack([np.asarray(r["y"], dtype=np.float32) for r in res.results], axis=0)


# version 06
# speedup vs baseline: 1.1866x; 1.0491x over previous
import numpy as np
import ml_dtypes
import concourse.bass as bass
import concourse.mybir as mybir
from concourse.bass_utils import run_bass_kernel_spmd
from contextlib import ExitStack

F32 = mybir.dt.float32
BF16 = mybir.dt.bfloat16
AF = mybir.ActivationFunctionType
ALU = mybir.AluOpType
AX = mybir.AxisListType

S = 2048
DM = 2048
DFF = 8192
NE = 8
NT = S // 128
ND = DM // 128
HALF = S // 2
NTH = NT // 2
EPS_RMS = 1.1920929e-07
EPS_LN = 1e-5
N_CORES = 8
SB_BASE = 20448
SB_LIMIT = 229344


class Sem:
    __slots__ = ("h", "n")

    def __init__(self, nc, name):
        self.h = nc.alloc_semaphore(name)
        self.n = 0

    def inc(self, instr, k=1):
        instr.then_inc(self.h, k)
        self.n += k
        return instr


def build_program(stages=("ffn1", "attn", "proj", "ffn2")):
    nc = bass.Bass("TRN2", target_bir_lowering=False)
    d = nc.dram_tensor
    x_in = d("x", [S, DM], F32, kind="ExternalInput").ap()
    wqkvT_p2 = d("wqkvT_p2", [48, DM, 128], F32, kind="ExternalInput").ap()
    wuT_blk = d("wuT_blk", [64, DM, 128], F32, kind="ExternalInput").ap()
    wgT_blk = d("wgT_blk", [64, DM, 128], F32, kind="ExternalInput").ap()
    woutT = d("woutT", [DFF, DM], F32, kind="ExternalInput").ap()
    gateT = d("gateT", [DM, NE], F32, kind="ExternalInput").ap()
    expb = d("expb", [NE, DM], F32, kind="ExternalInput").ap()
    expT_blk = d("expT_blk", [NE, 4, DM, 512], F32, kind="ExternalInput").ap()
    y_out = d("y", [S, DM], F32, kind="ExternalOutput").ap()
    xspill = d("xspill", [S, DM], F32, kind="Internal").ap()
    scr_r = d("scr_r", [2, NT, 128], F32, kind="Internal").ap()
    qkv_dram = d("qkvd", [3 * DM, S], F32, kind="Internal").ap()

    c_ident_bf = nc.inline_tensor(np.eye(128, dtype=ml_dtypes.bfloat16), "c_ib").ap()
    c_ident_f = nc.inline_tensor(np.eye(128, dtype=np.float32), "c_if").ap()
    c_eps = nc.inline_tensor(
        np.tile(np.array([[EPS_LN, EPS_RMS, 1e-6, 0.0]], dtype=np.float32), (128, 1)),
        "c_eps").ap()

    def at(name, shape, dtype, offset):
        offset += SB_BASE
        per = int(np.prod(shape[1:])) * mybir.dt.size(dtype)
        assert offset + per <= SB_LIMIT, f"sbuf overflow {name}: {offset + per}"
        return nc.alloc_sbuf_tensor_at(name, list(shape), dtype, offset=offset)

    o = [0]

    def nxt(name, shape, dtype):
        per = int(np.prod(shape[1:])) * mybir.dt.size(dtype)
        off = (o[0] + 31) // 32 * 32
        t = at(name, shape, dtype, off)
        o[0] = off + per
        return t

    ident_bf = nxt("ident_bf", [128, 128], BF16)
    ident_f = nxt("ident_f", [128, 128], F32)
    ones1_f = nxt("ones1_f", [1, 128], F32)
    gateT_bf = nxt("gateT_bf", [128, ND * NE], BF16)
    expb_bf = nxt("expb_bf", [NE, DM], BF16)
    wmaskT = nxt("wmaskT", [NE, S], F32)
    wmaskT_bf = nxt("wmaskT_bf", [NE, S], BF16)
    wm = nxt("wm", [128, NTH * NE], F32)
    gsc = nxt("gsc", [128, 32], F32)
    lnsc = nxt("lnsc", [128, 48], F32)
    rq_all = nxt("rq_all", [128, NT], F32)
    rk_all = nxt("rk_all", [128, NT], F32)
    rqkT = nxt("rqkT", [NT, 128], F32)
    ln_tm_off = (o[0] + 31) // 32 * 32
    ln_tm = nxt("ln_tm", [128, DM], BF16)
    ln_f32 = nxt("ln_f32", [128, DM], F32)
    rqh_row = at("rqh_row", [1, HALF], F32, ln_tm_off)
    rkh_row = at("rkh_row", [1, HALF], F32, ln_tm_off + 4096)
    zeros = nxt("zeros", [128, HALF], F32)
    stats_all = nxt("stats_all", [128, NT * 8 * 6], F32)
    rkb = nxt("rkb", [128, HALF], F32)
    ceps = nxt("ceps", [128, 4], F32)
    assert o[0] <= 49152, o[0]

    R1, R2, R2B, R3 = 49152, 114688, 147456, 180224
    xr1 = at("xr1", [128, NTH * DM], F32, R1)
    xrt = [at(f"xrt{i}", [128, DM], F32, R1 + i * 8192) for i in range(2)]
    wqk1 = [at(f"wqk1_{i}", [128, 16 * 512], BF16, R1 + i * 16384) for i in range(2)]
    yT = at("yT", [128, ND * S], BF16, R1)
    xr2 = at("xr2", [128, NTH * DM], F32, R1)
    lnT = at("lnT", [128, ND * S], BF16, R2)
    lnTh = at("lnTh", [128, ND * HALF], BF16, R2)
    wblk = [at(f"wblk{i}", [128, 16 * 512], BF16, R2B + i * 16384) for i in range(2)]
    h_sg = at("h_sg", [128, 16 * HALF], BF16, R2)
    wob = [at(f"wob{i}", [128, 16 * 512], BF16, R2B + i * 16384) for i in range(2)]
    wst = [at(f"wst{i}", [128, ND * 128], BF16, R3 + i * 4096) for i in range(2)]
    at1 = at("at1", [128, HALF], F32, R3 + 8192)
    at2 = at("at2", [128, HALF], F32, R3 + 12288)
    at3 = at("at3", [128, HALF], F32, R3 + 16384)
    at4 = at("at4", [128, HALF], F32, R3 + 20480)
    rqb = at("rqb", [128, HALF], F32, R3 + 24576)
    silu_t = at("silu_t", [128, HALF], F32, ln_tm_off)
    tmpout = [at(f"tmpout{i}", [128, 512], F32, ln_tm_off + 4096 + i * 2048)
              for i in range(2)]
    qkv_sb = [at(f"qkvsb{i}", [128, DM], F32, R3 + 8192 + i * 8192) for i in range(2)]
    accq = at("accq", [128, DM], F32, R1)
    acck = at("acck", [128, DM], F32, R1 + 8192)
    sq_t = at("sq_t", [128, DM], F32, R1 + 16384)
    ones_col = at("ones_col", [128, 1], F32, R1 + 24576)
    sk_row = at("sk_row", [1, DM], F32, R1 + 24608)
    vstrip = [at(f"vstrip{i}", [128, 512], F32, ln_tm_off + 8192 + i * 2048)
              for i in range(2)]

    ctx = ExitStack()
    pb = [ctx.enter_context(nc.psum_tensor(f"pb{i}", [128, 512], F32)) for i in range(8)]

    def pbf(i):
        return pb[i][:].bitcast(BF16)

    sems = {}

    def S_(name):
        if name not in sems:
            sems[name] = Sem(nc, "s_" + name)
        return sems[name]

    s_x = S_("x")
    s_pe = S_("pe")
    s_ve = S_("ve")
    s_se = S_("se")
    s_ev = S_("ev")
    s_ev2 = S_("ev2")
    s_sp = S_("sp")
    s_ad = S_("ad")
    s_od = S_("od")
    s_slq = [S_("slq0"), S_("slq1")]
    s_xl2 = [S_("xl2a"), S_("xl2b")]
    spill_q = [[], []]
    spill_all = []
    last_tr = [None]
    s_rr = S_("rr")
    adp_sems = [S_("adp0"), S_("adp1")]
    wsets = {}
    for nm in ("A", "B", "C", "D"):
        wsets[nm] = {"ld": [S_("wl" + nm + "0"), S_("wl" + nm + "1")], "i": 0, "done": []}

    def w_stream(g, setname, buf_list, src_ap):
        ws = wsets[setname]
        i = ws["i"]
        buf = buf_list[i % 2]
        if ws.get("pre"):
            sem, val = ws["pre"].pop(0)
            g.wait_ge(sem.h, val)
        if i >= 2:
            sem, val = ws["done"][i - 2]
            g.wait_ge(sem.h, val)
        nblk = src_ap.shape[1]
        ws["ld"][i % 2].inc(g.dma_start(
            out=buf[:].rearrange("p (b i) -> p b i", b=nblk), in_=src_ap), 16)
        ws["i"] = i + 1
        return buf

    def vedge(v, instr):
        s_ve.inc(instr)
        v.wait_ge(s_ve.h, s_ve.n)

    def sedge(e, instr):
        s_se.inc(instr)
        e.wait_ge(s_se.h, s_se.n)

    blk_ctx = nc.Block()
    blk = blk_ctx.__enter__()

    s_bar = S_("bar")
    bar_n = [0]

    def barrier():
        bar_n[0] += 1
        tgt = 5 * bar_n[0]
        for ename in ("sync", "tensor", "vector", "scalar", "gpsimd"):
            def _f(e, tgt=tgt):
                e.drain()
                e.nop().then_inc(s_bar.h, 1)
                e.wait_ge(s_bar.h, tgt)
            getattr(blk, ename)(_f)

    s_cg = S_("cg")

    @blk.sync
    def _(sy):
        s_x.inc(sy.dma_start(out=ident_bf[:], in_=c_ident_bf[:]), 16)
        s_x.inc(sy.dma_start(out=ceps[:], in_=c_eps[:]), 16)
        s_x.inc(sy.dma_start(out=ident_f[:], in_=c_ident_f[:]), 16)
        sy.wait_ge(s_x.h, s_x.n)

    @blk.gpsimd
    def _(g):
        s_cg.inc(g.dma_start(out=gateT_bf[:].rearrange("p (c e) -> p c e", e=NE),
                             in_=gateT.rearrange("(c p) e -> p c e", p=128)), 16)
        s_cg.inc(g.dma_start(out=expb_bf[:], in_=expb), 16)
        g.wait_ge(s_cg.h, s_cg.n)

    @blk.vector
    def _(v):
        v.memset(ones1_f[:], 1.0)
        s_ve.inc(v.memset(zeros[:], 0.0))

    barrier()

    s_lnt = S_("lnt")
    ln2_tm = [at(f"ln2tm{i}", [128, DM], BF16, R3 + i * 4096) for i in range(2)]
    a1f32 = [at(f"a1f{i}", [128, DM], F32, R3 + 8192 + i * 8192) for i in range(2)]

    def emit_ln_batch(srcs, dst_lnT, col_of, rms_fold, accum_rows=None,
                      sb=0, prev_tr=None, pre_wait=None):
        n = len(srcs)
        L = lnsc
        mean_str = L[:, 2 * sb:2 * (sb + n):2]
        var_str = L[:, 2 * sb + 1:2 * (sb + n):2]
        rsd = L[:, 16 + sb:16 + sb + n]
        scl = L[:, 24 + sb:24 + sb + n]
        bia = L[:, 32 + sb:32 + sb + n]
        tmp = L[:, 40 + sb:40 + sb + n]

        @blk.vector
        def _(v):
            if pre_wait is not None:
                v.wait_ge(pre_wait[0].h, pre_wait[1])
            for ci in range(n):
                for i in range(4):
                    ins = v.bn_stats(stats_all[:, (sb + ci) * 24 + i * 6:
                                               (sb + ci) * 24 + (i + 1) * 6],
                                     srcs[ci][:, i * 512:(i + 1) * 512])
            vedge(v, ins)
            for ci in range(n):
                ins = v.bn_aggr(L[:, 2 * (sb + ci):2 * (sb + ci) + 2],
                                stats_all[:, (sb + ci) * 24:(sb + ci) * 24 + 24])
            vedge(v, ins)

        @blk.scalar
        def _(e):
            e.wait_ge(s_ve.h, s_ve.n)
            sedge(e, e.activation(rsd, var_str, AF.Ln, bias=ceps[:, 0:1]))
            sedge(e, e.activation(rsd, rsd, AF.Exp, scale=-0.5))

        if rms_fold:
            @blk.vector
            def _(v):
                v.wait_ge(s_se.h, s_se.n)
                vedge(v, v.tensor_tensor(tmp, rsd, rsd, op=ALU.mult))
                vedge(v, v.tensor_tensor(tmp, var_str, tmp, op=ALU.mult))

            @blk.scalar
            def _(e):
                e.wait_ge(s_ve.h, s_ve.n)
                sedge(e, e.activation(tmp, tmp, AF.Ln, bias=ceps[:, 1:2]))
                sedge(e, e.activation(tmp, tmp, AF.Exp, scale=-0.5))

        @blk.vector
        def _(v):
            v.wait_ge(s_se.h, s_se.n)
            if rms_fold:
                vedge(v, v.tensor_tensor(scl, rsd, tmp, op=ALU.mult))
            else:
                vedge(v, v.tensor_copy(scl, rsd))
            ins = v.scalar_tensor_tensor(bia, mean_str, -1.0, scl,
                                         op0=ALU.mult, op1=ALU.mult)
            vedge(v, ins)

        affine_ve = [0] * n
        ad_marks = [adp_sems[0].n, adp_sems[1].n]
        width = dst_lnT.shape[1] // ND

        def emit_affine(ci):
            @blk.vector
            def _(v):
                if ci >= 2:
                    v.wait_ge(s_pe.h, tr_done[ci - 2])
                elif prev_tr is not None:
                    v.wait_ge(s_pe.h, prev_tr[ci])
                ins = v.tensor_scalar(ln2_tm[ci % 2][:], srcs[ci][:],
                                      L[:, 24 + sb + ci:25 + sb + ci],
                                      L[:, 32 + sb + ci:33 + sb + ci],
                                      op0=ALU.mult, op1=ALU.add)
                if accum_rows is None:
                    vedge(v, ins)
                else:
                    s_ve.inc(ins)
                    v.wait_ge(adp_sems[ci % 2].h, ad_marks[ci % 2])
                    ins = v.tensor_scalar(a1f32[ci % 2][:], srcs[ci][:],
                                          L[:, 24 + sb + ci:25 + sb + ci],
                                          L[:, 32 + sb + ci:33 + sb + ci],
                                          op0=ALU.mult, op1=ALU.add)
                    vedge(v, ins)
            affine_ve[ci] = s_ve.n

        tr_done = [0] * n
        emit_affine(0)
        for ci in range(n):
            col = col_of(ci)

            @blk.tensor
            def _(t):
                t.wait_ge(s_ve.h, affine_ve[ci])
                t.wait_ge(s_lnt.h, s_lnt.n)
                for h8 in range(2):
                    bank = 6 + h8
                    for i in range(8):
                        dt = h8 * 8 + i
                        ins = t.transpose(pbf(bank)[:, i * 128:(i + 1) * 128],
                                          ln2_tm[ci % 2][:, dt * 128:(dt + 1) * 128],
                                          ident_bf[:])
                    s_pe.inc(ins)
            tr_done[ci] = s_pe.n

            if accum_rows is not None:
                @blk.gpsimd
                def _(g):
                    g.wait_ge(s_ve.h, affine_ve[ci])
                    rows = accum_rows + ci * 128
                    ins = g.dma_start(out=xspill[rows:rows + 128, :],
                                      in_=a1f32[ci % 2][:], accum_op=ALU.add)
                    adp_sems[ci % 2].inc(ins, 16)
                ad_marks[ci % 2] = adp_sems[ci % 2].n

            if ci + 1 < n:
                emit_affine(ci + 1)

            @blk.vector
            def _(v):
                for h8 in range(2):
                    v.wait_ge(s_pe.h, tr_done[ci] - 1 + h8)
                    bank = 6 + h8
                    src = pbf(bank)[:, 0:1024].rearrange("p (b i) -> p b i", b=8)
                    dsts = dst_lnT[:].rearrange("p (dt w) -> p dt w", dt=ND)[
                        :, h8 * 8:(h8 + 1) * 8, col:col + 128]
                    ins = v.tensor_copy(dsts, src)
                s_lnt.inc(ins)
                v.wait_ge(s_lnt.h, s_lnt.n)
        return tr_done

    def emit_gating_batch(gbase, lnT_t):
        G = gsc
        glog = stats_all[:, 256:320]
        mk1 = stats_all[:, 320:384]
        mk2 = stats_all[:, 384:448]
        width = lnT_t.shape[1] // ND

        @blk.tensor
        def _(t):
            t.wait_ge(s_lnt.h, s_lnt.n)
            t.wait_ge(s_ev.h, s_ev.n)
            for ci in range(NTH):
                for dt in range(ND):
                    ins = t.matmul(
                        pb[5][:, ci * 8:(ci + 1) * 8],
                        lnT_t[:, dt * width + ci * 128: dt * width + ci * 128 + 128],
                        gateT_bf[:, dt * 8:(dt + 1) * 8],
                        start=(dt == 0), stop=(dt == ND - 1))
            s_pe.inc(ins)

        @blk.vector
        def _(v):
            v.wait_ge(s_pe.h, s_pe.n)
            ins = v.tensor_copy(glog, pb[5][:, 0:64])
            s_ev.inc(ins)
            v.wait_ge(s_ev.h, s_ev.n)
            for ci in range(NTH):
                gl = glog[:, ci * 8:(ci + 1) * 8]
                vedge(v, v.reduce_max(G[:, ci:ci + 1], gl, axis=AX.X))
                vedge(v, v.tensor_scalar(mk1[:, ci * 8:(ci + 1) * 8], gl,
                                         G[:, ci:ci + 1], None, op0=ALU.is_ge))
                vedge(v, v.scalar_tensor_tensor(mk2[:, ci * 8:(ci + 1) * 8],
                                                mk1[:, ci * 8:(ci + 1) * 8], -1e30, gl,
                                                op0=ALU.mult, op1=ALU.add))
                vedge(v, v.reduce_max(G[:, 8 + ci:9 + ci],
                                      mk2[:, ci * 8:(ci + 1) * 8], axis=AX.X))
                vedge(v, v.tensor_scalar(mk2[:, ci * 8:(ci + 1) * 8],
                                         mk2[:, ci * 8:(ci + 1) * 8],
                                         G[:, 8 + ci:9 + ci], None, op0=ALU.is_ge))
            vedge(v, v.tensor_tensor(G[:, 16:24], G[:, 0:8], G[:, 8:16], op=ALU.subtract))

        @blk.scalar
        def _(e):
            e.wait_ge(s_ve.h, s_ve.n)
            sedge(e, e.activation(G[:, 24:32], G[:, 16:24], AF.Exp, scale=-1.0))
            sedge(e, e.activation(G[:, 24:32], G[:, 24:32], AF.Ln, bias=1.0))
            sedge(e, e.activation(G[:, 24:32], G[:, 24:32], AF.Exp, scale=-1.0))

        @blk.vector
        def _(v):
            v.wait_ge(s_se.h, s_se.n)
            vedge(v, v.tensor_scalar_mul(G[:, 24:32], G[:, 24:32], 0.5))
            vedge(v, v.tensor_scalar(G[:, 8:16], G[:, 24:32], -1.0, 0.5,
                                     op0=ALU.mult, op1=ALU.add))
            for ci in range(NTH):
                vedge(v, v.tensor_scalar_mul(mk1[:, ci * 8:(ci + 1) * 8],
                                             mk1[:, ci * 8:(ci + 1) * 8],
                                             G[:, 24 + ci:25 + ci]))
                vedge(v, v.tensor_scalar_mul(mk2[:, ci * 8:(ci + 1) * 8],
                                             mk2[:, ci * 8:(ci + 1) * 8],
                                             G[:, 8 + ci:9 + ci]))
            vedge(v, v.tensor_tensor(wm[:], mk1[:, 0:64], mk2[:, 0:64], op=ALU.add))

        for rnd in range(2):
            @blk.tensor
            def _(t):
                t.wait_ge(s_ve.h, s_ve.n)
                t.wait_ge(s_ev.h, s_ev.n)
                for j in range(4):
                    ci = rnd * 4 + j
                    ins = t.transpose(pb[5][0:8, j * 128:(j + 1) * 128],
                                      wm[:, ci * 8:(ci + 1) * 8], ident_f[:])
                s_pe.inc(ins)

            @blk.vector
            def _(v):
                v.wait_ge(s_pe.h, s_pe.n)
                gc = (gbase + rnd * 4) * 128
                ins = v.tensor_copy(wmaskT[0:8, gc:gc + 512], pb[5][0:8, 0:512])
                s_ev.inc(ins)
                v.wait_ge(s_ev.h, s_ev.n)

        @blk.vector
        def _(v):
            gc = gbase * 128
            vedge(v, v.tensor_copy(wmaskT_bf[0:8, gc:gc + HALF], wmaskT[0:8, gc:gc + HALF]))

    def emit_ffn(ffn_idx, xr, src_dram, do_output):
        for half in range(2):
            gbase = half * NTH
            rows = half * HALF
            lq = [0, 0]

            @blk.scalar
            def _(e):
                if ffn_idx == 1 and half == 0:
                    e.wait_ge(s_pe.h, s_pe.n)
                    e.wait_ge(s_ad.h, s_ad.n)
                    for sp_ in adp_sems:
                        e.wait_ge(sp_.h, sp_.n)
                for q in range(2):
                    for s_, n_ in spill_q[q]:
                        e.wait_ge(s_.h, n_)
                    ins = e.dma_start(
                        out=xr[:, q * 4 * DM:(q + 1) * 4 * DM]
                            .rearrange("p (c d) -> p c d", c=4),
                        in_=src_dram[rows + q * 512: rows + (q + 1) * 512, :]
                            .rearrange("(c p) d -> p c d", p=128))
                    s_xl2[q].inc(ins, 16)
                    lq[q] = s_xl2[q].n

            trA = emit_ln_batch([xr[:, tc * DM:(tc + 1) * DM] for tc in range(4)],
                                lnTh, lambda ci: ci * 128, rms_fold=False,
                                sb=0, prev_tr=last_tr[0],
                                pre_wait=(s_xl2[0], lq[0]))
            trB = emit_ln_batch([xr[:, tc * DM:(tc + 1) * DM] for tc in range(4, NTH)],
                                lnTh, lambda ci: (4 + ci) * 128, rms_fold=False,
                                sb=4, prev_tr=trA[-2:],
                                pre_wait=(s_xl2[1], lq[1]))
            last_tr[0] = trB[-2:]
            emit_gating_batch(gbase, lnTh)

            entry_ve = s_ve.n
            entry_ev = s_ev.n
            bias_marks = {}
            for tc in range(NTH):
                gc = (gbase + tc) * 128
                bb = (0, 1, 2, 3) if tc % 2 == 0 else (4, 5, 6, 7)

                @blk.tensor
                def _(t, tc=tc, bb=bb, gc=gc):
                    if tc < 2:
                        t.wait_ge(s_ve.h, entry_ve)
                        t.wait_ge(s_ev.h, entry_ev)
                    else:
                        t.wait_ge(s_ev.h, bias_marks[tc - 2])
                    for ot in range(4):
                        ins = t.matmul(pb[bb[ot]][:], wmaskT_bf[0:8, gc:gc + 128],
                                       expb_bf[0:8, ot * 512:(ot + 1) * 512],
                                       start=True, stop=True)
                        s_pe.inc(ins)
                pe_mark = s_pe.n

                @blk.vector
                def _(v, tc=tc, bb=bb, pe_mark=pe_mark):
                    v.wait_ge(s_pe.h, pe_mark)
                    for ot in range(4):
                        dst = xr[:, tc * DM + ot * 512: tc * DM + (ot + 1) * 512]
                        ins = v.tensor_tensor(dst, dst, pb[bb[ot]][:], op=ALU.add)
                        s_ev.inc(ins)
                bias_marks[tc] = s_ev.n

            ev_base = s_ev.n
            nb = 0
            for ie in range(NE):
                for ot in range(4):
                    @blk.gpsimd
                    def _(g):
                        w_stream(g, "A", wblk,
                                 expT_blk[ie, ot].rearrange("(dc p) o -> p dc o", p=128))

                    ld_s = wsets["A"]["ld"][(wsets["A"]["i"] - 1) % 2]
                    ld_t = ld_s.n
                    bi = nb

                    @blk.tensor
                    def _(t):
                        t.wait_ge(ld_s.h, ld_t)
                        if bi == 0:
                            t.wait_ge(s_ev.h, ev_base)
                        for tc in range(NTH):
                            if bi > 0:
                                t.wait_ge(s_ev.h, ev_base + (bi - 1) * NTH + tc + 1)
                            for dt in range(ND):
                                ins = t.matmul(
                                    pb[tc][:],
                                    lnTh[:, dt * HALF + tc * 128: dt * HALF + tc * 128 + 128],
                                    wblk[bi % 2][:, dt * 512:(dt + 1) * 512],
                                    start=(dt == 0), stop=(dt == ND - 1))
                            s_pe.inc(ins)
                    wsets["A"]["done"].append((s_pe, s_pe.n))

                    pe_base = s_pe.n - NTH

                    @blk.vector
                    def _(v):
                        for tc in range(NTH):
                            v.wait_ge(s_pe.h, pe_base + tc + 1)
                            dst = xr[:, tc * DM + ot * 512: tc * DM + (ot + 1) * 512]
                            ins = v.scalar_tensor_tensor(
                                dst, pb[tc][:], wm[:, tc * 8 + ie: tc * 8 + ie + 1], dst,
                                op0=ALU.mult, op1=ALU.add)
                            s_ev.inc(ins)
                    nb += 1

            @blk.sync
            def _(sy):
                sy.wait_ge(s_ev.h, s_ev.n)
                tgt = y_out if do_output else xspill
                for q in range(2):
                    ins = sy.dma_start(
                        out=tgt[rows + q * 512: rows + (q + 1) * 512, :]
                            .rearrange("(c p) d -> p c d", p=128),
                        in_=xr[:, q * 4 * DM:(q + 1) * 4 * DM]
                            .rearrange("p (c d) -> p c d", c=4))
                    s_slq[q].inc(ins, 16)
                    spill_q[q] = [(s_slq[q], s_slq[q].n)]
                    spill_all.append((s_slq[q], s_slq[q].n))

    a1x = at("a1x", [128, NTH * DM], F32, R1)

    def emit_a1():
        ve_group_done = [0]
        for grp in range(2):
            rows = grp * HALF

            @blk.sync
            def _(sy):
                if grp == 0:
                    for s_, n_ in spill_all:
                        sy.wait_ge(s_.h, n_)
                else:
                    sy.wait_ge(s_ve.h, ve_group_done[0])
                ins = sy.dma_start(
                    out=a1x[:].rearrange("p (c d) -> p c d", c=NTH),
                    in_=xspill[rows:rows + HALF, :].rearrange("(c p) d -> p c d", p=128))
                s_x.inc(ins, 16)

            xl = s_x.n

            @blk.vector
            def _(v):
                v.wait_ge(s_x.h, xl)

            trg = emit_ln_batch([a1x[:, tc * DM:(tc + 1) * DM] for tc in range(NTH)],
                                lnT, lambda ci, g=grp: (g * NTH + ci) * 128,
                                rms_fold=True, accum_rows=rows,
                                prev_tr=last_tr[0])
            last_tr[0] = trg[-2:]
            ve_group_done[0] = s_ve.n

    s_qw = [S_("qw0"), S_("qw1")]
    s_qld = S_("qld")
    s_qld2 = [S_("qld_a"), S_("qld_b")]
    chain_ve = [0]
    a3_done_ve = [0, 0]
    yT_ready_ve = [0, 0]
    bank_guard = {0: [], 1: []}
    part1_qw = []
    part2_qw = []
    ch_m = {"d6": [0, 0], "d7": [0, 0]}
    ev_at2 = [0, 0]
    carry_kv = stats_all[:, 0:NT]
    carry_w = stats_all[:, NT:2 * NT]

    def a3_chain(half, fc, v_in_at2=False):
        SUBW = 512
        cs = [slice(s * SUBW, (s + 1) * SUBW) for s in range(2)]
        prev_d7 = list(ch_m["d7"])
        prev_d6 = list(ch_m["d6"])
        qld = [0, 0]
        vld = [0, 0]
        ev_v = list(ev_at2)

        @blk.sync
        def _(sy):
            for s_, n_ in (part1_qw if half == 0 else part2_qw):
                sy.wait_ge(s_.h, n_)
            sy.wait_ge(s_rr.h, s_rr.n)
            for s in range(2):
                c0 = half * HALF + s * SUBW
                sq2 = s_qld2[s]
                sy.wait_ge(s_ve.h, prev_d7[s])
                sy.dma_start(out=at3[:, cs[s]],
                             in_=qkv_dram[fc * 128:(fc + 1) * 128, c0:c0 + SUBW])\
                    .then_inc(sq2.h, 16)
                sq2.n += 16
                sy.dma_start(out=at1[:, cs[s]],
                             in_=qkv_dram[DM + fc * 128: DM + (fc + 1) * 128,
                                          c0:c0 + SUBW])\
                    .then_inc(sq2.h, 16)
                sq2.n += 16
                if not v_in_at2:
                    sy.wait_ge(s_ve.h, prev_d6[s])
                    sy.dma_start(out=at2[:, cs[s]],
                                 in_=qkv_dram[2 * DM + fc * 128:
                                              2 * DM + (fc + 1) * 128, c0:c0 + SUBW])\
                        .then_inc(sq2.h, 16)
                    sq2.n += 16
                qld[s] = sq2.n
                vld[s] = sq2.n

        mv = lambda ins: (s_ve.inc(ins), s_ve.n)[1]
        ms = lambda ins: (s_se.inc(ins), s_se.n)[1]
        m_d1 = [0, 0]; m_d3 = [0, 0]; m_s1 = [0, 0]; m_d4 = [0, 0]
        m_c4 = [0, 0]; m_s3 = [0, 0]; m_s4 = [0, 0]; m_d5 = [0, 0]
        m_d6 = [0, 0]; m_c6 = [0, 0]; m_s5 = [0, 0]; m_d7 = [0, 0]

        @blk.vector
        def _(v):
            for s in range(2):
                v.wait_ge(s_qld2[s].h, qld[s])
                m_d1[s] = mv(v.tensor_tensor(at1[:, cs[s]], at1[:, cs[s]],
                                             rkb[:, cs[s]], op=ALU.mult))
                m_d3[s] = mv(v.tensor_tensor(at3[:, cs[s]], at3[:, cs[s]],
                                             rqb[:, cs[s]], op=ALU.mult))

        @blk.scalar
        def _(e):
            for s in range(2):
                e.wait_ge(s_ve.h, m_d1[s])
                m_s1[s] = ms(e.activation(at1[:, cs[s]], at1[:, cs[s]], AF.Exp))
                e.wait_ge(s_ve.h, m_d3[s])
                mtmp = ms(e.activation(at3[:, cs[s]], at3[:, cs[s]], AF.Exp,
                                       scale=-1.0))
                e.wait_ge(s_se.h, mtmp)
                m_s3[s] = ms(e.activation(at3[:, cs[s]], at3[:, cs[s]],
                                          AF.Ln, bias=1.0))

        @blk.vector
        def _(v):
            for s in range(2):
                v.wait_ge(s_se.h, m_s1[s])
                if v_in_at2:
                    v.wait_ge(s_ev2.h, ev_v[s])
                else:
                    v.wait_ge(s_qld2[s].h, vld[s])
                v.tensor_tensor(at2[:, cs[s]], at2[:, cs[s]],
                                at1[:, cs[s]], op=ALU.mult)
                if s == 1:
                    v.wait_ge(s_ve.h, m_c4[0])
                m_d4[s] = mv(v.tensor_tensor_scan(at4[:, cs[s]], at1[:, cs[s]],
                                                  zeros[:, 0:SUBW],
                                                  carry_w[:, fc:fc + 1],
                                                  op0=ALU.add, op1=ALU.add))
                v.wait_ge(s_ve.h, m_d4[s])
                m_c4[s] = mv(v.tensor_copy(carry_w[:, fc:fc + 1],
                                           at4[:, (s + 1) * SUBW - 1:(s + 1) * SUBW]))

        @blk.scalar
        def _(e):
            for s in range(2):
                e.wait_ge(s_ve.h, m_c4[s])
                m_s4[s] = ms(e.activation(at4[:, cs[s]], at4[:, cs[s]],
                                          AF.Ln, bias=ceps[:, 2:3]))

        @blk.vector
        def _(v):
            for s in range(2):
                v.wait_ge(s_se.h, m_s3[s])
                v.wait_ge(s_se.h, m_s4[s])
                m_d5[s] = mv(v.tensor_tensor(at3[:, cs[s]], at3[:, cs[s]],
                                             at4[:, cs[s]], op=ALU.add))
                if s == 1:
                    v.wait_ge(s_ve.h, m_c6[0])
                m_d6[s] = mv(v.tensor_tensor_scan(at1[:, cs[s]], at2[:, cs[s]],
                                                  zeros[:, 0:SUBW],
                                                  carry_kv[:, fc:fc + 1],
                                                  op0=ALU.add, op1=ALU.add))
                v.wait_ge(s_ve.h, m_d6[s])
                m_c6[s] = mv(v.tensor_copy(carry_kv[:, fc:fc + 1],
                                           at1[:, (s + 1) * SUBW - 1:(s + 1) * SUBW]))

        @blk.scalar
        def _(e):
            for s in range(2):
                e.wait_ge(s_ve.h, m_d5[s])
                m_s5[s] = ms(e.activation(at3[:, cs[s]], at3[:, cs[s]],
                                          AF.Exp, scale=-1.0))

        @blk.vector
        def _(v):
            for s in range(2):
                v.wait_ge(s_se.h, m_s5[s])
                v.wait_ge(s_ve.h, m_c6[s])
                m_d7[s] = mv(v.tensor_tensor(
                    yT[:, fc * S + half * HALF + s * SUBW:
                       fc * S + half * HALF + (s + 1) * SUBW],
                    at1[:, cs[s]], at3[:, cs[s]], op=ALU.mult))

        ch_m["d6"] = list(m_d6)
        ch_m["d7"] = list(m_d7)
        chain_ve[0] = m_d7[1]

    def emit_bcast(half):
        @blk.sync
        def _(sy):
            sy.wait_ge(s_rr.h, s_rr.n)
            flat0 = scr_r[0].rearrange("a b -> (a b)").unsqueeze(0)
            flat1 = scr_r[1].rearrange("a b -> (a b)").unsqueeze(0)
            sy.dma_start(out=rqh_row[:], in_=flat0[:, half * HALF:(half + 1) * HALF])\
                .then_inc(s_x.h, 16)
            s_x.n += 16
            sy.dma_start(out=rkh_row[:], in_=flat1[:, half * HALF:(half + 1) * HALF])\
                .then_inc(s_x.h, 16)
            s_x.n += 16
        xl = s_x.n

        @blk.tensor
        def _(t):
            t.wait_ge(s_x.h, xl)
            t.wait_ge(s_ve.h, chain_ve[0])
            t.wait_ge(s_ev.h, s_ev.n)
            t.wait_ge(s_ev2.h, s_ev2.n)
            t.wait_ge(s_se.h, s_se.n)
            for i in range(2):
                t.matmul(pb[i][:], ones1_f[:], rqh_row[:, i * 512:(i + 1) * 512],
                         start=True, stop=True)
            for i in range(2):
                ins = t.matmul(pb[2 + i][:], ones1_f[:], rkh_row[:, i * 512:(i + 1) * 512],
                               start=True, stop=True)
            s_pe.inc(ins)

        @blk.vector
        def _(v):
            v.wait_ge(s_pe.h, s_pe.n)
            v.tensor_copy(rqb[:, 0:512], pb[0][:])
            v.tensor_copy(rqb[:, 512:1024], pb[1][:])
            v.tensor_copy(rkb[:, 0:512], pb[2][:])
            ins = v.tensor_copy(rkb[:, 512:1024], pb[3][:])
            s_ev.inc(ins)
            v.wait_ge(s_ev.h, s_ev.n)
        bank_guard[0] = [(s_ev, s_ev.n)]

    def emit_qkv_attn():
        wsets["C"]["pre"] = [(s_pe, s_pe.n), (s_pe, s_pe.n)]
        qkv_entry_ve = s_ve.n
        qkv_entry_ev = s_ev.n
        qkv_entry_lnt = s_lnt.n

        @blk.vector
        def _(v):
            for sp_ in adp_sems:
                v.wait_ge(sp_.h, sp_.n)
            v.memset(accq[:], 0.0)
            v.memset(acck[:], 0.0)
            v.memset(ones_col[:], 1.0)
            v.memset(carry_kv, 0.0)
            vedge(v, v.memset(carry_w, 0.0))

        def qkv_chunk(i, jc, do_sq):
            @blk.gpsimd
            def _(g):
                w_stream(g, "C", wst, wqkvT_p2[jc].rearrange("(dc p) j -> p dc j", p=128))

            ld_s = wsets["C"]["ld"][(wsets["C"]["i"] - 1) % 2]
            ld_t = ld_s.n
            banks = (0, 1, 2, 3) if i % 2 == 0 else (4, 5, 6, 7)
            evq_guard = list(bank_guard[i % 2])

            @blk.tensor
            def _(t):
                t.wait_ge(ld_s.h, ld_t)
                if i < 2:
                    t.wait_ge(s_ve.h, qkv_entry_ve)
                    t.wait_ge(s_ev.h, qkv_entry_ev)
                    t.wait_ge(s_lnt.h, qkv_entry_lnt)
                for s_, n_ in evq_guard:
                    t.wait_ge(s_.h, n_)
                for ni in range(4):
                    for dt in range(ND):
                        ins = t.matmul(
                            pb[banks[ni]][:],
                            wst[(wsets["C"]["i"] - 1) % 2][:, dt * 128:(dt + 1) * 128],
                            lnT[:, dt * S + ni * 512:(dt) * S + (ni + 1) * 512],
                            start=(dt == 0), stop=(dt == ND - 1))
                    s_pe.inc(ins)
            wsets["C"]["done"].append((s_pe, s_pe.n))
            pe_b = s_pe.n - 4

            if do_sq is not None:
                @blk.vector
                def _(v):
                    v.wait_ge(s_qw[i % 2].h, s_qw[i % 2].n)
                    for ni in range(4):
                        v.wait_ge(s_pe.h, pe_b + ni + 1)
                        ins = v.tensor_copy(qkv_sb[i % 2][:, ni * 512:(ni + 1) * 512],
                                            pb[banks[ni]][:])
                        s_ev.inc(ins)
                    v.wait_ge(s_ev.h, s_ev.n)
                ev_t = s_ev.n

                @blk.sync
                def _(sy):
                    sy.wait_ge(s_ev.h, ev_t)
                    ins = sy.dma_start(out=qkv_dram[jc * 128:(jc + 1) * 128, :],
                                       in_=qkv_sb[i % 2][:])
                    s_qw[i % 2].inc(ins, 16)

                @blk.scalar
                def _(e):
                    e.wait_ge(s_ev.h, ev_t)
                    e.wait_ge(s_ve.h, s_ve.n)
                    sedge(e, e.activation(sq_t[:], qkv_sb[i % 2][:], AF.Square))

                @blk.vector
                def _(v):
                    v.wait_ge(s_se.h, s_se.n)
                    vedge(v, v.tensor_tensor(do_sq, do_sq, sq_t[:], op=ALU.add))
            else:
                d6w = list(ch_m["d6"])
                qw_now = [(s_, s_.n) for s_ in s_qw]

                @blk.scalar
                def _(e):
                    for s_, n_ in part1_qw:
                        e.wait_ge(s_.h, n_)
                    for ni in range(2):
                        e.wait_ge(s_ve.h, d6w[ni])
                        e.wait_ge(s_pe.h, pe_b + ni + 1)
                        ins = e.activation(at2[:, ni * 512:(ni + 1) * 512],
                                           pb[banks[ni]][:], AF.Copy)
                        s_ev2.inc(ins)
                        ev_at2[ni] = s_ev2.n
                    e.wait_ge(s_pe.h, pe_b + 3)
                    e.wait_ge(qw_now[0][0].h, qw_now[0][1])
                    ins = e.activation(vstrip[0][:], pb[banks[2]][:], AF.Copy)
                    s_ev2.inc(ins)
                ev2_t = s_ev2.n

                @blk.vector
                def _(v):
                    v.wait_ge(s_pe.h, pe_b + 4)
                    v.wait_ge(qw_now[1][0].h, qw_now[1][1])
                    ins = v.tensor_copy(vstrip[1][:], pb[banks[3]][:])
                    s_ev.inc(ins)
                ev_t = s_ev.n

                @blk.sync
                def _(sy):
                    sy.wait_ge(s_ev2.h, ev2_t)
                    sy.wait_ge(s_ev.h, ev_t)
                    ins = sy.dma_start(
                        out=qkv_dram[jc * 128:(jc + 1) * 128, HALF:HALF + 512],
                        in_=vstrip[0][:])
                    s_qw[0].inc(ins, 16)
                    ins = sy.dma_start(
                        out=qkv_dram[jc * 128:(jc + 1) * 128, HALF + 512:S],
                        in_=vstrip[1][:])
                    s_qw[1].inc(ins, 16)
            bank_guard[i % 2] = ([(s_ev2, s_ev2.n), (s_ev, s_ev.n)]
                                 if do_sq is None else [(s_ev, s_ev.n)])

        idx = 0
        bank_guard[0] = [(s_ev, s_ev.n)]
        bank_guard[1] = [(s_ev, s_ev.n)]
        for jc in range(ND):
            qkv_chunk(idx, jc, accq[:])
            idx += 1
        for jc in range(ND, 2 * ND):
            qkv_chunk(idx, jc, acck[:])
            idx += 1

        @blk.tensor
        def _(t):
            t.wait_ge(s_ve.h, s_ve.n)
            t.wait_ge(s_ev.h, s_ev.n)
            t.wait_ge(s_ev2.h, s_ev2.n)
            t.wait_ge(s_se.h, s_se.n)
            for w_, acc in ((0, accq), (1, acck)):
                for ni in range(4):
                    ins = t.matmul(pb[w_ * 4 + ni][0:1, :], ones_col[:],
                                   acc[:, ni * 512:(ni + 1) * 512],
                                   start=True, stop=True)
                    s_pe.inc(ins)

        @blk.scalar
        def _(e):
            e.wait_ge(s_pe.h, s_pe.n)
            for w_, row in ((0, sq_t), (1, sk_row)):
                for ni in range(4):
                    ins = e.activation(row[0:1, ni * 512:(ni + 1) * 512],
                                       pb[w_ * 4 + ni][0:1, :],
                                       AF.Ln, bias=ceps[0:1, 1:2], scale=1.0 / DM)
                    if w_ == 1 and ni == 3:
                        s_se.inc(ins)
            e.wait_ge(s_se.h, s_se.n)
            e.activation(sq_t[0:1, 0:DM], sq_t[0:1, 0:DM], AF.Exp, scale=-0.5)
            sedge(e, e.activation(sk_row[0:1, 0:DM], sk_row[0:1, 0:DM], AF.Exp,
                                  scale=-0.5))
        se_t = s_se.n

        @blk.sync
        def _(sy):
            sy.wait_ge(s_se.h, se_t)
            sy.dma_start(out=scr_r[0].rearrange("b c -> (b c)").unsqueeze(0),
                         in_=sq_t[0:1, 0:DM]).then_inc(s_rr.h, 16)
            s_rr.n += 16
            ins = sy.dma_start(out=scr_r[1].rearrange("b c -> (b c)").unsqueeze(0),
                               in_=sk_row[0:1, 0:DM])
            s_rr.inc(ins, 16)

        part1_qw.extend((s_, s_.n) for s_ in s_qw)
        emit_bcast(0)
        for fc in range(ND):
            i = idx
            jc = 2 * ND + fc
            qkv_chunk(i, jc, None)
            a3_chain(0, fc, v_in_at2=True)
            idx += 1
        a3_done_ve[0] = s_ve.n
        yT_ready_ve[0] = s_ve.n
        part2_qw.extend((s_, s_.n) for s_ in s_qw)

        emit_bcast(1)

    def emit_a3h1_hook(mi):
        a3_chain(1, mi)
        if mi == ND - 1:
            a3_done_ve[1] = s_ve.n
            yT_ready_ve[1] = s_ve.n

    def emit_p5(pre_m_hook=None):
        p5_entry_ev = s_ev.n
        p5_entry_ev2 = s_ev2.n
        s_swh = S_("swh")
        s_opv = S_("opv")
        s_adp = adp_sems
        pending = []
        planned = [s_adp[0].n // 16, s_adp[1].n // 16]
        nD_blocks = [0]

        def flush_one():
            if pending:
                pending.pop(0)()

        for half in range(2):
            for sg in range(4):
                swh_base = s_swh.n
                opv_sg_guard = s_opv.n
                for mi in range(16):
                    if pre_m_hook is not None and half == 0 and sg == 0:
                        pre_m_hook(mi)
                    m = sg * 16 + mi
                    pe_marks = []
                    for gi, src_blk in ((0, wuT_blk), (1, wgT_blk)):
                        @blk.gpsimd
                        def _(g):
                            w_stream(g, "C", wst,
                                     src_blk[m].rearrange("(dc p) j -> p dc j", p=128))
                        if gi == 1:
                            flush_one()

                        ld_s = wsets["C"]["ld"][(wsets["C"]["i"] - 1) % 2]
                        ld_t = ld_s.n
                        banks = (((0, 1), (2, 3)) if mi % 2 == 0 else ((4, 5), (6, 7)))[gi]

                        @blk.tensor
                        def _(t):
                            t.wait_ge(ld_s.h, ld_t)
                            if half == 0 and sg == 0 and mi < 2:
                                t.wait_ge(s_ev.h, p5_entry_ev)
                                t.wait_ge(s_ev2.h, p5_entry_ev2)
                            if gi == 0:
                                t.wait_ge(s_ve.h, yT_ready_ve[half])
                                if mi >= 2:
                                    t.wait_ge(s_swh.h, swh_base + mi - 1)
                                if mi == 0:
                                    t.wait_ge(s_opv.h, opv_sg_guard)
                            for dt in range(ND):
                                for ni in range(2):
                                    ins = t.matmul(
                                        pb[banks[ni]][:],
                                        wst[(wsets["C"]["i"] - 1) % 2][:, dt * 128:(dt + 1) * 128],
                                        yT[:, dt * S + half * HALF + ni * 512:
                                           dt * S + half * HALF + (ni + 1) * 512],
                                        start=(dt == 0), stop=(dt == ND - 1))
                            s_pe.inc(ins)
                        wsets["C"]["done"].append((s_pe, s_pe.n))
                        pe_marks.append(s_pe.n)

                    pe_u, pe_g = pe_marks
                    bu, bg = (((0, 1), (2, 3)) if mi % 2 == 0 else ((4, 5), (6, 7)))

                    @blk.scalar
                    def _(e):
                        e.wait_ge(s_pe.h, pe_g)
                        e.wait_ge(s_swh.h, swh_base + mi)
                        e.activation(silu_t[:, 0:512], pb[bg[0]][:], AF.Silu)
                        sedge(e, e.activation(silu_t[:, 512:1024], pb[bg[1]][:], AF.Silu))

                    @blk.vector
                    def _(v):
                        v.wait_ge(s_se.h, s_se.n)
                        v.wait_ge(s_pe.h, pe_u)
                        col = mi * HALF
                        v.tensor_tensor(h_sg[:, col:col + 512], pb[bu[0]][:],
                                        silu_t[:, 0:512], op=ALU.mult)
                        ins = v.tensor_tensor(h_sg[:, col + 512:col + 1024], pb[bu[1]][:],
                                              silu_t[:, 512:1024], op=ALU.mult)
                        s_swh.inc(ins)
                        v.wait_ge(s_swh.h, s_swh.n)

                for oq in range(4):
                    @blk.gpsimd
                    def _(g):
                        if nD_blocks[0] < 2:
                            for sp_ in s_qw:
                                g.wait_ge(sp_.h, sp_.n)
                            g.wait_ge(s_ve.h, a3_done_ve[0])
                        nD_blocks[0] += 1
                        w_stream(g, "D", wob,
                                 woutT[sg * DM:(sg + 1) * DM, oq * 512:(oq + 1) * 512]
                                 .rearrange("(dc p) o -> p dc o", p=128))
                    flush_one()

                    ld_s = wsets["D"]["ld"][(wsets["D"]["i"] - 1) % 2]
                    ld_t = ld_s.n
                    opv_base = s_opv.n
                    ad_base = [planned[0], planned[1]]

                    @blk.tensor
                    def _(t):
                        t.wait_ge(ld_s.h, ld_t)
                        t.wait_ge(s_swh.h, swh_base + 16)
                        for tc in range(NTH):
                            if tc >= 2:
                                t.wait_ge(s_opv.h, opv_base + tc - 1)
                            elif oq == 0 and sg == 0 and half == 0:
                                pass
                            else:
                                t.wait_ge(s_opv.h, opv_base - 2 + tc + 1)
                            for dffc in range(16):
                                ins = t.matmul(
                                    pb[tc % 2][:],
                                    h_sg[:, dffc * HALF + tc * 128: dffc * HALF + tc * 128 + 128],
                                    wob[(wsets["D"]["i"] - 1) % 2][:, dffc * 512:(dffc + 1) * 512],
                                    start=(dffc == 0), stop=(dffc == 15))
                            s_pe.inc(ins)
                    wsets["D"]["done"].append((s_pe, s_pe.n))

                    pe_base = s_pe.n - NTH

                    @blk.vector
                    def _(v):
                        for tc in range(NTH):
                            v.wait_ge(s_pe.h, pe_base + tc + 1)
                            v.wait_ge(s_adp[tc % 2].h,
                                      (ad_base[tc % 2] + tc // 2) * 16)
                            ins = v.tensor_copy(tmpout[tc % 2][:], pb[tc % 2][:])
                            s_opv.inc(ins)

                    def mk_accum(opv_base=opv_base, half=half, sg=sg, oq=oq):
                        def emit():
                            @blk.gpsimd
                            def _(g):
                                for tc in range(NTH):
                                    g.wait_ge(s_opv.h, opv_base + tc + 1)
                                    rows = half * HALF + tc * 128
                                    ins = g.dma_start(
                                        out=xspill[rows:rows + 128,
                                                   oq * 512:(oq + 1) * 512],
                                        in_=tmpout[tc % 2][:], accum_op=ALU.add)
                                    s_adp[tc % 2].inc(ins, 16)
                        return emit

                    pending.append(mk_accum())
                    planned[0] += 4
                    planned[1] += 4

            while pending:
                flush_one()

    if "ffn1" in stages:
        emit_ffn(0, xr1, x_in, do_output=(stages == ("ffn1",)))
    if "attn" in stages:
        emit_a1()
        emit_qkv_attn()
    if "proj" in stages:
        emit_p5(pre_m_hook=emit_a3h1_hook)
    if "ffn2" in stages:
        wsets["A"]["pre"] = [(s_pe, s_pe.n), (s_pe, s_pe.n)]
        last_tr[0] = [s_pe.n, s_pe.n]
        emit_ffn(1, xr2, xspill, do_output=True)
    elif stages != ("ffn1",):
        @blk.sync
        def _(sy):
            for tc in range(NT):
                ins = sy.dma_start(out=xrt[0][:], in_=xspill[tc * 128:(tc + 1) * 128, :])
                s_x.inc(ins, 16)
                sy.wait_ge(s_x.h, s_x.n)
                ins = sy.dma_start(out=y_out[tc * 128:(tc + 1) * 128, :], in_=xrt[0][:])
                s_od.inc(ins, 16)
                sy.wait_ge(s_od.h, s_od.n)

    @blk.sync
    def _(sy):
        sy.wait_ge(s_od.h, s_od.n)
        for s_, n_ in spill_all:
            sy.wait_ge(s_.h, n_)

    blk_ctx.__exit__(None, None, None)
    ctx.close()
    return nc


def np_split_inputs(inputs):
    x = np.asarray(inputs["L0_attn_x"], dtype=np.float32)
    w_qkv = np.asarray(inputs["L0_attn_w_qkv"], dtype=np.float32)
    w_sw = np.asarray(inputs["L0_attn_w_swiglu"], dtype=np.float32)
    w_out = np.asarray(inputs["L0_attn_w_out"], dtype=np.float32)
    gate_w = np.asarray(inputs["L0_ffn_gate_w"], dtype=np.float32)
    exp_w = np.asarray(inputs["L0_ffn_expert_w"], dtype=np.float32)
    exp_b = np.asarray(inputs["L0_ffn_expert_b"], dtype=np.float32)

    wqkvT = np.ascontiguousarray(w_qkv.T)
    shared = {
        "wqkvT_p2": np.ascontiguousarray(wqkvT.reshape(DM, 48, 128).transpose(1, 0, 2)),
        "wuT_blk": np.ascontiguousarray(
            w_sw[:DFF].T.reshape(DM, 64, 128).transpose(1, 0, 2)),
        "wgT_blk": np.ascontiguousarray(
            w_sw[DFF:].T.reshape(DM, 64, 128).transpose(1, 0, 2)),
        "woutT": np.ascontiguousarray(w_out.T),
        "gateT": np.ascontiguousarray(gate_w.T),
        "expb": np.ascontiguousarray(exp_b),
        "expT_blk": np.ascontiguousarray(
            exp_w.transpose(0, 2, 1).reshape(NE, DM, 4, 512).transpose(0, 2, 1, 3)),
    }
    return [dict(shared, x=np.ascontiguousarray(x[c])) for c in range(N_CORES)]


def kernel(**inputs):
    in_maps = np_split_inputs(inputs)
    nc = build_program()
    res = run_bass_kernel_spmd(nc, in_maps, core_ids=list(range(N_CORES)))
    return np.stack([np.asarray(r["y"], dtype=np.float32) for r in res.results], axis=0)


# version 08
# speedup vs baseline: 1.1969x; 1.0087x over previous
import numpy as np
import ml_dtypes
import concourse.bass as bass
import concourse.mybir as mybir
from concourse.bass_utils import run_bass_kernel_spmd
from contextlib import ExitStack

F32 = mybir.dt.float32
BF16 = mybir.dt.bfloat16
AF = mybir.ActivationFunctionType
ALU = mybir.AluOpType
AX = mybir.AxisListType

S = 2048
DM = 2048
DFF = 8192
NE = 8
NT = S // 128
ND = DM // 128
HALF = S // 2
NTH = NT // 2
EPS_RMS = 1.1920929e-07
EPS_LN = 1e-5
N_CORES = 8
SB_BASE = 20448
SB_LIMIT = 229344


class Sem:
    __slots__ = ("h", "n")

    def __init__(self, nc, name):
        self.h = nc.alloc_semaphore(name)
        self.n = 0

    def inc(self, instr, k=1):
        instr.then_inc(self.h, k)
        self.n += k
        return instr


def build_program(stages=("ffn1", "attn", "proj", "ffn2")):
    nc = bass.Bass("TRN2", target_bir_lowering=False)
    d = nc.dram_tensor
    x_in = d("x", [S, DM], F32, kind="ExternalInput").ap()
    wqkvT_p2 = d("wqkvT_p2", [48, DM, 128], F32, kind="ExternalInput").ap()
    wuT_blk = d("wuT_blk", [64, DM, 128], F32, kind="ExternalInput").ap()
    wgT_blk = d("wgT_blk", [64, DM, 128], F32, kind="ExternalInput").ap()
    woutT = d("woutT", [DFF, DM], F32, kind="ExternalInput").ap()
    gateT = d("gateT", [DM, NE], F32, kind="ExternalInput").ap()
    expb = d("expb", [NE, DM], F32, kind="ExternalInput").ap()
    expT_blk = d("expT_blk", [NE, 4, DM, 512], F32, kind="ExternalInput").ap()
    y_out = d("y", [S, DM], F32, kind="ExternalOutput").ap()
    xspill = d("xspill", [S, DM], F32, kind="Internal").ap()
    scr_r = d("scr_r", [2, NT, 128], F32, kind="Internal").ap()
    qkv_dram = d("qkvd", [3 * DM, S], F32, kind="Internal").ap()

    c_ident_bf = nc.inline_tensor(np.eye(128, dtype=ml_dtypes.bfloat16), "c_ib").ap()
    c_ident_f = nc.inline_tensor(np.eye(128, dtype=np.float32), "c_if").ap()
    c_eps = nc.inline_tensor(
        np.tile(np.array([[EPS_LN, EPS_RMS, 1e-6, 0.0]], dtype=np.float32), (128, 1)),
        "c_eps").ap()

    def at(name, shape, dtype, offset):
        offset += SB_BASE
        per = int(np.prod(shape[1:])) * mybir.dt.size(dtype)
        assert offset + per <= SB_LIMIT, f"sbuf overflow {name}: {offset + per}"
        return nc.alloc_sbuf_tensor_at(name, list(shape), dtype, offset=offset)

    o = [0]

    def nxt(name, shape, dtype):
        per = int(np.prod(shape[1:])) * mybir.dt.size(dtype)
        off = (o[0] + 31) // 32 * 32
        t = at(name, shape, dtype, off)
        o[0] = off + per
        return t

    ident_bf = nxt("ident_bf", [128, 128], BF16)
    ident_f = nxt("ident_f", [128, 128], F32)
    ones1_f = nxt("ones1_f", [1, 128], F32)
    gateT_bf = nxt("gateT_bf", [128, ND * NE], BF16)
    expb_bf = nxt("expb_bf", [NE, DM], BF16)
    wmaskT = nxt("wmaskT", [NE, S], F32)
    wmaskT_bf = nxt("wmaskT_bf", [NE, S], BF16)
    wm = nxt("wm", [128, NTH * NE], F32)
    gsc = nxt("gsc", [128, 32], F32)
    lnsc = nxt("lnsc", [128, 48], F32)
    rq_all = nxt("rq_all", [128, NT], F32)
    rk_all = nxt("rk_all", [128, NT], F32)
    rqkT = nxt("rqkT", [NT, 128], F32)
    ln_tm_off = (o[0] + 31) // 32 * 32
    ln_tm = nxt("ln_tm", [128, DM], BF16)
    ln_f32 = nxt("ln_f32", [128, DM], F32)
    rqh_row = at("rqh_row", [1, HALF], F32, ln_tm_off)
    rkh_row = at("rkh_row", [1, HALF], F32, ln_tm_off + 4096)
    zeros = nxt("zeros", [128, HALF], F32)
    stats_all = nxt("stats_all", [128, NT * 8 * 6], F32)
    rkb = nxt("rkb", [128, HALF], F32)
    ceps = nxt("ceps", [128, 4], F32)
    assert o[0] <= 49152, o[0]

    R1, R2, R2B, R3 = 49152, 114688, 147456, 180224
    xr1 = at("xr1", [128, NTH * DM], F32, R1)
    xrt = [at(f"xrt{i}", [128, DM], F32, R1 + i * 8192) for i in range(2)]
    wqk1 = [at(f"wqk1_{i}", [128, 16 * 512], BF16, R1 + i * 16384) for i in range(2)]
    yT = at("yT", [128, ND * S], BF16, R1)
    xr2 = at("xr2", [128, NTH * DM], F32, R1)
    lnT = at("lnT", [128, ND * S], BF16, R2)
    lnTh = at("lnTh", [128, ND * HALF], BF16, R2)
    wblk = [at(f"wblk{i}", [128, 16 * 512], BF16, R2B + i * 16384) for i in range(2)]
    h_sg = at("h_sg", [128, 16 * HALF], BF16, R2)
    wob = [at(f"wob{i}", [128, 16 * 512], BF16, R2B + i * 16384) for i in range(2)]
    wst = [at(f"wst{i}", [128, ND * 128], BF16, R3 + i * 4096) for i in range(2)]
    at1 = at("at1", [128, HALF], F32, R3 + 8192)
    at2 = at("at2", [128, HALF], F32, R3 + 12288)
    at3 = at("at3", [128, HALF], F32, R3 + 16384)
    at4 = at("at4", [128, HALF], F32, R3 + 20480)
    rqb = at("rqb", [128, HALF], F32, R3 + 24576)
    silu_t = at("silu_t", [128, HALF], F32, ln_tm_off)
    tmpout = [at(f"tmpout{i}", [128, 512], F32, ln_tm_off + 4096 + i * 2048)
              for i in range(4)]
    qkv_sb = [at(f"qkvsb{i}", [128, DM], F32, R3 + 8192 + i * 8192) for i in range(2)]
    accq = at("accq", [128, DM], F32, R1)
    acck = at("acck", [128, DM], F32, R1 + 8192)
    sq_t = at("sq_t", [128, DM], F32, R1 + 16384)
    ones_col = at("ones_col", [128, 1], F32, R1 + 24576)
    sk_row = at("sk_row", [1, DM], F32, R1 + 24608)
    vstrip = [at(f"vstrip{i}", [128, 512], F32, ln_tm_off + 8192 + i * 2048)
              for i in range(2)]

    ctx = ExitStack()
    pb = [ctx.enter_context(nc.psum_tensor(f"pb{i}", [128, 512], F32)) for i in range(8)]

    def pbf(i):
        return pb[i][:].bitcast(BF16)

    sems = {}

    def S_(name):
        if name not in sems:
            sems[name] = Sem(nc, "s_" + name)
        return sems[name]

    s_x = S_("x")
    s_pe = S_("pe")
    s_ve = S_("ve")
    s_se = S_("se")
    s_ev = S_("ev")
    s_ev2 = S_("ev2")
    s_sp = S_("sp")
    s_ad = S_("ad")
    s_od = S_("od")
    s_slq = [S_("slq0"), S_("slq1")]
    s_xl2 = [S_("xl2a"), S_("xl2b")]
    spill_q = [[], []]
    spill_all = []
    last_tr = [None]
    s_rr = S_("rr")
    adp_sems = [S_("adp0"), S_("adp1"), S_("adp2"), S_("adp3")]
    wsets = {}
    for nm in ("A", "B", "C", "D"):
        wsets[nm] = {"ld": [S_("wl" + nm + "0"), S_("wl" + nm + "1")], "i": 0, "done": []}

    def w_stream(g, setname, buf_list, src_ap):
        ws = wsets[setname]
        i = ws["i"]
        buf = buf_list[i % 2]
        if ws.get("pre"):
            sem, val = ws["pre"].pop(0)
            g.wait_ge(sem.h, val)
        if i >= 2:
            sem, val = ws["done"][i - 2]
            g.wait_ge(sem.h, val)
        nblk = src_ap.shape[1]
        ws["ld"][i % 2].inc(g.dma_start(
            out=buf[:].rearrange("p (b i) -> p b i", b=nblk), in_=src_ap), 16)
        ws["i"] = i + 1
        return buf

    def vedge(v, instr):
        s_ve.inc(instr)
        v.wait_ge(s_ve.h, s_ve.n)

    def sedge(e, instr):
        s_se.inc(instr)
        e.wait_ge(s_se.h, s_se.n)

    blk_ctx = nc.Block()
    blk = blk_ctx.__enter__()

    s_bar = S_("bar")
    bar_n = [0]

    def barrier():
        bar_n[0] += 1
        tgt = 5 * bar_n[0]
        for ename in ("sync", "tensor", "vector", "scalar", "gpsimd"):
            def _f(e, tgt=tgt):
                e.drain()
                e.nop().then_inc(s_bar.h, 1)
                e.wait_ge(s_bar.h, tgt)
            getattr(blk, ename)(_f)

    s_cg = S_("cg")

    @blk.sync
    def _(sy):
        s_x.inc(sy.dma_start(out=ident_bf[:], in_=c_ident_bf[:]), 16)
        s_x.inc(sy.dma_start(out=ceps[:], in_=c_eps[:]), 16)
        s_x.inc(sy.dma_start(out=ident_f[:], in_=c_ident_f[:]), 16)
        sy.wait_ge(s_x.h, s_x.n)

    @blk.gpsimd
    def _(g):
        s_cg.inc(g.dma_start(out=gateT_bf[:].rearrange("p (c e) -> p c e", e=NE),
                             in_=gateT.rearrange("(c p) e -> p c e", p=128)), 16)
        s_cg.inc(g.dma_start(out=expb_bf[:], in_=expb), 16)
        g.wait_ge(s_cg.h, s_cg.n)

    @blk.vector
    def _(v):
        v.memset(ones1_f[:], 1.0)
        s_ve.inc(v.memset(zeros[:], 0.0))

    barrier()

    s_lnt = S_("lnt")
    ln2_tm = [at(f"ln2tm{i}", [128, DM], BF16, R3 + i * 4096) for i in range(2)]
    a1f32 = [at(f"a1f{i}", [128, DM], F32, R3 + 8192 + i * 8192) for i in range(2)]

    xstg = [at("xstg0", [128, DM], F32, R3), at("xstg1", [128, DM], F32, R3 + 8192)]
    s_st2 = [S_("st0"), S_("st1")]

    def mk_early_stats(src_dram, src_rows, guard_pe, extra_waits=()):
        marks = {}
        steps = []
        for tc in range(8):
            def step(tc=tc):
                @blk.sync
                def _(sy):
                    sy.wait_ge(s_pe.h, guard_pe)
                    for s_, n_ in extra_waits:
                        sy.wait_ge(s_.h, n_)
                    if tc >= 2:
                        sy.wait_ge(s_ve.h, marks[tc - 2])
                    ins = sy.dma_start(
                        out=xstg[tc % 2][:],
                        in_=src_dram[src_rows + tc * 128: src_rows + (tc + 1) * 128, :])
                    s_st2[tc % 2].inc(ins, 16)
                ld = s_st2[tc % 2].n

                @blk.vector
                def _(v):
                    v.wait_ge(s_st2[tc % 2].h, ld)
                    for i in range(4):
                        ins = v.bn_stats(stats_all[:, tc * 24 + i * 6:
                                                   tc * 24 + (i + 1) * 6],
                                         xstg[tc % 2][:, i * 512:(i + 1) * 512])
                    vedge(v, ins)
                    ins = v.bn_aggr(lnsc[:, 2 * tc:2 * tc + 2],
                                    stats_all[:, tc * 24:tc * 24 + 24])
                    vedge(v, ins)
                    marks[tc] = s_ve.n
            steps.append(step)
        return steps

    def emit_ln_stats(srcs, sb, pre_wait=None):
        n = len(srcs)
        L = lnsc

        @blk.vector
        def _(v):
            if pre_wait is not None:
                v.wait_ge(pre_wait[0].h, pre_wait[1])
            for ci in range(n):
                for i in range(4):
                    ins = v.bn_stats(stats_all[:, (sb + ci) * 24 + i * 6:
                                               (sb + ci) * 24 + (i + 1) * 6],
                                     srcs[ci][:, i * 512:(i + 1) * 512])
            vedge(v, ins)
            for ci in range(n):
                ins = v.bn_aggr(L[:, 2 * (sb + ci):2 * (sb + ci) + 2],
                                stats_all[:, (sb + ci) * 24:(sb + ci) * 24 + 24])
            vedge(v, ins)

    def emit_ln_batch(srcs, dst_lnT, col_of, rms_fold, accum_rows=None,
                      sb=0, prev_tr=None, pre_wait=None, skip_stats=False):
        n = len(srcs)
        L = lnsc
        mean_str = L[:, 2 * sb:2 * (sb + n):2]
        var_str = L[:, 2 * sb + 1:2 * (sb + n):2]
        rsd = L[:, 16 + sb:16 + sb + n]
        scl = L[:, 24 + sb:24 + sb + n]
        bia = L[:, 32 + sb:32 + sb + n]
        tmp = L[:, 40 + sb:40 + sb + n]

        if not skip_stats:
            emit_ln_stats(srcs, sb, pre_wait=pre_wait)

        @blk.scalar
        def _(e):
            e.wait_ge(s_ve.h, s_ve.n)
            sedge(e, e.activation(rsd, var_str, AF.Ln, bias=ceps[:, 0:1]))
            sedge(e, e.activation(rsd, rsd, AF.Exp, scale=-0.5))

        if rms_fold:
            @blk.vector
            def _(v):
                v.wait_ge(s_se.h, s_se.n)
                vedge(v, v.tensor_tensor(tmp, rsd, rsd, op=ALU.mult))
                vedge(v, v.tensor_tensor(tmp, var_str, tmp, op=ALU.mult))

            @blk.scalar
            def _(e):
                e.wait_ge(s_ve.h, s_ve.n)
                sedge(e, e.activation(tmp, tmp, AF.Ln, bias=ceps[:, 1:2]))
                sedge(e, e.activation(tmp, tmp, AF.Exp, scale=-0.5))

        @blk.vector
        def _(v):
            v.wait_ge(s_se.h, s_se.n)
            if rms_fold:
                vedge(v, v.tensor_tensor(scl, rsd, tmp, op=ALU.mult))
            else:
                vedge(v, v.tensor_copy(scl, rsd))
            ins = v.scalar_tensor_tensor(bia, mean_str, -1.0, scl,
                                         op0=ALU.mult, op1=ALU.mult)
            vedge(v, ins)

        affine_ve = [0] * n
        ad_marks = [adp_sems[0].n, adp_sems[1].n]
        width = dst_lnT.shape[1] // ND

        def emit_affine(ci):
            @blk.vector
            def _(v):
                if ci >= 2:
                    v.wait_ge(s_pe.h, tr_done[ci - 2])
                elif prev_tr is not None:
                    v.wait_ge(s_pe.h, prev_tr[ci])
                ins = v.tensor_scalar(ln2_tm[ci % 2][:], srcs[ci][:],
                                      L[:, 24 + sb + ci:25 + sb + ci],
                                      L[:, 32 + sb + ci:33 + sb + ci],
                                      op0=ALU.mult, op1=ALU.add)
                if accum_rows is None:
                    vedge(v, ins)
                else:
                    s_ve.inc(ins)
                    v.wait_ge(adp_sems[ci % 2].h, ad_marks[ci % 2])
                    ins = v.tensor_scalar(a1f32[ci % 2][:], srcs[ci][:],
                                          L[:, 24 + sb + ci:25 + sb + ci],
                                          L[:, 32 + sb + ci:33 + sb + ci],
                                          op0=ALU.mult, op1=ALU.add)
                    vedge(v, ins)
            affine_ve[ci] = s_ve.n

        tr_done = [0] * n
        emit_affine(0)
        for ci in range(n):
            col = col_of(ci)

            @blk.tensor
            def _(t):
                t.wait_ge(s_ve.h, affine_ve[ci])
                t.wait_ge(s_lnt.h, s_lnt.n)
                for h8 in range(2):
                    bank = 6 + h8
                    for i in range(8):
                        dt = h8 * 8 + i
                        ins = t.transpose(pbf(bank)[:, i * 128:(i + 1) * 128],
                                          ln2_tm[ci % 2][:, dt * 128:(dt + 1) * 128],
                                          ident_bf[:])
                    s_pe.inc(ins)
            tr_done[ci] = s_pe.n

            if accum_rows is not None:
                @blk.gpsimd
                def _(g):
                    g.wait_ge(s_ve.h, affine_ve[ci])
                    rows = accum_rows + ci * 128
                    ins = g.dma_start(out=xspill[rows:rows + 128, :],
                                      in_=a1f32[ci % 2][:], accum_op=ALU.add)
                    adp_sems[ci % 2].inc(ins, 16)
                ad_marks[ci % 2] = adp_sems[ci % 2].n

            if ci + 1 < n:
                emit_affine(ci + 1)

            @blk.vector
            def _(v):
                for h8 in range(2):
                    v.wait_ge(s_pe.h, tr_done[ci] - 1 + h8)
                    bank = 6 + h8
                    src = pbf(bank)[:, 0:1024].rearrange("p (b i) -> p b i", b=8)
                    dsts = dst_lnT[:].rearrange("p (dt w) -> p dt w", dt=ND)[
                        :, h8 * 8:(h8 + 1) * 8, col:col + 128]
                    ins = v.tensor_copy(dsts, src)
                s_lnt.inc(ins)
                v.wait_ge(s_lnt.h, s_lnt.n)
        return tr_done

    def emit_gating_batch(gbase, lnT_t):
        G = gsc
        glog = stats_all[:, 256:320]
        mk1 = stats_all[:, 320:384]
        mk2 = stats_all[:, 384:448]
        width = lnT_t.shape[1] // ND

        @blk.tensor
        def _(t):
            t.wait_ge(s_lnt.h, s_lnt.n)
            t.wait_ge(s_ev.h, s_ev.n)
            for ci in range(NTH):
                for dt in range(ND):
                    ins = t.matmul(
                        pb[5][:, ci * 8:(ci + 1) * 8],
                        lnT_t[:, dt * width + ci * 128: dt * width + ci * 128 + 128],
                        gateT_bf[:, dt * 8:(dt + 1) * 8],
                        start=(dt == 0), stop=(dt == ND - 1))
            s_pe.inc(ins)

        @blk.vector
        def _(v):
            v.wait_ge(s_pe.h, s_pe.n)
            ins = v.tensor_copy(glog, pb[5][:, 0:64])
            s_ev.inc(ins)
            v.wait_ge(s_ev.h, s_ev.n)
            for ci in range(NTH):
                ins = v.reduce_max(G[:, ci:ci + 1], glog[:, ci * 8:(ci + 1) * 8],
                                   axis=AX.X)
            vedge(v, ins)
            for ci in range(NTH):
                ins = v.tensor_scalar(mk1[:, ci * 8:(ci + 1) * 8],
                                      glog[:, ci * 8:(ci + 1) * 8],
                                      G[:, ci:ci + 1], None, op0=ALU.is_ge)
            vedge(v, ins)
            for ci in range(NTH):
                ins = v.scalar_tensor_tensor(mk2[:, ci * 8:(ci + 1) * 8],
                                             mk1[:, ci * 8:(ci + 1) * 8], -1e30,
                                             glog[:, ci * 8:(ci + 1) * 8],
                                             op0=ALU.mult, op1=ALU.add)
            vedge(v, ins)
            for ci in range(NTH):
                ins = v.reduce_max(G[:, 8 + ci:9 + ci],
                                   mk2[:, ci * 8:(ci + 1) * 8], axis=AX.X)
            vedge(v, ins)
            for ci in range(NTH):
                v.tensor_scalar(mk2[:, ci * 8:(ci + 1) * 8],
                                mk2[:, ci * 8:(ci + 1) * 8],
                                G[:, 8 + ci:9 + ci], None, op0=ALU.is_ge)
            vedge(v, v.tensor_tensor(G[:, 16:24], G[:, 0:8], G[:, 8:16], op=ALU.subtract))

        @blk.scalar
        def _(e):
            e.wait_ge(s_ve.h, s_ve.n)
            sedge(e, e.activation(G[:, 24:32], G[:, 16:24], AF.Exp, scale=-1.0))
            sedge(e, e.activation(G[:, 24:32], G[:, 24:32], AF.Ln, bias=1.0))
            sedge(e, e.activation(G[:, 24:32], G[:, 24:32], AF.Exp, scale=-1.0))

        @blk.vector
        def _(v):
            v.wait_ge(s_se.h, s_se.n)
            vedge(v, v.tensor_scalar_mul(G[:, 24:32], G[:, 24:32], 0.5))
            ins = v.tensor_scalar(G[:, 8:16], G[:, 24:32], -1.0, 0.5,
                                  op0=ALU.mult, op1=ALU.add)
            vedge(v, ins)
            for ci in range(NTH):
                v.tensor_scalar_mul(mk1[:, ci * 8:(ci + 1) * 8],
                                    mk1[:, ci * 8:(ci + 1) * 8],
                                    G[:, 24 + ci:25 + ci])
                ins = v.tensor_scalar_mul(mk2[:, ci * 8:(ci + 1) * 8],
                                          mk2[:, ci * 8:(ci + 1) * 8],
                                          G[:, 8 + ci:9 + ci])
            vedge(v, ins)
            vedge(v, v.tensor_tensor(wm[:], mk1[:, 0:64], mk2[:, 0:64], op=ALU.add))

        for rnd in range(2):
            @blk.tensor
            def _(t):
                t.wait_ge(s_ve.h, s_ve.n)
                t.wait_ge(s_ev.h, s_ev.n)
                for j in range(4):
                    ci = rnd * 4 + j
                    ins = t.transpose(pb[5][0:8, j * 128:(j + 1) * 128],
                                      wm[:, ci * 8:(ci + 1) * 8], ident_f[:])
                s_pe.inc(ins)

            @blk.vector
            def _(v):
                v.wait_ge(s_pe.h, s_pe.n)
                gc = (gbase + rnd * 4) * 128
                ins = v.tensor_copy(wmaskT[0:8, gc:gc + 512], pb[5][0:8, 0:512])
                s_ev.inc(ins)
                v.wait_ge(s_ev.h, s_ev.n)

        @blk.vector
        def _(v):
            gc = gbase * 128
            vedge(v, v.tensor_copy(wmaskT_bf[0:8, gc:gc + HALF], wmaskT[0:8, gc:gc + HALF]))

    def emit_ffn(ffn_idx, xr, src_dram, do_output, post_hook_factory=None):
        early_done = [False]
        for half in range(2):
            gbase = half * NTH
            rows = half * HALF
            lq = [0, 0]

            @blk.scalar
            def _(e):
                if ffn_idx == 1 and half == 0:
                    e.wait_ge(s_pe.h, s_pe.n)
                    e.wait_ge(s_ad.h, s_ad.n)
                    for sp_ in adp_sems:
                        e.wait_ge(sp_.h, sp_.n)
                for q in range(2):
                    for s_, n_ in spill_q[q]:
                        e.wait_ge(s_.h, n_)
                    ins = e.dma_start(
                        out=xr[:, q * 4 * DM:(q + 1) * 4 * DM]
                            .rearrange("p (c d) -> p c d", c=4),
                        in_=src_dram[rows + q * 512: rows + (q + 1) * 512, :]
                            .rearrange("(c p) d -> p c d", p=128))
                    s_xl2[q].inc(ins, 16)
                    lq[q] = s_xl2[q].n

            early = early_done[0] if half == 1 else False
            if not early:
                emit_ln_stats([xr[:, tc * DM:(tc + 1) * DM] for tc in range(4)],
                              sb=0, pre_wait=(s_xl2[0], lq[0]))
                emit_ln_stats([xr[:, tc * DM:(tc + 1) * DM] for tc in range(4, NTH)],
                              sb=4, pre_wait=(s_xl2[1], lq[1]))
            else:
                @blk.vector
                def _(v):
                    for q in range(2):
                        v.wait_ge(s_xl2[q].h, lq[q])
            trA = emit_ln_batch([xr[:, tc * DM:(tc + 1) * DM] for tc in range(4)],
                                lnTh, lambda ci: ci * 128, rms_fold=False,
                                sb=0, prev_tr=last_tr[0], skip_stats=True)
            trB = emit_ln_batch([xr[:, tc * DM:(tc + 1) * DM] for tc in range(4, NTH)],
                                lnTh, lambda ci: (4 + ci) * 128, rms_fold=False,
                                sb=4, prev_tr=trA[-2:], skip_stats=True)
            last_tr[0] = trB[-2:]
            if half == 0:
                hook_steps = mk_early_stats(src_dram, HALF, trB[-1])
            elif post_hook_factory is not None:
                hook_steps = post_hook_factory(trB[-1])
            else:
                hook_steps = []
            emit_gating_batch(gbase, lnTh)

            entry_ve = s_ve.n
            entry_ev = s_ev.n
            bias_marks = {}
            for tc in range(NTH):
                gc = (gbase + tc) * 128
                bb = (0, 1, 2, 3) if tc % 2 == 0 else (4, 5, 6, 7)

                @blk.tensor
                def _(t, tc=tc, bb=bb, gc=gc):
                    if tc < 2:
                        t.wait_ge(s_ve.h, entry_ve)
                        t.wait_ge(s_ev.h, entry_ev)
                    else:
                        t.wait_ge(s_ev.h, bias_marks[tc - 2])
                    for ot in range(4):
                        ins = t.matmul(pb[bb[ot]][:], wmaskT_bf[0:8, gc:gc + 128],
                                       expb_bf[0:8, ot * 512:(ot + 1) * 512],
                                       start=True, stop=True)
                        s_pe.inc(ins)
                pe_mark = s_pe.n

                @blk.vector
                def _(v, tc=tc, bb=bb, pe_mark=pe_mark):
                    v.wait_ge(s_pe.h, pe_mark)
                    for ot in range(4):
                        dst = xr[:, tc * DM + ot * 512: tc * DM + (ot + 1) * 512]
                        ins = v.tensor_tensor(dst, dst, pb[bb[ot]][:], op=ALU.add)
                        s_ev.inc(ins)
                bias_marks[tc] = s_ev.n

            ev_base = s_ev.n
            nb = 0
            for ie in range(NE):
                for ot in range(4):
                    @blk.gpsimd
                    def _(g):
                        w_stream(g, "A", wblk,
                                 expT_blk[ie, ot].rearrange("(dc p) o -> p dc o", p=128))

                    ld_s = wsets["A"]["ld"][(wsets["A"]["i"] - 1) % 2]
                    ld_t = ld_s.n
                    bi = nb

                    @blk.tensor
                    def _(t):
                        t.wait_ge(ld_s.h, ld_t)
                        if bi == 0:
                            t.wait_ge(s_ev.h, ev_base)
                        for tc in range(NTH):
                            if bi > 0:
                                t.wait_ge(s_ev.h, ev_base + (bi - 1) * NTH + tc + 1)
                            for dt in range(ND):
                                ins = t.matmul(
                                    pb[tc][:],
                                    lnTh[:, dt * HALF + tc * 128: dt * HALF + tc * 128 + 128],
                                    wblk[bi % 2][:, dt * 512:(dt + 1) * 512],
                                    start=(dt == 0), stop=(dt == ND - 1))
                            s_pe.inc(ins)
                    wsets["A"]["done"].append((s_pe, s_pe.n))

                    pe_base = s_pe.n - NTH

                    @blk.vector
                    def _(v):
                        for tc in range(NTH):
                            v.wait_ge(s_pe.h, pe_base + tc + 1)
                            dst = xr[:, tc * DM + ot * 512: tc * DM + (ot + 1) * 512]
                            ins = v.scalar_tensor_tensor(
                                dst, pb[tc][:], wm[:, tc * 8 + ie: tc * 8 + ie + 1], dst,
                                op0=ALU.mult, op1=ALU.add)
                            s_ev.inc(ins)
                    if nb >= 2 and nb % 4 == 2 and hook_steps:
                        hook_steps.pop(0)()
                    nb += 1

            while hook_steps:
                hook_steps.pop(0)()
            early_done[0] = True

            @blk.sync
            def _(sy):
                sy.wait_ge(s_ev.h, s_ev.n)
                tgt = y_out if do_output else xspill
                for q in range(2):
                    ins = sy.dma_start(
                        out=tgt[rows + q * 512: rows + (q + 1) * 512, :]
                            .rearrange("(c p) d -> p c d", p=128),
                        in_=xr[:, q * 4 * DM:(q + 1) * 4 * DM]
                            .rearrange("p (c d) -> p c d", c=4))
                    s_slq[q].inc(ins, 16)
                    spill_q[q] = [(s_slq[q], s_slq[q].n)]
                    spill_all.append((s_slq[q], s_slq[q].n))

    a1x = at("a1x", [128, NTH * DM], F32, R1)

    a1_early = [False]

    def emit_a1():
        ve_group_done = [0]
        for grp in range(2):
            rows = grp * HALF

            @blk.sync
            def _(sy):
                if grp == 0:
                    for s_, n_ in spill_all:
                        sy.wait_ge(s_.h, n_)
                else:
                    sy.wait_ge(s_ve.h, ve_group_done[0])
                ins = sy.dma_start(
                    out=a1x[:].rearrange("p (c d) -> p c d", c=NTH),
                    in_=xspill[rows:rows + HALF, :].rearrange("(c p) d -> p c d", p=128))
                s_x.inc(ins, 16)

            xl = s_x.n

            @blk.vector
            def _(v):
                v.wait_ge(s_x.h, xl)

            trg = emit_ln_batch([a1x[:, tc * DM:(tc + 1) * DM] for tc in range(NTH)],
                                lnT, lambda ci, g=grp: (g * NTH + ci) * 128,
                                rms_fold=True, accum_rows=rows,
                                prev_tr=last_tr[0],
                                skip_stats=(grp == 0 and a1_early[0]))
            last_tr[0] = trg[-2:]
            ve_group_done[0] = s_ve.n

    s_qw = [S_("qw0"), S_("qw1")]
    s_qld = S_("qld")
    s_qld2 = [S_("qld_a"), S_("qld_b")]
    s_gp = S_("gp")
    chain_ve = [0]
    a3_done_ve = [0, 0]
    yT_ready_ve = [0, 0]
    bank_guard = {0: [], 1: []}
    part1_qw = []
    part2_qw = []
    ch_m = {"d6": [0, 0], "d7": [0, 0]}
    ev_at2 = [0, 0]
    rqb_ready = [0]
    carry_kv = stats_all[:, 0:NT]
    carry_w = stats_all[:, NT:2 * NT]

    def a3_chain(half, fc, v_in_at2=False):
        SUBW = 512
        cs = [slice(s * SUBW, (s + 1) * SUBW) for s in range(2)]
        prev_d7 = list(ch_m["d7"])
        prev_d6 = list(ch_m["d6"])
        qld = [0, 0]
        vld = [0, 0]
        ev_v = list(ev_at2)

        @blk.sync
        def _(sy):
            for s_, n_ in (part1_qw if half == 0 else part2_qw):
                sy.wait_ge(s_.h, n_)
            sy.wait_ge(s_rr.h, s_rr.n)
            for s in range(2):
                c0 = half * HALF + s * SUBW
                sq2 = s_qld2[s]
                sy.wait_ge(s_ve.h, prev_d7[s])
                sy.dma_start(out=at3[:, cs[s]],
                             in_=qkv_dram[fc * 128:(fc + 1) * 128, c0:c0 + SUBW])\
                    .then_inc(sq2.h, 16)
                sq2.n += 16
                sy.dma_start(out=at1[:, cs[s]],
                             in_=qkv_dram[DM + fc * 128: DM + (fc + 1) * 128,
                                          c0:c0 + SUBW])\
                    .then_inc(sq2.h, 16)
                sq2.n += 16
                if not v_in_at2:
                    sy.wait_ge(s_ve.h, prev_d6[s])
                    sy.dma_start(out=at2[:, cs[s]],
                                 in_=qkv_dram[2 * DM + fc * 128:
                                              2 * DM + (fc + 1) * 128, c0:c0 + SUBW])\
                        .then_inc(sq2.h, 16)
                    sq2.n += 16
                qld[s] = sq2.n
                vld[s] = sq2.n

        mv = lambda ins: (s_ve.inc(ins), s_ve.n)[1]
        ms = lambda ins: (s_se.inc(ins), s_se.n)[1]
        mg = lambda ins: (s_gp.inc(ins), s_gp.n)[1]
        m_d1 = [0, 0]; m_d3 = [0, 0]; m_s1 = [0, 0]; m_d4 = [0, 0]
        m_c4 = [0, 0]; m_s3 = [0, 0]; m_s4 = [0, 0]; m_d5 = [0, 0]
        m_d6 = [0, 0]; m_c6 = [0, 0]; m_s5 = [0, 0]; m_d7 = [0, 0]

        @blk.gpsimd
        def _(g):
            g.wait_ge(s_ev.h, rqb_ready[0])
            for s in range(2):
                g.wait_ge(s_qld2[s].h, qld[s])
                m_d1[s] = mg(g.tensor_tensor(at1[:, cs[s]], at1[:, cs[s]],
                                             rkb[:, cs[s]], op=ALU.mult))
                m_d3[s] = mg(g.tensor_tensor(at3[:, cs[s]], at3[:, cs[s]],
                                             rqb[:, cs[s]], op=ALU.mult))

        @blk.scalar
        def _(e):
            for s in range(2):
                e.wait_ge(s_gp.h, m_d1[s])
                m_s1[s] = ms(e.activation(at1[:, cs[s]], at1[:, cs[s]], AF.Exp))
                e.wait_ge(s_gp.h, m_d3[s])
                mtmp = ms(e.activation(at3[:, cs[s]], at3[:, cs[s]], AF.Exp,
                                       scale=-1.0))
                e.wait_ge(s_se.h, mtmp)
                m_s3[s] = ms(e.activation(at3[:, cs[s]], at3[:, cs[s]],
                                          AF.Ln, bias=1.0))

        @blk.vector
        def _(v):
            for s in range(2):
                v.wait_ge(s_se.h, m_s1[s])
                if v_in_at2:
                    v.wait_ge(s_ev2.h, ev_v[s])
                else:
                    v.wait_ge(s_qld2[s].h, vld[s])
                v.tensor_tensor(at2[:, cs[s]], at2[:, cs[s]],
                                at1[:, cs[s]], op=ALU.mult)
                if s == 1:
                    v.wait_ge(s_ve.h, m_c4[0])
                m_d4[s] = mv(v.tensor_tensor_scan(at4[:, cs[s]], at1[:, cs[s]],
                                                  zeros[:, 0:SUBW],
                                                  carry_w[:, fc:fc + 1],
                                                  op0=ALU.add, op1=ALU.add))
                v.wait_ge(s_ve.h, m_d4[s])
                m_c4[s] = mv(v.tensor_copy(carry_w[:, fc:fc + 1],
                                           at4[:, (s + 1) * SUBW - 1:(s + 1) * SUBW]))

        @blk.scalar
        def _(e):
            for s in range(2):
                e.wait_ge(s_ve.h, m_c4[s])
                m_s4[s] = ms(e.activation(at4[:, cs[s]], at4[:, cs[s]],
                                          AF.Ln, bias=ceps[:, 2:3]))

        @blk.vector
        def _(v):
            for s in range(2):
                v.wait_ge(s_se.h, m_s3[s])
                v.wait_ge(s_se.h, m_s4[s])
                m_d5[s] = mv(v.tensor_tensor(at3[:, cs[s]], at3[:, cs[s]],
                                             at4[:, cs[s]], op=ALU.add))
                if s == 1:
                    v.wait_ge(s_ve.h, m_c6[0])
                m_d6[s] = mv(v.tensor_tensor_scan(at1[:, cs[s]], at2[:, cs[s]],
                                                  zeros[:, 0:SUBW],
                                                  carry_kv[:, fc:fc + 1],
                                                  op0=ALU.add, op1=ALU.add))
                v.wait_ge(s_ve.h, m_d6[s])
                m_c6[s] = mv(v.tensor_copy(carry_kv[:, fc:fc + 1],
                                           at1[:, (s + 1) * SUBW - 1:(s + 1) * SUBW]))

        @blk.scalar
        def _(e):
            for s in range(2):
                e.wait_ge(s_ve.h, m_d5[s])
                m_s5[s] = ms(e.activation(at3[:, cs[s]], at3[:, cs[s]],
                                          AF.Exp, scale=-1.0))

        @blk.vector
        def _(v):
            for s in range(2):
                v.wait_ge(s_se.h, m_s5[s])
                v.wait_ge(s_ve.h, m_c6[s])
                m_d7[s] = mv(v.tensor_tensor(
                    yT[:, fc * S + half * HALF + s * SUBW:
                       fc * S + half * HALF + (s + 1) * SUBW],
                    at1[:, cs[s]], at3[:, cs[s]], op=ALU.mult))

        ch_m["d6"] = list(m_d6)
        ch_m["d7"] = list(m_d7)
        chain_ve[0] = m_d7[1]

    def emit_bcast(half):
        @blk.sync
        def _(sy):
            sy.wait_ge(s_rr.h, s_rr.n)
            flat0 = scr_r[0].rearrange("a b -> (a b)").unsqueeze(0)
            flat1 = scr_r[1].rearrange("a b -> (a b)").unsqueeze(0)
            sy.dma_start(out=rqh_row[:], in_=flat0[:, half * HALF:(half + 1) * HALF])\
                .then_inc(s_x.h, 16)
            s_x.n += 16
            sy.dma_start(out=rkh_row[:], in_=flat1[:, half * HALF:(half + 1) * HALF])\
                .then_inc(s_x.h, 16)
            s_x.n += 16
        xl = s_x.n

        @blk.tensor
        def _(t):
            t.wait_ge(s_x.h, xl)
            t.wait_ge(s_ve.h, chain_ve[0])
            t.wait_ge(s_ev.h, s_ev.n)
            t.wait_ge(s_ev2.h, s_ev2.n)
            t.wait_ge(s_se.h, s_se.n)
            for i in range(2):
                t.matmul(pb[i][:], ones1_f[:], rqh_row[:, i * 512:(i + 1) * 512],
                         start=True, stop=True)
            for i in range(2):
                ins = t.matmul(pb[2 + i][:], ones1_f[:], rkh_row[:, i * 512:(i + 1) * 512],
                               start=True, stop=True)
            s_pe.inc(ins)

        @blk.vector
        def _(v):
            v.wait_ge(s_pe.h, s_pe.n)
            v.tensor_copy(rqb[:, 0:512], pb[0][:])
            v.tensor_copy(rqb[:, 512:1024], pb[1][:])
            v.tensor_copy(rkb[:, 0:512], pb[2][:])
            ins = v.tensor_copy(rkb[:, 512:1024], pb[3][:])
            s_ev.inc(ins)
            v.wait_ge(s_ev.h, s_ev.n)
        bank_guard[0] = [(s_ev, s_ev.n)]
        rqb_ready[0] = s_ev.n

    def emit_qkv_attn():
        wsets["C"]["pre"] = [(s_pe, s_pe.n), (s_pe, s_pe.n)]
        qkv_entry_ve = s_ve.n
        qkv_entry_ev = s_ev.n
        qkv_entry_lnt = s_lnt.n

        @blk.vector
        def _(v):
            for sp_ in adp_sems:
                v.wait_ge(sp_.h, sp_.n)
            v.memset(accq[:], 0.0)
            v.memset(acck[:], 0.0)
            v.memset(ones_col[:], 1.0)
            v.memset(carry_kv, 0.0)
            vedge(v, v.memset(carry_w, 0.0))

        def qkv_chunk(i, jc, do_sq):
            @blk.gpsimd
            def _(g):
                w_stream(g, "C", wst, wqkvT_p2[jc].rearrange("(dc p) j -> p dc j", p=128))

            ld_s = wsets["C"]["ld"][(wsets["C"]["i"] - 1) % 2]
            ld_t = ld_s.n
            banks = (0, 1, 2, 3) if i % 2 == 0 else (4, 5, 6, 7)
            evq_guard = list(bank_guard[i % 2])

            @blk.tensor
            def _(t):
                t.wait_ge(ld_s.h, ld_t)
                if i < 2:
                    t.wait_ge(s_ve.h, qkv_entry_ve)
                    t.wait_ge(s_ev.h, qkv_entry_ev)
                    t.wait_ge(s_lnt.h, qkv_entry_lnt)
                for s_, n_ in evq_guard:
                    t.wait_ge(s_.h, n_)
                for ni in range(4):
                    for dt in range(ND):
                        ins = t.matmul(
                            pb[banks[ni]][:],
                            wst[(wsets["C"]["i"] - 1) % 2][:, dt * 128:(dt + 1) * 128],
                            lnT[:, dt * S + ni * 512:(dt) * S + (ni + 1) * 512],
                            start=(dt == 0), stop=(dt == ND - 1))
                    s_pe.inc(ins)
            wsets["C"]["done"].append((s_pe, s_pe.n))
            pe_b = s_pe.n - 4

            if do_sq is not None:
                @blk.vector
                def _(v):
                    v.wait_ge(s_qw[i % 2].h, s_qw[i % 2].n)
                    for ni in range(4):
                        v.wait_ge(s_pe.h, pe_b + ni + 1)
                        ins = v.tensor_copy(qkv_sb[i % 2][:, ni * 512:(ni + 1) * 512],
                                            pb[banks[ni]][:])
                        s_ev.inc(ins)
                    v.wait_ge(s_ev.h, s_ev.n)
                ev_t = s_ev.n

                @blk.sync
                def _(sy):
                    sy.wait_ge(s_ev.h, ev_t)
                    ins = sy.dma_start(out=qkv_dram[jc * 128:(jc + 1) * 128, :],
                                       in_=qkv_sb[i % 2][:])
                    s_qw[i % 2].inc(ins, 16)

                @blk.scalar
                def _(e):
                    e.wait_ge(s_ev.h, ev_t)
                    e.wait_ge(s_ve.h, s_ve.n)
                    sedge(e, e.activation(sq_t[:], qkv_sb[i % 2][:], AF.Square))

                @blk.vector
                def _(v):
                    v.wait_ge(s_se.h, s_se.n)
                    vedge(v, v.tensor_tensor(do_sq, do_sq, sq_t[:], op=ALU.add))
            else:
                d6w = list(ch_m["d6"])
                qw_now = [(s_, s_.n) for s_ in s_qw]

                @blk.scalar
                def _(e):
                    for s_, n_ in part1_qw:
                        e.wait_ge(s_.h, n_)
                    for ni in range(2):
                        e.wait_ge(s_ve.h, d6w[ni])
                        e.wait_ge(s_pe.h, pe_b + ni + 1)
                        ins = e.activation(at2[:, ni * 512:(ni + 1) * 512],
                                           pb[banks[ni]][:], AF.Copy)
                        s_ev2.inc(ins)
                        ev_at2[ni] = s_ev2.n
                ev2_t = s_ev2.n

                @blk.vector
                def _(v):
                    v.wait_ge(s_pe.h, pe_b + 3)
                    v.wait_ge(qw_now[0][0].h, qw_now[0][1])
                    ins = v.tensor_copy(vstrip[0][:], pb[banks[2]][:])
                    s_ev.inc(ins)
                    v.wait_ge(s_pe.h, pe_b + 4)
                    v.wait_ge(qw_now[1][0].h, qw_now[1][1])
                    ins = v.tensor_copy(vstrip[1][:], pb[banks[3]][:])
                    s_ev.inc(ins)
                ev_t = s_ev.n

                @blk.sync
                def _(sy):
                    sy.wait_ge(s_ev2.h, ev2_t)
                    sy.wait_ge(s_ev.h, ev_t)
                    ins = sy.dma_start(
                        out=qkv_dram[jc * 128:(jc + 1) * 128, HALF:HALF + 512],
                        in_=vstrip[0][:])
                    s_qw[0].inc(ins, 16)
                    ins = sy.dma_start(
                        out=qkv_dram[jc * 128:(jc + 1) * 128, HALF + 512:S],
                        in_=vstrip[1][:])
                    s_qw[1].inc(ins, 16)
            bank_guard[i % 2] = ([(s_ev2, s_ev2.n), (s_ev, s_ev.n)]
                                 if do_sq is None else [(s_ev, s_ev.n)])

        idx = 0
        bank_guard[0] = [(s_ev, s_ev.n)]
        bank_guard[1] = [(s_ev, s_ev.n)]
        for jc in range(ND):
            qkv_chunk(idx, jc, accq[:])
            idx += 1
        for jc in range(ND, 2 * ND):
            qkv_chunk(idx, jc, acck[:])
            idx += 1

        @blk.tensor
        def _(t):
            t.wait_ge(s_ve.h, s_ve.n)
            t.wait_ge(s_ev.h, s_ev.n)
            t.wait_ge(s_ev2.h, s_ev2.n)
            t.wait_ge(s_se.h, s_se.n)
            for w_, acc in ((0, accq), (1, acck)):
                for ni in range(4):
                    ins = t.matmul(pb[w_ * 4 + ni][0:1, :], ones_col[:],
                                   acc[:, ni * 512:(ni + 1) * 512],
                                   start=True, stop=True)
                    s_pe.inc(ins)

        @blk.scalar
        def _(e):
            e.wait_ge(s_pe.h, s_pe.n)
            for w_, row in ((0, sq_t), (1, sk_row)):
                for ni in range(4):
                    ins = e.activation(row[0:1, ni * 512:(ni + 1) * 512],
                                       pb[w_ * 4 + ni][0:1, :],
                                       AF.Ln, bias=ceps[0:1, 1:2], scale=1.0 / DM)
                    if w_ == 1 and ni == 3:
                        s_se.inc(ins)
            e.wait_ge(s_se.h, s_se.n)
            e.activation(sq_t[0:1, 0:DM], sq_t[0:1, 0:DM], AF.Exp, scale=-0.5)
            sedge(e, e.activation(sk_row[0:1, 0:DM], sk_row[0:1, 0:DM], AF.Exp,
                                  scale=-0.5))
        se_t = s_se.n

        @blk.sync
        def _(sy):
            sy.wait_ge(s_se.h, se_t)
            sy.dma_start(out=scr_r[0].rearrange("b c -> (b c)").unsqueeze(0),
                         in_=sq_t[0:1, 0:DM]).then_inc(s_rr.h, 16)
            s_rr.n += 16
            ins = sy.dma_start(out=scr_r[1].rearrange("b c -> (b c)").unsqueeze(0),
                               in_=sk_row[0:1, 0:DM])
            s_rr.inc(ins, 16)

        part1_qw.extend((s_, s_.n) for s_ in s_qw)
        emit_bcast(0)
        for fc in range(ND):
            i = idx
            jc = 2 * ND + fc
            qkv_chunk(i, jc, None)
            a3_chain(0, fc, v_in_at2=True)
            idx += 1
        a3_done_ve[0] = s_ve.n
        yT_ready_ve[0] = s_ve.n
        part2_qw.extend((s_, s_.n) for s_ in s_qw)

        emit_bcast(1)

    def emit_a3h1_hook(mi):
        a3_chain(1, mi)
        if mi == ND - 1:
            a3_done_ve[1] = s_ve.n
            yT_ready_ve[1] = s_ve.n

    def emit_p5(pre_m_hook=None):
        p5_entry_ev = s_ev.n
        p5_entry_ev2 = s_ev2.n
        s_swh = S_("swh")
        s_opv = S_("opv")
        s_adp = adp_sems
        pending = []
        planned = [s_.n // 16 for s_ in s_adp]
        tm_need = {}
        nD_blocks = [0]

        def flush_one():
            if pending:
                pending.pop(0)()

        for half in range(2):
            for sg in range(4):
                swh_base = s_swh.n
                opv_sg_guard = s_opv.n
                for mi in range(16):
                    if pre_m_hook is not None and half == 0 and sg == 0:
                        pre_m_hook(mi)
                    m = sg * 16 + mi
                    pe_marks = []
                    for gi, src_blk in ((0, wuT_blk), (1, wgT_blk)):
                        @blk.gpsimd
                        def _(g):
                            w_stream(g, "C", wst,
                                     src_blk[m].rearrange("(dc p) j -> p dc j", p=128))
                        if gi == 1:
                            flush_one()

                        ld_s = wsets["C"]["ld"][(wsets["C"]["i"] - 1) % 2]
                        ld_t = ld_s.n
                        banks = (((0, 1), (2, 3)) if mi % 2 == 0 else ((4, 5), (6, 7)))[gi]

                        @blk.tensor
                        def _(t):
                            t.wait_ge(ld_s.h, ld_t)
                            if half == 0 and sg == 0 and mi < 2:
                                t.wait_ge(s_ev.h, p5_entry_ev)
                                t.wait_ge(s_ev2.h, p5_entry_ev2)
                            if gi == 0:
                                t.wait_ge(s_ve.h, yT_ready_ve[half])
                                if mi >= 2:
                                    t.wait_ge(s_swh.h, swh_base + mi - 1)
                                if mi == 1:
                                    t.wait_ge(s_opv.h, opv_sg_guard)
                            for dt in range(ND):
                                for ni in range(2):
                                    ins = t.matmul(
                                        pb[banks[ni]][:],
                                        wst[(wsets["C"]["i"] - 1) % 2][:, dt * 128:(dt + 1) * 128],
                                        yT[:, dt * S + half * HALF + ni * 512:
                                           dt * S + half * HALF + (ni + 1) * 512],
                                        start=(dt == 0), stop=(dt == ND - 1))
                            s_pe.inc(ins)
                        wsets["C"]["done"].append((s_pe, s_pe.n))
                        pe_marks.append(s_pe.n)

                    pe_u, pe_g = pe_marks
                    bu, bg = (((0, 1), (2, 3)) if mi % 2 == 0 else ((4, 5), (6, 7)))

                    @blk.scalar
                    def _(e):
                        e.wait_ge(s_pe.h, pe_g)
                        e.wait_ge(s_swh.h, swh_base + mi)
                        e.activation(silu_t[:, 0:512], pb[bg[0]][:], AF.Silu)
                        sedge(e, e.activation(silu_t[:, 512:1024], pb[bg[1]][:], AF.Silu))

                    @blk.vector
                    def _(v):
                        v.wait_ge(s_se.h, s_se.n)
                        v.wait_ge(s_pe.h, pe_u)
                        col = mi * HALF
                        v.tensor_tensor(h_sg[:, col:col + 512], pb[bu[0]][:],
                                        silu_t[:, 0:512], op=ALU.mult)
                        ins = v.tensor_tensor(h_sg[:, col + 512:col + 1024], pb[bu[1]][:],
                                              silu_t[:, 512:1024], op=ALU.mult)
                        s_swh.inc(ins)
                        v.wait_ge(s_swh.h, s_swh.n)

                sil_last = s_se.n
                for oq in range(4):
                    @blk.gpsimd
                    def _(g):
                        if nD_blocks[0] < 2:
                            for sp_ in s_qw:
                                g.wait_ge(sp_.h, sp_.n)
                            g.wait_ge(s_ve.h, a3_done_ve[0])
                        nD_blocks[0] += 1
                        w_stream(g, "D", wob,
                                 woutT[sg * DM:(sg + 1) * DM, oq * 512:(oq + 1) * 512]
                                 .rearrange("(dc p) o -> p dc o", p=128))
                    flush_one()

                    ld_s = wsets["D"]["ld"][(wsets["D"]["i"] - 1) % 2]
                    ld_t = ld_s.n
                    opv_base = s_opv.n
                    ad_base = [planned[0], planned[1]]

                    @blk.tensor
                    def _(t):
                        t.wait_ge(ld_s.h, ld_t)
                        if oq == 0:
                            t.wait_ge(s_se.h, sil_last)
                        for tc in range(NTH):
                            if tc >= 2:
                                t.wait_ge(s_opv.h, opv_base + tc - 1)
                            elif oq == 0 and sg == 0 and half == 0:
                                pass
                            else:
                                t.wait_ge(s_opv.h, opv_base - 2 + tc + 1)
                            for dffc in range(16):
                                if oq == 0 and tc == 0:
                                    t.wait_ge(s_swh.h, swh_base + dffc + 1)
                                ins = t.matmul(
                                    pb[6 + tc % 2][:],
                                    h_sg[:, dffc * HALF + tc * 128: dffc * HALF + tc * 128 + 128],
                                    wob[(wsets["D"]["i"] - 1) % 2][:, dffc * 512:(dffc + 1) * 512],
                                    start=(dffc == 0), stop=(dffc == 15))
                            s_pe.inc(ins)
                    wsets["D"]["done"].append((s_pe, s_pe.n))

                    pe_base = s_pe.n - NTH

                    @blk.vector
                    def _(v):
                        for tc in range(NTH):
                            j = tc % 4
                            v.wait_ge(s_pe.h, pe_base + tc + 1)
                            if j in tm_need:
                                p_, cnt = tm_need[j]
                                v.wait_ge(s_adp[p_].h, cnt * 16)
                            ins = v.tensor_copy(tmpout[j][:], pb[6 + tc % 2][:])
                            s_opv.inc(ins)
                            planned[j] += 1
                            tm_need[j] = (j, planned[j])

                    def mk_accum(opv_base=opv_base, half=half, sg=sg, oq=oq):
                        def emit():
                            @blk.gpsimd
                            def _(g):
                                for tc in range(NTH):
                                    g.wait_ge(s_opv.h, opv_base + tc + 1)
                                    rows = half * HALF + tc * 128
                                    ins = g.dma_start(
                                        out=xspill[rows:rows + 128,
                                                   oq * 512:(oq + 1) * 512],
                                        in_=tmpout[tc % 4][:], accum_op=ALU.add)
                                    s_adp[tc % 4].inc(ins, 16)
                        return emit

                    pending.append(mk_accum())

            while pending:
                flush_one()

    if "ffn1" in stages:
        if "attn" in stages:
            def a1_factory(guard_pe):
                a1_early[0] = True
                return mk_early_stats(xspill, 0, guard_pe,
                                      extra_waits=[p for p in spill_q[0]] +
                                                  [p for p in spill_q[1]])
            emit_ffn(0, xr1, x_in, do_output=False, post_hook_factory=a1_factory)
        else:
            emit_ffn(0, xr1, x_in, do_output=(stages == ("ffn1",)))
    if "attn" in stages:
        emit_a1()
        emit_qkv_attn()
    if "proj" in stages:
        emit_p5(pre_m_hook=emit_a3h1_hook)
    if "ffn2" in stages:
        wsets["A"]["pre"] = [(s_pe, s_pe.n), (s_pe, s_pe.n)]
        last_tr[0] = [s_pe.n, s_pe.n]
        emit_ffn(1, xr2, xspill, do_output=True)
    elif stages != ("ffn1",):
        @blk.sync
        def _(sy):
            for tc in range(NT):
                ins = sy.dma_start(out=xrt[0][:], in_=xspill[tc * 128:(tc + 1) * 128, :])
                s_x.inc(ins, 16)
                sy.wait_ge(s_x.h, s_x.n)
                ins = sy.dma_start(out=y_out[tc * 128:(tc + 1) * 128, :], in_=xrt[0][:])
                s_od.inc(ins, 16)
                sy.wait_ge(s_od.h, s_od.n)

    @blk.sync
    def _(sy):
        sy.wait_ge(s_od.h, s_od.n)
        for s_, n_ in spill_all:
            sy.wait_ge(s_.h, n_)

    blk_ctx.__exit__(None, None, None)
    ctx.close()
    return nc


def np_split_inputs(inputs):
    x = np.asarray(inputs["L0_attn_x"], dtype=np.float32)
    w_qkv = np.asarray(inputs["L0_attn_w_qkv"], dtype=np.float32)
    w_sw = np.asarray(inputs["L0_attn_w_swiglu"], dtype=np.float32)
    w_out = np.asarray(inputs["L0_attn_w_out"], dtype=np.float32)
    gate_w = np.asarray(inputs["L0_ffn_gate_w"], dtype=np.float32)
    exp_w = np.asarray(inputs["L0_ffn_expert_w"], dtype=np.float32)
    exp_b = np.asarray(inputs["L0_ffn_expert_b"], dtype=np.float32)

    wqkvT = np.ascontiguousarray(w_qkv.T)
    shared = {
        "wqkvT_p2": np.ascontiguousarray(wqkvT.reshape(DM, 48, 128).transpose(1, 0, 2)),
        "wuT_blk": np.ascontiguousarray(
            w_sw[:DFF].T.reshape(DM, 64, 128).transpose(1, 0, 2)),
        "wgT_blk": np.ascontiguousarray(
            w_sw[DFF:].T.reshape(DM, 64, 128).transpose(1, 0, 2)),
        "woutT": np.ascontiguousarray(w_out.T),
        "gateT": np.ascontiguousarray(gate_w.T),
        "expb": np.ascontiguousarray(exp_b),
        "expT_blk": np.ascontiguousarray(
            exp_w.transpose(0, 2, 1).reshape(NE, DM, 4, 512).transpose(0, 2, 1, 3)),
    }
    return [dict(shared, x=np.ascontiguousarray(x[c])) for c in range(N_CORES)]


def kernel(**inputs):
    in_maps = np_split_inputs(inputs)
    nc = build_program()
    res = run_bass_kernel_spmd(nc, in_maps, core_ids=list(range(N_CORES)))
    return np.stack([np.asarray(r["y"], dtype=np.float32) for r in res.results], axis=0)


# version 09
# speedup vs baseline: 1.2442x; 1.0395x over previous
import numpy as np
import ml_dtypes
import concourse.bass as bass
import concourse.mybir as mybir
from concourse.bass_utils import run_bass_kernel_spmd
from contextlib import ExitStack

F32 = mybir.dt.float32
BF16 = mybir.dt.bfloat16
AF = mybir.ActivationFunctionType
ALU = mybir.AluOpType
AX = mybir.AxisListType

S = 2048
DM = 2048
DFF = 8192
NE = 8
NT = S // 128
ND = DM // 128
HALF = S // 2
NTH = NT // 2
EPS_RMS = 1.1920929e-07
EPS_LN = 1e-5
N_CORES = 8
SB_BASE = 20448
SB_LIMIT = 229344


class Sem:
    __slots__ = ("h", "n")

    def __init__(self, nc, name):
        self.h = nc.alloc_semaphore(name)
        self.n = 0

    def inc(self, instr, k=1):
        instr.then_inc(self.h, k)
        self.n += k
        return instr


def build_program(stages=("ffn1", "attn", "proj", "ffn2")):
    nc = bass.Bass("TRN2", target_bir_lowering=False)
    d = nc.dram_tensor
    x_in = d("x", [S, DM], F32, kind="ExternalInput").ap()
    wqkvT_p2 = d("wqkvT_p2", [48, DM, 128], F32, kind="ExternalInput").ap()
    wuT_blk = d("wuT_blk", [64, DM, 128], F32, kind="ExternalInput").ap()
    wgT_blk = d("wgT_blk", [64, DM, 128], F32, kind="ExternalInput").ap()
    woutT = d("woutT", [DFF, DM], F32, kind="ExternalInput").ap()
    gateT = d("gateT", [DM, NE], F32, kind="ExternalInput").ap()
    expb = d("expb", [NE, DM], F32, kind="ExternalInput").ap()
    expT_blk = d("expT_blk", [NE, 4, DM, 512], F32, kind="ExternalInput").ap()
    y_out = d("y", [S, DM], F32, kind="ExternalOutput").ap()
    xspill = d("xspill", [S, DM], F32, kind="Internal").ap()
    scr_r = d("scr_r", [2, NT, 128], F32, kind="Internal").ap()
    qkv_dram = d("qkvd", [3 * DM, S], F32, kind="Internal").ap()

    c_ident_bf = nc.inline_tensor(np.eye(128, dtype=ml_dtypes.bfloat16), "c_ib").ap()
    c_ident_f = nc.inline_tensor(np.eye(128, dtype=np.float32), "c_if").ap()
    c_eps = nc.inline_tensor(
        np.tile(np.array([[EPS_LN, EPS_RMS, 1e-6, 0.0]], dtype=np.float32), (128, 1)),
        "c_eps").ap()

    def at(name, shape, dtype, offset):
        offset += SB_BASE
        per = int(np.prod(shape[1:])) * mybir.dt.size(dtype)
        assert offset + per <= SB_LIMIT, f"sbuf overflow {name}: {offset + per}"
        return nc.alloc_sbuf_tensor_at(name, list(shape), dtype, offset=offset)

    o = [0]

    def nxt(name, shape, dtype):
        per = int(np.prod(shape[1:])) * mybir.dt.size(dtype)
        off = (o[0] + 31) // 32 * 32
        t = at(name, shape, dtype, off)
        o[0] = off + per
        return t

    ident_bf = nxt("ident_bf", [128, 128], BF16)
    ident_f = nxt("ident_f", [128, 128], F32)
    ones1_f = nxt("ones1_f", [1, 128], F32)
    gateT_bf = nxt("gateT_bf", [128, ND * NE], BF16)
    expb_bf = nxt("expb_bf", [NE, DM], BF16)
    wmaskT = nxt("wmaskT", [NE, S], F32)
    wmaskT_bf = nxt("wmaskT_bf", [NE, S], BF16)
    wm = nxt("wm", [128, NTH * NE], F32)
    gsc = nxt("gsc", [128, 32], F32)
    lnsc = nxt("lnsc", [128, 48], F32)
    rq_all = nxt("rq_all", [128, NT], F32)
    rk_all = nxt("rk_all", [128, NT], F32)
    rqkT = nxt("rqkT", [NT, 128], F32)
    ln_tm_off = (o[0] + 31) // 32 * 32
    ln_tm = nxt("ln_tm", [128, DM], BF16)
    ln_f32 = nxt("ln_f32", [128, DM], F32)
    rqh_row = at("rqh_row", [1, HALF], F32, ln_tm_off)
    rkh_row = at("rkh_row", [1, HALF], F32, ln_tm_off + 4096)
    zeros = nxt("zeros", [128, HALF], F32)
    stats_all = nxt("stats_all", [128, NT * 8 * 6], F32)
    rkb = nxt("rkb", [128, HALF], F32)
    ceps = nxt("ceps", [128, 4], F32)
    assert o[0] <= 49152, o[0]

    R1, R2, R2B, R3 = 49152, 114688, 147456, 180224
    xr1 = at("xr1", [128, NTH * DM], F32, R1)
    xrt = [at(f"xrt{i}", [128, DM], F32, R1 + i * 8192) for i in range(2)]
    wqk1 = [at(f"wqk1_{i}", [128, 16 * 512], BF16, R1 + i * 16384) for i in range(2)]
    yT = at("yT", [128, ND * S], BF16, R1)
    xr2 = at("xr2", [128, NTH * DM], F32, R1)
    lnT = at("lnT", [128, ND * S], BF16, R2)
    lnTh = at("lnTh", [128, ND * HALF], BF16, R2)
    wblk = [at(f"wblk{i}", [128, 16 * 512], BF16, R2B + i * 16384) for i in range(2)]
    h_sg = at("h_sg", [128, 16 * HALF], BF16, R2)
    wob = [at(f"wob{i}", [128, 16 * 512], BF16, R2B + i * 16384) for i in range(2)]
    wst = [at(f"wst{i}", [128, ND * 128], BF16, R3 + i * 4096) for i in range(2)]
    at1 = at("at1", [128, HALF], F32, R3 + 8192)
    at2 = at("at2", [128, HALF], F32, R3 + 12288)
    at3 = at("at3", [128, HALF], F32, R3 + 16384)
    at4 = at("at4", [128, HALF], F32, R3 + 20480)
    rqb = at("rqb", [128, HALF], F32, R3 + 24576)
    silu_t = at("silu_t", [128, HALF], F32, ln_tm_off)
    tmpout = [at(f"tmpout{i}", [128, 512], F32, ln_tm_off + 4096 + i * 2048)
              for i in range(4)]
    qkv_sb = [at(f"qkvsb{i}", [128, DM], F32, R3 + 8192 + i * 8192) for i in range(2)]
    accq = at("accq", [128, DM], F32, R1)
    acck = at("acck", [128, DM], F32, R1 + 8192)
    sq_t = at("sq_t", [128, DM], F32, R1 + 16384)
    ones_col = at("ones_col", [128, 1], F32, R1 + 24576)
    sk_row = at("sk_row", [1, DM], F32, R1 + 24608)
    vstrip = [at(f"vstrip{i}", [128, 512], F32, ln_tm_off + 8192 + i * 2048)
              for i in range(2)]

    ctx = ExitStack()
    pb = [ctx.enter_context(nc.psum_tensor(f"pb{i}", [128, 512], F32)) for i in range(8)]

    def pbf(i):
        return pb[i][:].bitcast(BF16)

    sems = {}

    def S_(name):
        if name not in sems:
            sems[name] = Sem(nc, "s_" + name)
        return sems[name]

    s_x = S_("x")
    s_pe = S_("pe")
    s_ve = S_("ve")
    s_se = S_("se")
    s_ev = S_("ev")
    s_ev2 = S_("ev2")
    s_sp = S_("sp")
    s_ad = S_("ad")
    s_od = S_("od")
    s_slq = [S_("slq0"), S_("slq1")]
    s_xl2 = [S_("xl2a"), S_("xl2b")]
    spill_q = [[], []]
    spill_all = []
    last_tr = [None]
    s_rr = S_("rr")
    adp_sems = [S_("adp0"), S_("adp1"), S_("adp2"), S_("adp3")]
    wsets = {}
    for nm in ("A", "B", "C", "D"):
        wsets[nm] = {"ld": [S_("wl" + nm + "0"), S_("wl" + nm + "1")], "i": 0, "done": []}

    def w_stream(g, setname, buf_list, src_ap):
        ws = wsets[setname]
        i = ws["i"]
        buf = buf_list[i % 2]
        if ws.get("pre"):
            sem, val = ws["pre"].pop(0)
            g.wait_ge(sem.h, val)
        if i >= 2:
            sem, val = ws["done"][i - 2]
            g.wait_ge(sem.h, val)
        nblk = src_ap.shape[1]
        ws["ld"][i % 2].inc(g.dma_start(
            out=buf[:].rearrange("p (b i) -> p b i", b=nblk), in_=src_ap), 16)
        ws["i"] = i + 1
        return buf

    def vedge(v, instr):
        s_ve.inc(instr)
        v.wait_ge(s_ve.h, s_ve.n)

    def sedge(e, instr):
        s_se.inc(instr)
        e.wait_ge(s_se.h, s_se.n)

    blk_ctx = nc.Block()
    blk = blk_ctx.__enter__()

    s_bar = S_("bar")
    bar_n = [0]

    def barrier():
        bar_n[0] += 1
        tgt = 5 * bar_n[0]
        for ename in ("sync", "tensor", "vector", "scalar", "gpsimd"):
            def _f(e, tgt=tgt):
                e.drain()
                e.nop().then_inc(s_bar.h, 1)
                e.wait_ge(s_bar.h, tgt)
            getattr(blk, ename)(_f)

    s_cg = S_("cg")

    @blk.sync
    def _(sy):
        s_x.inc(sy.dma_start(out=ident_bf[:], in_=c_ident_bf[:]), 16)
        s_x.inc(sy.dma_start(out=ceps[:], in_=c_eps[:]), 16)
        s_x.inc(sy.dma_start(out=ident_f[:], in_=c_ident_f[:]), 16)
        sy.wait_ge(s_x.h, s_x.n)

    @blk.gpsimd
    def _(g):
        s_cg.inc(g.dma_start(out=gateT_bf[:].rearrange("p (c e) -> p c e", e=NE),
                             in_=gateT.rearrange("(c p) e -> p c e", p=128)), 16)
        s_cg.inc(g.dma_start(out=expb_bf[:], in_=expb), 16)
        g.wait_ge(s_cg.h, s_cg.n)

    @blk.vector
    def _(v):
        v.memset(ones1_f[:], 1.0)
        s_ve.inc(v.memset(zeros[:], 0.0))

    barrier()

    s_lnt = S_("lnt")
    ln2_tm = [at(f"ln2tm{i}", [128, DM], BF16, R3 + i * 4096) for i in range(2)]
    a1f32 = [at(f"a1f{i}", [128, DM], F32, R3 + 8192 + i * 8192) for i in range(2)]

    xstg = [at("xstg0", [128, DM], F32, R3), at("xstg1", [128, DM], F32, R3 + 8192)]
    s_st2 = [S_("st0"), S_("st1")]

    def mk_early_stats(src_dram, src_rows, guard_pe, extra_waits=()):
        marks = {}
        steps = []
        for tc in range(8):
            def step(tc=tc):
                @blk.sync
                def _(sy):
                    sy.wait_ge(s_pe.h, guard_pe)
                    for s_, n_ in extra_waits:
                        sy.wait_ge(s_.h, n_)
                    if tc >= 2:
                        sy.wait_ge(s_ve.h, marks[tc - 2])
                    ins = sy.dma_start(
                        out=xstg[tc % 2][:],
                        in_=src_dram[src_rows + tc * 128: src_rows + (tc + 1) * 128, :])
                    s_st2[tc % 2].inc(ins, 16)
                ld = s_st2[tc % 2].n

                @blk.vector
                def _(v):
                    v.wait_ge(s_st2[tc % 2].h, ld)
                    for i in range(4):
                        ins = v.bn_stats(stats_all[:, tc * 24 + i * 6:
                                                   tc * 24 + (i + 1) * 6],
                                         xstg[tc % 2][:, i * 512:(i + 1) * 512])
                    vedge(v, ins)
                    ins = v.bn_aggr(lnsc[:, 2 * tc:2 * tc + 2],
                                    stats_all[:, tc * 24:tc * 24 + 24])
                    vedge(v, ins)
                    marks[tc] = s_ve.n
            steps.append(step)
        return steps

    def emit_ln_stats(srcs, sb, pre_wait=None):
        n = len(srcs)
        L = lnsc

        @blk.vector
        def _(v):
            if pre_wait is not None:
                v.wait_ge(pre_wait[0].h, pre_wait[1])
            for ci in range(n):
                for i in range(4):
                    ins = v.bn_stats(stats_all[:, (sb + ci) * 24 + i * 6:
                                               (sb + ci) * 24 + (i + 1) * 6],
                                     srcs[ci][:, i * 512:(i + 1) * 512])
            vedge(v, ins)
            for ci in range(n):
                ins = v.bn_aggr(L[:, 2 * (sb + ci):2 * (sb + ci) + 2],
                                stats_all[:, (sb + ci) * 24:(sb + ci) * 24 + 24])
            vedge(v, ins)

    def emit_ln_batch(srcs, dst_lnT, col_of, rms_fold, accum_rows=None,
                      sb=0, prev_tr=None, pre_wait=None, skip_stats=False):
        n = len(srcs)
        L = lnsc
        mean_str = L[:, 2 * sb:2 * (sb + n):2]
        var_str = L[:, 2 * sb + 1:2 * (sb + n):2]
        rsd = L[:, 16 + sb:16 + sb + n]
        scl = L[:, 24 + sb:24 + sb + n]
        bia = L[:, 32 + sb:32 + sb + n]
        tmp = L[:, 40 + sb:40 + sb + n]

        if not skip_stats:
            emit_ln_stats(srcs, sb, pre_wait=pre_wait)

        @blk.scalar
        def _(e):
            e.wait_ge(s_ve.h, s_ve.n)
            sedge(e, e.activation(rsd, var_str, AF.Ln, bias=ceps[:, 0:1]))
            sedge(e, e.activation(rsd, rsd, AF.Exp, scale=-0.5))

        if rms_fold:
            @blk.vector
            def _(v):
                v.wait_ge(s_se.h, s_se.n)
                vedge(v, v.tensor_tensor(tmp, rsd, rsd, op=ALU.mult))
                vedge(v, v.tensor_tensor(tmp, var_str, tmp, op=ALU.mult))

            @blk.scalar
            def _(e):
                e.wait_ge(s_ve.h, s_ve.n)
                sedge(e, e.activation(tmp, tmp, AF.Ln, bias=ceps[:, 1:2]))
                sedge(e, e.activation(tmp, tmp, AF.Exp, scale=-0.5))

        @blk.vector
        def _(v):
            v.wait_ge(s_se.h, s_se.n)
            if rms_fold:
                vedge(v, v.tensor_tensor(scl, rsd, tmp, op=ALU.mult))
            else:
                vedge(v, v.tensor_copy(scl, rsd))
            ins = v.scalar_tensor_tensor(bia, mean_str, -1.0, scl,
                                         op0=ALU.mult, op1=ALU.mult)
            vedge(v, ins)

        affine_ve = [0] * n
        ad_marks = [adp_sems[0].n, adp_sems[1].n]
        width = dst_lnT.shape[1] // ND

        def emit_affine(ci):
            @blk.vector
            def _(v):
                if ci >= 2:
                    v.wait_ge(s_pe.h, tr_done[ci - 2])
                elif prev_tr is not None:
                    v.wait_ge(s_pe.h, prev_tr[ci])
                ins = v.tensor_scalar(ln2_tm[ci % 2][:], srcs[ci][:],
                                      L[:, 24 + sb + ci:25 + sb + ci],
                                      L[:, 32 + sb + ci:33 + sb + ci],
                                      op0=ALU.mult, op1=ALU.add)
                if accum_rows is None:
                    vedge(v, ins)
                else:
                    s_ve.inc(ins)
                    v.wait_ge(adp_sems[ci % 2].h, ad_marks[ci % 2])
                    ins = v.tensor_scalar(a1f32[ci % 2][:], srcs[ci][:],
                                          L[:, 24 + sb + ci:25 + sb + ci],
                                          L[:, 32 + sb + ci:33 + sb + ci],
                                          op0=ALU.mult, op1=ALU.add)
                    vedge(v, ins)
            affine_ve[ci] = s_ve.n

        tr_done = [0] * n
        emit_affine(0)
        for ci in range(n):
            col = col_of(ci)

            @blk.tensor
            def _(t):
                t.wait_ge(s_ve.h, affine_ve[ci])
                t.wait_ge(s_lnt.h, s_lnt.n)
                for h8 in range(2):
                    bank = 6 + h8
                    for i in range(8):
                        dt = h8 * 8 + i
                        ins = t.transpose(pbf(bank)[:, i * 128:(i + 1) * 128],
                                          ln2_tm[ci % 2][:, dt * 128:(dt + 1) * 128],
                                          ident_bf[:])
                    s_pe.inc(ins)
            tr_done[ci] = s_pe.n

            if accum_rows is not None:
                @blk.gpsimd
                def _(g):
                    g.wait_ge(s_ve.h, affine_ve[ci])
                    rows = accum_rows + ci * 128
                    ins = g.dma_start(out=xspill[rows:rows + 128, :],
                                      in_=a1f32[ci % 2][:], accum_op=ALU.add)
                    adp_sems[ci % 2].inc(ins, 16)
                ad_marks[ci % 2] = adp_sems[ci % 2].n

            if ci + 1 < n:
                emit_affine(ci + 1)

            @blk.vector
            def _(v):
                for h8 in range(2):
                    v.wait_ge(s_pe.h, tr_done[ci] - 1 + h8)
                    bank = 6 + h8
                    src = pbf(bank)[:, 0:1024].rearrange("p (b i) -> p b i", b=8)
                    dsts = dst_lnT[:].rearrange("p (dt w) -> p dt w", dt=ND)[
                        :, h8 * 8:(h8 + 1) * 8, col:col + 128]
                    ins = v.tensor_copy(dsts, src)
                s_lnt.inc(ins)
                v.wait_ge(s_lnt.h, s_lnt.n)
        return tr_done

    def emit_gating_batch(gbase, lnT_t):
        G = gsc
        glog = stats_all[:, 256:320]
        mk1 = stats_all[:, 320:384]
        mk2 = stats_all[:, 384:448]
        width = lnT_t.shape[1] // ND

        @blk.tensor
        def _(t):
            t.wait_ge(s_lnt.h, s_lnt.n)
            t.wait_ge(s_ev.h, s_ev.n)
            for ci in range(NTH):
                for dt in range(ND):
                    ins = t.matmul(
                        pb[5][:, ci * 8:(ci + 1) * 8],
                        lnT_t[:, dt * width + ci * 128: dt * width + ci * 128 + 128],
                        gateT_bf[:, dt * 8:(dt + 1) * 8],
                        start=(dt == 0), stop=(dt == ND - 1))
            s_pe.inc(ins)

        @blk.vector
        def _(v):
            v.wait_ge(s_pe.h, s_pe.n)
            ins = v.tensor_copy(glog, pb[5][:, 0:64])
            s_ev.inc(ins)
            v.wait_ge(s_ev.h, s_ev.n)
            for ci in range(NTH):
                ins = v.reduce_max(G[:, ci:ci + 1], glog[:, ci * 8:(ci + 1) * 8],
                                   axis=AX.X)
            vedge(v, ins)
            for ci in range(NTH):
                ins = v.tensor_scalar(mk1[:, ci * 8:(ci + 1) * 8],
                                      glog[:, ci * 8:(ci + 1) * 8],
                                      G[:, ci:ci + 1], None, op0=ALU.is_ge)
            vedge(v, ins)
            for ci in range(NTH):
                ins = v.scalar_tensor_tensor(mk2[:, ci * 8:(ci + 1) * 8],
                                             mk1[:, ci * 8:(ci + 1) * 8], -1e30,
                                             glog[:, ci * 8:(ci + 1) * 8],
                                             op0=ALU.mult, op1=ALU.add)
            vedge(v, ins)
            for ci in range(NTH):
                ins = v.reduce_max(G[:, 8 + ci:9 + ci],
                                   mk2[:, ci * 8:(ci + 1) * 8], axis=AX.X)
            vedge(v, ins)
            for ci in range(NTH):
                v.tensor_scalar(mk2[:, ci * 8:(ci + 1) * 8],
                                mk2[:, ci * 8:(ci + 1) * 8],
                                G[:, 8 + ci:9 + ci], None, op0=ALU.is_ge)
            vedge(v, v.tensor_tensor(G[:, 16:24], G[:, 0:8], G[:, 8:16], op=ALU.subtract))

        @blk.scalar
        def _(e):
            e.wait_ge(s_ve.h, s_ve.n)
            sedge(e, e.activation(G[:, 24:32], G[:, 16:24], AF.Exp, scale=-1.0))
            sedge(e, e.activation(G[:, 24:32], G[:, 24:32], AF.Ln, bias=1.0))
            sedge(e, e.activation(G[:, 24:32], G[:, 24:32], AF.Exp, scale=-1.0))

        @blk.vector
        def _(v):
            v.wait_ge(s_se.h, s_se.n)
            vedge(v, v.tensor_scalar_mul(G[:, 24:32], G[:, 24:32], 0.5))
            ins = v.tensor_scalar(G[:, 8:16], G[:, 24:32], -1.0, 0.5,
                                  op0=ALU.mult, op1=ALU.add)
            vedge(v, ins)
            for ci in range(NTH):
                v.tensor_scalar_mul(mk1[:, ci * 8:(ci + 1) * 8],
                                    mk1[:, ci * 8:(ci + 1) * 8],
                                    G[:, 24 + ci:25 + ci])
                ins = v.tensor_scalar_mul(mk2[:, ci * 8:(ci + 1) * 8],
                                          mk2[:, ci * 8:(ci + 1) * 8],
                                          G[:, 8 + ci:9 + ci])
            vedge(v, ins)
            vedge(v, v.tensor_tensor(wm[:], mk1[:, 0:64], mk2[:, 0:64], op=ALU.add))

        for rnd in range(2):
            @blk.tensor
            def _(t):
                t.wait_ge(s_ve.h, s_ve.n)
                t.wait_ge(s_ev.h, s_ev.n)
                for j in range(4):
                    ci = rnd * 4 + j
                    ins = t.transpose(pb[5][0:8, j * 128:(j + 1) * 128],
                                      wm[:, ci * 8:(ci + 1) * 8], ident_f[:])
                s_pe.inc(ins)

            @blk.vector
            def _(v):
                v.wait_ge(s_pe.h, s_pe.n)
                gc = (gbase + rnd * 4) * 128
                ins = v.tensor_copy(wmaskT[0:8, gc:gc + 512], pb[5][0:8, 0:512])
                s_ev.inc(ins)
                v.wait_ge(s_ev.h, s_ev.n)

        @blk.vector
        def _(v):
            gc = gbase * 128
            vedge(v, v.tensor_copy(wmaskT_bf[0:8, gc:gc + HALF], wmaskT[0:8, gc:gc + HALF]))

    def emit_ffn(ffn_idx, xr, src_dram, do_output, post_hook_factory=None):
        early_done = [False]
        for half in range(2):
            gbase = half * NTH
            rows = half * HALF
            lq = [0, 0]

            @blk.scalar
            def _(e):
                if ffn_idx == 1 and half == 0:
                    e.wait_ge(s_pe.h, s_pe.n)
                    e.wait_ge(s_ad.h, s_ad.n)
                    for sp_ in adp_sems:
                        e.wait_ge(sp_.h, sp_.n)
                for q in range(2):
                    for s_, n_ in spill_q[q]:
                        e.wait_ge(s_.h, n_)
                    ins = e.dma_start(
                        out=xr[:, q * 4 * DM:(q + 1) * 4 * DM]
                            .rearrange("p (c d) -> p c d", c=4),
                        in_=src_dram[rows + q * 512: rows + (q + 1) * 512, :]
                            .rearrange("(c p) d -> p c d", p=128))
                    s_xl2[q].inc(ins, 16)
                    lq[q] = s_xl2[q].n

            early = early_done[0] if half == 1 else False
            if not early:
                emit_ln_stats([xr[:, tc * DM:(tc + 1) * DM] for tc in range(4)],
                              sb=0, pre_wait=(s_xl2[0], lq[0]))
                emit_ln_stats([xr[:, tc * DM:(tc + 1) * DM] for tc in range(4, NTH)],
                              sb=4, pre_wait=(s_xl2[1], lq[1]))
            else:
                @blk.vector
                def _(v):
                    for q in range(2):
                        v.wait_ge(s_xl2[q].h, lq[q])
            trA = emit_ln_batch([xr[:, tc * DM:(tc + 1) * DM] for tc in range(4)],
                                lnTh, lambda ci: ci * 128, rms_fold=False,
                                sb=0, prev_tr=last_tr[0], skip_stats=True)
            trB = emit_ln_batch([xr[:, tc * DM:(tc + 1) * DM] for tc in range(4, NTH)],
                                lnTh, lambda ci: (4 + ci) * 128, rms_fold=False,
                                sb=4, prev_tr=trA[-2:], skip_stats=True)
            last_tr[0] = trB[-2:]
            if half == 0:
                hook_steps = mk_early_stats(src_dram, HALF, trB[-1])
            elif post_hook_factory is not None:
                hook_steps = post_hook_factory(trB[-1])
            else:
                hook_steps = []
            emit_gating_batch(gbase, lnTh)

            entry_ve = s_ve.n
            entry_ev = s_ev.n
            bias_marks = {}
            for tc in range(NTH):
                gc = (gbase + tc) * 128
                bb = (0, 1, 2, 3) if tc % 2 == 0 else (4, 5, 6, 7)

                @blk.tensor
                def _(t, tc=tc, bb=bb, gc=gc):
                    if tc < 2:
                        t.wait_ge(s_ve.h, entry_ve)
                        t.wait_ge(s_ev.h, entry_ev)
                    else:
                        t.wait_ge(s_ev.h, bias_marks[tc - 2])
                    for ot in range(4):
                        ins = t.matmul(pb[bb[ot]][:], wmaskT_bf[0:8, gc:gc + 128],
                                       expb_bf[0:8, ot * 512:(ot + 1) * 512],
                                       start=True, stop=True)
                        s_pe.inc(ins)
                pe_mark = s_pe.n

                @blk.vector
                def _(v, tc=tc, bb=bb, pe_mark=pe_mark):
                    v.wait_ge(s_pe.h, pe_mark)
                    for ot in range(4):
                        dst = xr[:, tc * DM + ot * 512: tc * DM + (ot + 1) * 512]
                        ins = v.tensor_tensor(dst, dst, pb[bb[ot]][:], op=ALU.add)
                        s_ev.inc(ins)
                bias_marks[tc] = s_ev.n

            ev_base = s_ev.n
            nb = 0
            for ie in range(NE):
                for ot in range(4):
                    @blk.gpsimd
                    def _(g):
                        w_stream(g, "A", wblk,
                                 expT_blk[ie, ot].rearrange("(dc p) o -> p dc o", p=128))

                    ld_s = wsets["A"]["ld"][(wsets["A"]["i"] - 1) % 2]
                    ld_t = ld_s.n
                    bi = nb

                    @blk.tensor
                    def _(t):
                        t.wait_ge(ld_s.h, ld_t)
                        if bi == 0:
                            t.wait_ge(s_ev.h, ev_base)
                        for tc in range(NTH):
                            if bi > 0:
                                t.wait_ge(s_ev.h, ev_base + (bi - 1) * NTH + tc + 1)
                            for dt in range(ND):
                                ins = t.matmul(
                                    pb[tc][:],
                                    lnTh[:, dt * HALF + tc * 128: dt * HALF + tc * 128 + 128],
                                    wblk[bi % 2][:, dt * 512:(dt + 1) * 512],
                                    start=(dt == 0), stop=(dt == ND - 1))
                            s_pe.inc(ins)
                    wsets["A"]["done"].append((s_pe, s_pe.n))

                    pe_base = s_pe.n - NTH

                    @blk.vector
                    def _(v):
                        for tc in range(NTH):
                            v.wait_ge(s_pe.h, pe_base + tc + 1)
                            dst = xr[:, tc * DM + ot * 512: tc * DM + (ot + 1) * 512]
                            ins = v.scalar_tensor_tensor(
                                dst, pb[tc][:], wm[:, tc * 8 + ie: tc * 8 + ie + 1], dst,
                                op0=ALU.mult, op1=ALU.add)
                            s_ev.inc(ins)
                    if nb >= 2 and nb % 4 == 2 and hook_steps:
                        hook_steps.pop(0)()
                    nb += 1

            while hook_steps:
                hook_steps.pop(0)()
            early_done[0] = True

            @blk.sync
            def _(sy):
                sy.wait_ge(s_ev.h, s_ev.n)
                tgt = y_out if do_output else xspill
                for q in range(2):
                    ins = sy.dma_start(
                        out=tgt[rows + q * 512: rows + (q + 1) * 512, :]
                            .rearrange("(c p) d -> p c d", p=128),
                        in_=xr[:, q * 4 * DM:(q + 1) * 4 * DM]
                            .rearrange("p (c d) -> p c d", c=4))
                    s_slq[q].inc(ins, 16)
                    spill_q[q] = [(s_slq[q], s_slq[q].n)]
                    spill_all.append((s_slq[q], s_slq[q].n))

    a1x = at("a1x", [128, NTH * DM], F32, R1)

    a1_early = [False]

    def emit_a1():
        ve_group_done = [0]
        for grp in range(2):
            rows = grp * HALF

            @blk.sync
            def _(sy):
                if grp == 0:
                    for s_, n_ in spill_all:
                        sy.wait_ge(s_.h, n_)
                else:
                    sy.wait_ge(s_ve.h, ve_group_done[0])
                ins = sy.dma_start(
                    out=a1x[:].rearrange("p (c d) -> p c d", c=NTH),
                    in_=xspill[rows:rows + HALF, :].rearrange("(c p) d -> p c d", p=128))
                s_x.inc(ins, 16)

            xl = s_x.n

            @blk.vector
            def _(v):
                v.wait_ge(s_x.h, xl)

            trg = emit_ln_batch([a1x[:, tc * DM:(tc + 1) * DM] for tc in range(NTH)],
                                lnT, lambda ci, g=grp: (g * NTH + ci) * 128,
                                rms_fold=True, accum_rows=rows,
                                prev_tr=last_tr[0],
                                skip_stats=(grp == 0 and a1_early[0]))
            last_tr[0] = trg[-2:]
            ve_group_done[0] = s_ve.n

    s_qw = [S_("qw0"), S_("qw1")]
    s_qld = S_("qld")
    s_qld2 = [S_("qld_a"), S_("qld_b")]
    s_gp = S_("gp")
    chain_ve = [0]
    a3_done_ve = [0, 0]
    yT_ready_ve = [0, 0]
    bank_guard = {0: [], 1: []}
    part1_qw = []
    part2_qw = []
    ch_m = {"d6": [0, 0], "d7": [0, 0]}
    ev_at2 = [0, 0]
    rqb_ready = [0]
    carry_kv = stats_all[:, 0:NT]
    carry_w = stats_all[:, NT:2 * NT]

    def a3_chain(half, fc, v_in_at2=False):
        SUBW = 512
        cs = [slice(s * SUBW, (s + 1) * SUBW) for s in range(2)]
        prev_d7 = list(ch_m["d7"])
        prev_d6 = list(ch_m["d6"])
        qld = [0, 0]
        vld = [0, 0]
        ev_v = list(ev_at2)

        @blk.sync
        def _(sy):
            for s_, n_ in (part1_qw if half == 0 else part2_qw):
                sy.wait_ge(s_.h, n_)
            sy.wait_ge(s_rr.h, s_rr.n)
            for s in range(2):
                c0 = half * HALF + s * SUBW
                sq2 = s_qld2[s]
                sy.wait_ge(s_ve.h, prev_d7[s])
                sy.dma_start(out=at3[:, cs[s]],
                             in_=qkv_dram[fc * 128:(fc + 1) * 128, c0:c0 + SUBW])\
                    .then_inc(sq2.h, 16)
                sq2.n += 16
                sy.dma_start(out=at1[:, cs[s]],
                             in_=qkv_dram[DM + fc * 128: DM + (fc + 1) * 128,
                                          c0:c0 + SUBW])\
                    .then_inc(sq2.h, 16)
                sq2.n += 16
                if not v_in_at2:
                    sy.wait_ge(s_ve.h, prev_d6[s])
                    sy.dma_start(out=at2[:, cs[s]],
                                 in_=qkv_dram[2 * DM + fc * 128:
                                              2 * DM + (fc + 1) * 128, c0:c0 + SUBW])\
                        .then_inc(sq2.h, 16)
                    sq2.n += 16
                qld[s] = sq2.n
                vld[s] = sq2.n

        mv = lambda ins: (s_ve.inc(ins), s_ve.n)[1]
        ms = lambda ins: (s_se.inc(ins), s_se.n)[1]
        mg = lambda ins: (s_gp.inc(ins), s_gp.n)[1]
        m_d1 = [0, 0]; m_d3 = [0, 0]; m_s1 = [0, 0]; m_d4 = [0, 0]
        m_c4 = [0, 0]; m_s3 = [0, 0]; m_s4 = [0, 0]; m_d5 = [0, 0]
        m_d6 = [0, 0]; m_c6 = [0, 0]; m_s5 = [0, 0]; m_d7 = [0, 0]

        @blk.vector
        def _(v):
            for s in range(2):
                v.wait_ge(s_qld2[s].h, qld[s])
                m_d1[s] = mv(v.tensor_tensor(at1[:, cs[s]], at1[:, cs[s]],
                                             rkb[:, cs[s]], op=ALU.mult))
                m_d3[s] = mv(v.tensor_tensor(at3[:, cs[s]], at3[:, cs[s]],
                                             rqb[:, cs[s]], op=ALU.mult))

        @blk.scalar
        def _(e):
            for s in range(2):
                e.wait_ge(s_ve.h, m_d1[s])
                m_s1[s] = ms(e.activation(at1[:, cs[s]], at1[:, cs[s]], AF.Exp))
                e.wait_ge(s_ve.h, m_d3[s])
                mtmp = ms(e.activation(at3[:, cs[s]], at3[:, cs[s]], AF.Exp,
                                       scale=-1.0))
                e.wait_ge(s_se.h, mtmp)
                m_s3[s] = ms(e.activation(at3[:, cs[s]], at3[:, cs[s]],
                                          AF.Ln, bias=1.0))

        @blk.vector
        def _(v):
            for s in range(2):
                v.wait_ge(s_se.h, m_s1[s])
                if v_in_at2:
                    v.wait_ge(s_ev2.h, ev_v[s])
                else:
                    v.wait_ge(s_qld2[s].h, vld[s])
                v.tensor_tensor(at2[:, cs[s]], at2[:, cs[s]],
                                at1[:, cs[s]], op=ALU.mult)
                if s == 1:
                    v.wait_ge(s_ve.h, m_c4[0])
                m_d4[s] = mv(v.tensor_tensor_scan(at4[:, cs[s]], at1[:, cs[s]],
                                                  zeros[:, 0:SUBW],
                                                  carry_w[:, fc:fc + 1],
                                                  op0=ALU.add, op1=ALU.add))
                v.wait_ge(s_ve.h, m_d4[s])
                m_c4[s] = mv(v.tensor_copy(carry_w[:, fc:fc + 1],
                                           at4[:, (s + 1) * SUBW - 1:(s + 1) * SUBW]))

        @blk.scalar
        def _(e):
            for s in range(2):
                e.wait_ge(s_ve.h, m_c4[s])
                m_s4[s] = ms(e.activation(at4[:, cs[s]], at4[:, cs[s]],
                                          AF.Ln, bias=ceps[:, 2:3]))

        @blk.vector
        def _(v):
            for s in range(2):
                v.wait_ge(s_se.h, m_s3[s])
                v.wait_ge(s_se.h, m_s4[s])
                m_d5[s] = mv(v.tensor_tensor(at3[:, cs[s]], at3[:, cs[s]],
                                             at4[:, cs[s]], op=ALU.add))
                if s == 1:
                    v.wait_ge(s_ve.h, m_c6[0])
                m_d6[s] = mv(v.tensor_tensor_scan(at1[:, cs[s]], at2[:, cs[s]],
                                                  zeros[:, 0:SUBW],
                                                  carry_kv[:, fc:fc + 1],
                                                  op0=ALU.add, op1=ALU.add))
                v.wait_ge(s_ve.h, m_d6[s])
                m_c6[s] = mv(v.tensor_copy(carry_kv[:, fc:fc + 1],
                                           at1[:, (s + 1) * SUBW - 1:(s + 1) * SUBW]))

        @blk.scalar
        def _(e):
            for s in range(2):
                e.wait_ge(s_ve.h, m_d5[s])
                m_s5[s] = ms(e.activation(at3[:, cs[s]], at3[:, cs[s]],
                                          AF.Exp, scale=-1.0))

        @blk.vector
        def _(v):
            for s in range(2):
                v.wait_ge(s_se.h, m_s5[s])
                v.wait_ge(s_ve.h, m_c6[s])
                m_d7[s] = mv(v.tensor_tensor(
                    yT[:, fc * S + half * HALF + s * SUBW:
                       fc * S + half * HALF + (s + 1) * SUBW],
                    at1[:, cs[s]], at3[:, cs[s]], op=ALU.mult))

        ch_m["d6"] = list(m_d6)
        ch_m["d7"] = list(m_d7)
        chain_ve[0] = m_d7[1]

    def emit_bcast(half):
        @blk.sync
        def _(sy):
            sy.wait_ge(s_rr.h, s_rr.n)
            flat0 = scr_r[0].rearrange("a b -> (a b)").unsqueeze(0)
            flat1 = scr_r[1].rearrange("a b -> (a b)").unsqueeze(0)
            sy.dma_start(out=rqh_row[:], in_=flat0[:, half * HALF:(half + 1) * HALF])\
                .then_inc(s_x.h, 16)
            s_x.n += 16
            sy.dma_start(out=rkh_row[:], in_=flat1[:, half * HALF:(half + 1) * HALF])\
                .then_inc(s_x.h, 16)
            s_x.n += 16
        xl = s_x.n

        @blk.tensor
        def _(t):
            t.wait_ge(s_x.h, xl)
            t.wait_ge(s_ve.h, chain_ve[0])
            t.wait_ge(s_ev.h, s_ev.n)
            t.wait_ge(s_ev2.h, s_ev2.n)
            t.wait_ge(s_se.h, s_se.n)
            for i in range(2):
                t.matmul(pb[i][:], ones1_f[:], rqh_row[:, i * 512:(i + 1) * 512],
                         start=True, stop=True)
            for i in range(2):
                ins = t.matmul(pb[2 + i][:], ones1_f[:], rkh_row[:, i * 512:(i + 1) * 512],
                               start=True, stop=True)
            s_pe.inc(ins)

        @blk.vector
        def _(v):
            v.wait_ge(s_pe.h, s_pe.n)
            v.tensor_copy(rqb[:, 0:512], pb[0][:])
            v.tensor_copy(rqb[:, 512:1024], pb[1][:])
            v.tensor_copy(rkb[:, 0:512], pb[2][:])
            ins = v.tensor_copy(rkb[:, 512:1024], pb[3][:])
            s_ev.inc(ins)
            v.wait_ge(s_ev.h, s_ev.n)
        bank_guard[0] = [(s_ev, s_ev.n)]
        rqb_ready[0] = s_ev.n

    def emit_qkv_attn():
        wsets["C"]["pre"] = [(s_pe, s_pe.n), (s_pe, s_pe.n)]
        qkv_entry_ve = s_ve.n
        qkv_entry_ev = s_ev.n
        qkv_entry_lnt = s_lnt.n

        @blk.vector
        def _(v):
            for sp_ in adp_sems:
                v.wait_ge(sp_.h, sp_.n)
            v.memset(accq[:], 0.0)
            v.memset(acck[:], 0.0)
            v.memset(ones_col[:], 1.0)
            v.memset(carry_kv, 0.0)
            vedge(v, v.memset(carry_w, 0.0))

        def qkv_chunk(i, jc, do_sq):
            @blk.gpsimd
            def _(g):
                w_stream(g, "C", wst, wqkvT_p2[jc].rearrange("(dc p) j -> p dc j", p=128))

            ld_s = wsets["C"]["ld"][(wsets["C"]["i"] - 1) % 2]
            ld_t = ld_s.n
            banks = (0, 1, 2, 3) if i % 2 == 0 else (4, 5, 6, 7)
            evq_guard = list(bank_guard[i % 2])

            @blk.tensor
            def _(t):
                t.wait_ge(ld_s.h, ld_t)
                if i < 2:
                    t.wait_ge(s_ve.h, qkv_entry_ve)
                    t.wait_ge(s_ev.h, qkv_entry_ev)
                    t.wait_ge(s_lnt.h, qkv_entry_lnt)
                for s_, n_ in evq_guard:
                    t.wait_ge(s_.h, n_)
                for ni in range(4):
                    for dt in range(ND):
                        ins = t.matmul(
                            pb[banks[ni]][:],
                            wst[(wsets["C"]["i"] - 1) % 2][:, dt * 128:(dt + 1) * 128],
                            lnT[:, dt * S + ni * 512:(dt) * S + (ni + 1) * 512],
                            start=(dt == 0), stop=(dt == ND - 1))
                    s_pe.inc(ins)
            wsets["C"]["done"].append((s_pe, s_pe.n))
            pe_b = s_pe.n - 4

            if do_sq is not None:
                @blk.vector
                def _(v):
                    v.wait_ge(s_qw[i % 2].h, s_qw[i % 2].n)
                    for ni in range(4):
                        v.wait_ge(s_pe.h, pe_b + ni + 1)
                        ins = v.tensor_copy(qkv_sb[i % 2][:, ni * 512:(ni + 1) * 512],
                                            pb[banks[ni]][:])
                        s_ev.inc(ins)
                    v.wait_ge(s_ev.h, s_ev.n)
                ev_t = s_ev.n

                @blk.sync
                def _(sy):
                    sy.wait_ge(s_ev.h, ev_t)
                    ins = sy.dma_start(out=qkv_dram[jc * 128:(jc + 1) * 128, :],
                                       in_=qkv_sb[i % 2][:])
                    s_qw[i % 2].inc(ins, 16)

                @blk.scalar
                def _(e):
                    e.wait_ge(s_ev.h, ev_t)
                    e.wait_ge(s_ve.h, s_ve.n)
                    sedge(e, e.activation(sq_t[:], qkv_sb[i % 2][:], AF.Square))

                @blk.vector
                def _(v):
                    v.wait_ge(s_se.h, s_se.n)
                    vedge(v, v.tensor_tensor(do_sq, do_sq, sq_t[:], op=ALU.add))
            else:
                d6w = list(ch_m["d6"])
                qw_now = [(s_, s_.n) for s_ in s_qw]

                @blk.scalar
                def _(e):
                    for s_, n_ in part1_qw:
                        e.wait_ge(s_.h, n_)
                    for ni in range(2):
                        e.wait_ge(s_ve.h, d6w[ni])
                        e.wait_ge(s_pe.h, pe_b + ni + 1)
                        ins = e.activation(at2[:, ni * 512:(ni + 1) * 512],
                                           pb[banks[ni]][:], AF.Copy)
                        s_ev2.inc(ins)
                        ev_at2[ni] = s_ev2.n
                ev2_t = s_ev2.n

                @blk.vector
                def _(v):
                    v.wait_ge(s_pe.h, pe_b + 3)
                    v.wait_ge(qw_now[0][0].h, qw_now[0][1])
                    ins = v.tensor_copy(vstrip[0][:], pb[banks[2]][:])
                    s_ev.inc(ins)
                    v.wait_ge(s_pe.h, pe_b + 4)
                    v.wait_ge(qw_now[1][0].h, qw_now[1][1])
                    ins = v.tensor_copy(vstrip[1][:], pb[banks[3]][:])
                    s_ev.inc(ins)
                ev_t = s_ev.n

                @blk.sync
                def _(sy):
                    sy.wait_ge(s_ev2.h, ev2_t)
                    sy.wait_ge(s_ev.h, ev_t)
                    ins = sy.dma_start(
                        out=qkv_dram[jc * 128:(jc + 1) * 128, HALF:HALF + 512],
                        in_=vstrip[0][:])
                    s_qw[0].inc(ins, 16)
                    ins = sy.dma_start(
                        out=qkv_dram[jc * 128:(jc + 1) * 128, HALF + 512:S],
                        in_=vstrip[1][:])
                    s_qw[1].inc(ins, 16)
            bank_guard[i % 2] = ([(s_ev2, s_ev2.n), (s_ev, s_ev.n)]
                                 if do_sq is None else [(s_ev, s_ev.n)])

        idx = 0
        bank_guard[0] = [(s_ev, s_ev.n)]
        bank_guard[1] = [(s_ev, s_ev.n)]
        for jc in range(ND):
            qkv_chunk(idx, jc, accq[:])
            idx += 1
        for jc in range(ND, 2 * ND):
            qkv_chunk(idx, jc, acck[:])
            idx += 1

        @blk.tensor
        def _(t):
            t.wait_ge(s_ve.h, s_ve.n)
            t.wait_ge(s_ev.h, s_ev.n)
            t.wait_ge(s_ev2.h, s_ev2.n)
            t.wait_ge(s_se.h, s_se.n)
            for w_, acc in ((0, accq), (1, acck)):
                for ni in range(4):
                    ins = t.matmul(pb[w_ * 4 + ni][0:1, :], ones_col[:],
                                   acc[:, ni * 512:(ni + 1) * 512],
                                   start=True, stop=True)
                    s_pe.inc(ins)

        @blk.scalar
        def _(e):
            e.wait_ge(s_pe.h, s_pe.n)
            for w_, row in ((0, sq_t), (1, sk_row)):
                for ni in range(4):
                    ins = e.activation(row[0:1, ni * 512:(ni + 1) * 512],
                                       pb[w_ * 4 + ni][0:1, :],
                                       AF.Ln, bias=ceps[0:1, 1:2], scale=1.0 / DM)
                    if w_ == 1 and ni == 3:
                        s_se.inc(ins)
            e.wait_ge(s_se.h, s_se.n)
            e.activation(sq_t[0:1, 0:DM], sq_t[0:1, 0:DM], AF.Exp, scale=-0.5)
            sedge(e, e.activation(sk_row[0:1, 0:DM], sk_row[0:1, 0:DM], AF.Exp,
                                  scale=-0.5))
        se_t = s_se.n

        @blk.sync
        def _(sy):
            sy.wait_ge(s_se.h, se_t)
            sy.dma_start(out=scr_r[0].rearrange("b c -> (b c)").unsqueeze(0),
                         in_=sq_t[0:1, 0:DM]).then_inc(s_rr.h, 16)
            s_rr.n += 16
            ins = sy.dma_start(out=scr_r[1].rearrange("b c -> (b c)").unsqueeze(0),
                               in_=sk_row[0:1, 0:DM])
            s_rr.inc(ins, 16)

        part1_qw.extend((s_, s_.n) for s_ in s_qw)
        emit_bcast(0)
        for fc in range(ND):
            i = idx
            jc = 2 * ND + fc
            qkv_chunk(i, jc, None)
            a3_chain(0, fc, v_in_at2=True)
            idx += 1
        a3_done_ve[0] = s_ve.n
        yT_ready_ve[0] = s_ve.n
        part2_qw.extend((s_, s_.n) for s_ in s_qw)

        emit_bcast(1)

    def emit_a3h1_hook(mi):
        a3_chain(1, mi)
        if mi == ND - 1:
            a3_done_ve[1] = s_ve.n
            yT_ready_ve[1] = s_ve.n

    def emit_p5(pre_m_hook=None):
        p5_entry_ev = s_ev.n
        p5_entry_ev2 = s_ev2.n
        s_swh = S_("swh")
        s_opv = S_("opv")
        s_adp = adp_sems
        pending = []
        planned = [s_.n // 16 for s_ in s_adp]
        tm_need = {}
        nD_blocks = [0]

        def flush_one():
            if pending:
                pending.pop(0)()

        for half in range(2):
            for sg in range(4):
                swh_base = s_swh.n
                opv_sg_guard = s_opv.n
                for mi in range(16):
                    if pre_m_hook is not None and half == 0 and sg == 0:
                        pre_m_hook(mi)
                    m = sg * 16 + mi
                    pe_marks = []
                    for gi, src_blk in ((0, wuT_blk), (1, wgT_blk)):
                        @blk.gpsimd
                        def _(g):
                            w_stream(g, "C", wst,
                                     src_blk[m].rearrange("(dc p) j -> p dc j", p=128))
                        if gi == 1:
                            flush_one()

                        ld_s = wsets["C"]["ld"][(wsets["C"]["i"] - 1) % 2]
                        ld_t = ld_s.n
                        banks = (((0, 1), (2, 3)) if mi % 2 == 0 else ((4, 5), (6, 7)))[gi]

                        @blk.tensor
                        def _(t):
                            t.wait_ge(ld_s.h, ld_t)
                            if half == 0 and sg == 0 and mi < 2:
                                t.wait_ge(s_ev.h, p5_entry_ev)
                                t.wait_ge(s_ev2.h, p5_entry_ev2)
                            if gi == 0:
                                t.wait_ge(s_ve.h, yT_ready_ve[half])
                                if mi >= 2:
                                    t.wait_ge(s_swh.h, swh_base + mi - 1)
                                if mi == 1:
                                    t.wait_ge(s_opv.h, opv_sg_guard)
                            for dt in range(ND):
                                for ni in range(2):
                                    ins = t.matmul(
                                        pb[banks[ni]][:],
                                        wst[(wsets["C"]["i"] - 1) % 2][:, dt * 128:(dt + 1) * 128],
                                        yT[:, dt * S + half * HALF + ni * 512:
                                           dt * S + half * HALF + (ni + 1) * 512],
                                        start=(dt == 0), stop=(dt == ND - 1))
                            s_pe.inc(ins)
                        wsets["C"]["done"].append((s_pe, s_pe.n))
                        pe_marks.append(s_pe.n)

                    pe_u, pe_g = pe_marks
                    bu, bg = (((0, 1), (2, 3)) if mi % 2 == 0 else ((4, 5), (6, 7)))

                    @blk.scalar
                    def _(e):
                        e.wait_ge(s_pe.h, pe_g)
                        e.wait_ge(s_swh.h, swh_base + mi)
                        e.activation(silu_t[:, 0:512], pb[bg[0]][:], AF.Silu)
                        sedge(e, e.activation(silu_t[:, 512:1024], pb[bg[1]][:], AF.Silu))

                    @blk.vector
                    def _(v):
                        v.wait_ge(s_se.h, s_se.n)
                        v.wait_ge(s_pe.h, pe_u)
                        col = mi * HALF
                        v.tensor_tensor(h_sg[:, col:col + 512], pb[bu[0]][:],
                                        silu_t[:, 0:512], op=ALU.mult)
                        ins = v.tensor_tensor(h_sg[:, col + 512:col + 1024], pb[bu[1]][:],
                                              silu_t[:, 512:1024], op=ALU.mult)
                        s_swh.inc(ins)
                        v.wait_ge(s_swh.h, s_swh.n)

                sil_last = s_se.n
                for oq in range(4):
                    @blk.gpsimd
                    def _(g):
                        if nD_blocks[0] < 2:
                            for sp_ in s_qw:
                                g.wait_ge(sp_.h, sp_.n)
                            g.wait_ge(s_ve.h, a3_done_ve[0])
                        nD_blocks[0] += 1
                        w_stream(g, "D", wob,
                                 woutT[sg * DM:(sg + 1) * DM, oq * 512:(oq + 1) * 512]
                                 .rearrange("(dc p) o -> p dc o", p=128))
                    flush_one()

                    ld_s = wsets["D"]["ld"][(wsets["D"]["i"] - 1) % 2]
                    ld_t = ld_s.n
                    opv_base = s_opv.n
                    ad_base = [planned[0], planned[1]]

                    @blk.tensor
                    def _(t):
                        t.wait_ge(ld_s.h, ld_t)
                        if oq == 0:
                            t.wait_ge(s_se.h, sil_last)
                        for tc in range(NTH):
                            if tc >= 2:
                                t.wait_ge(s_opv.h, opv_base + tc - 1)
                            elif oq == 0 and sg == 0 and half == 0:
                                pass
                            else:
                                t.wait_ge(s_opv.h, opv_base - 2 + tc + 1)
                            for dffc in range(16):
                                if oq == 0 and tc == 0:
                                    t.wait_ge(s_swh.h, swh_base + dffc + 1)
                                ins = t.matmul(
                                    pb[6 + tc % 2][:],
                                    h_sg[:, dffc * HALF + tc * 128: dffc * HALF + tc * 128 + 128],
                                    wob[(wsets["D"]["i"] - 1) % 2][:, dffc * 512:(dffc + 1) * 512],
                                    start=(dffc == 0), stop=(dffc == 15))
                            s_pe.inc(ins)
                    wsets["D"]["done"].append((s_pe, s_pe.n))

                    pe_base = s_pe.n - NTH

                    @blk.vector
                    def _(v):
                        for tc in range(NTH):
                            j = tc % 4
                            v.wait_ge(s_pe.h, pe_base + tc + 1)
                            if j in tm_need:
                                p_, cnt = tm_need[j]
                                v.wait_ge(s_adp[p_].h, cnt * 16)
                            ins = v.tensor_copy(tmpout[j][:], pb[6 + tc % 2][:])
                            s_opv.inc(ins)
                            planned[j] += 1
                            tm_need[j] = (j, planned[j])

                    def mk_accum(opv_base=opv_base, half=half, sg=sg, oq=oq):
                        def emit():
                            @blk.gpsimd
                            def _(g):
                                for tc in range(NTH):
                                    g.wait_ge(s_opv.h, opv_base + tc + 1)
                                    rows = half * HALF + tc * 128
                                    ins = g.dma_start(
                                        out=xspill[rows:rows + 128,
                                                   oq * 512:(oq + 1) * 512],
                                        in_=tmpout[tc % 4][:], accum_op=ALU.add)
                                    s_adp[tc % 4].inc(ins, 16)
                        return emit

                    pending.append(mk_accum())

            while pending:
                flush_one()

    if "ffn1" in stages:
        if "attn" in stages:
            def a1_factory(guard_pe):
                a1_early[0] = True
                return mk_early_stats(xspill, 0, guard_pe,
                                      extra_waits=[p for p in spill_q[0]] +
                                                  [p for p in spill_q[1]])
            emit_ffn(0, xr1, x_in, do_output=False, post_hook_factory=a1_factory)
        else:
            emit_ffn(0, xr1, x_in, do_output=(stages == ("ffn1",)))
    if "attn" in stages:
        emit_a1()
        emit_qkv_attn()
    if "proj" in stages:
        emit_p5(pre_m_hook=emit_a3h1_hook)
    if "ffn2" in stages:
        wsets["A"]["pre"] = [(s_pe, s_pe.n), (s_pe, s_pe.n)]
        last_tr[0] = [s_pe.n, s_pe.n]
        emit_ffn(1, xr2, xspill, do_output=True)
    elif stages != ("ffn1",):
        @blk.sync
        def _(sy):
            for tc in range(NT):
                ins = sy.dma_start(out=xrt[0][:], in_=xspill[tc * 128:(tc + 1) * 128, :])
                s_x.inc(ins, 16)
                sy.wait_ge(s_x.h, s_x.n)
                ins = sy.dma_start(out=y_out[tc * 128:(tc + 1) * 128, :], in_=xrt[0][:])
                s_od.inc(ins, 16)
                sy.wait_ge(s_od.h, s_od.n)

    @blk.sync
    def _(sy):
        sy.wait_ge(s_od.h, s_od.n)
        for s_, n_ in spill_all:
            sy.wait_ge(s_.h, n_)

    blk_ctx.__exit__(None, None, None)
    ctx.close()
    return nc


def np_split_inputs(inputs):
    x = np.asarray(inputs["L0_attn_x"], dtype=np.float32)
    w_qkv = np.asarray(inputs["L0_attn_w_qkv"], dtype=np.float32)
    w_sw = np.asarray(inputs["L0_attn_w_swiglu"], dtype=np.float32)
    w_out = np.asarray(inputs["L0_attn_w_out"], dtype=np.float32)
    gate_w = np.asarray(inputs["L0_ffn_gate_w"], dtype=np.float32)
    exp_w = np.asarray(inputs["L0_ffn_expert_w"], dtype=np.float32)
    exp_b = np.asarray(inputs["L0_ffn_expert_b"], dtype=np.float32)

    wqkvT = np.ascontiguousarray(w_qkv.T)
    shared = {
        "wqkvT_p2": np.ascontiguousarray(wqkvT.reshape(DM, 48, 128).transpose(1, 0, 2)),
        "wuT_blk": np.ascontiguousarray(
            w_sw[:DFF].T.reshape(DM, 64, 128).transpose(1, 0, 2)),
        "wgT_blk": np.ascontiguousarray(
            w_sw[DFF:].T.reshape(DM, 64, 128).transpose(1, 0, 2)),
        "woutT": np.ascontiguousarray(w_out.T),
        "gateT": np.ascontiguousarray(gate_w.T),
        "expb": np.ascontiguousarray(exp_b),
        "expT_blk": np.ascontiguousarray(
            exp_w.transpose(0, 2, 1).reshape(NE, DM, 4, 512).transpose(0, 2, 1, 3)),
    }
    return [dict(shared, x=np.ascontiguousarray(x[c])) for c in range(N_CORES)]


def kernel(**inputs):
    in_maps = np_split_inputs(inputs)
    nc = build_program()
    res = run_bass_kernel_spmd(nc, in_maps, core_ids=list(range(N_CORES)))
    return np.stack([np.asarray(r["y"], dtype=np.float32) for r in res.results], axis=0)
